# version 06
# baseline (speedup 1.0000x reference)
import numpy as np
from contextlib import ExitStack

import concourse.bass as bass
import concourse.mybir as mybir
from concourse.bass_utils import run_bass_kernel_spmd

DEDUP_WAITS = True

F32 = mybir.dt.float32
BF16 = mybir.dt.bfloat16
AF = mybir.ActivationFunctionType
ALU = mybir.AluOpType
AX = mybir.AxisListType

T = 4096
D = 1024
PD = 128
DC = D // PD
NT5 = T // 512
NT2 = T // 256
LN_EPS = 1e-5
RMS_EPS = 1e-6

ENG2BLOCK = dict(pe="tensor", act="scalar", dve="vector", pool="gpsimd", sp="sync")


class Prog:
    def __init__(self):
        self.nc = bass.Bass("TRN2", target_bir_lowering=False)
        self.es = ExitStack()
        self.scopes = [self.es]
        self.q = {e: [] for e in ENG2BLOCK}
        self.cnt = {}
        self.sem = {}
        self.lastwait = {}
        self._names = {}
        self.banks = None
        self.channels = set()
        self.chmap = {}
        self.epoch = 0

    def _nm(self, base):
        i = self._names.get(base, 0)
        self._names[base] = i + 1
        return f"{base}_{i}" if i else base

    def sb(self, base, shape, dt=F32):
        return self.scopes[-1].enter_context(self.nc.sbuf_tensor(self._nm(base), shape, dt))

    def dram(self, base, shape, dt=F32, kind="Internal"):
        return self.nc.dram_tensor(self._nm(base), shape, dt, kind=kind).ap()

    def scope(self, barrier=True):
        prog = self

        class _S:
            def __enter__(sf):
                sf.stk = ExitStack()
                prog.scopes.append(sf.stk)
                return sf.stk

            def __exit__(sf, *a):
                if barrier:
                    prog.barrier()
                prog.scopes.pop()
                return sf.stk.__exit__(*a)

        return _S()

    def reset_channels(self):
        self.chmap = {}
        self.epoch += 1

    def alloc_banks(self):
        self.banks = [self.es.enter_context(self.nc.psum_tensor(f"bank{i}", [PD, 512], F32))
                      for i in range(8)]

    OOO = ("dve", "act", "pool")

    def emit(self, eng, fn, deps=(), sig=False, dma=False, ch=None, inc=None):
        deps = list(deps)
        if not dma and inc is None:
            if eng in self.OOO:
                sig = True
            if sig:
                prev = self.cnt.get(eng, 0)
                if prev:
                    deps.append((eng, prev))
        waits = {}
        for d in deps:
            if d is None:
                continue
            s, v = d
            if v <= 0:
                continue
            assert v <= self.cnt.get(s, 0), f"dep ({s},{v}) not yet emitted (have {self.cnt.get(s, 0)})"
            if (s == eng and eng not in self.OOO
                    and not ((sig or inc is not None) and v == self.cnt.get(eng, 0))):
                continue
            if waits.get(s, 0) < v:
                waits[s] = v
        if DEDUP_WAITS:
            ww = {}
            for s2, v2 in waits.items():
                k = (eng, s2)
                if self.lastwait.get(k, 0) >= v2:
                    continue
                self.lastwait[k] = v2
                ww[s2] = v2
            waits = ww
        waits = list(waits.items())
        inc_sem, inc_amt, tok = None, 0, None
        if dma:
            chn = ch or (eng + "d")
            if chn not in ("glob", "bar"):
                if eng == "pool":
                    chn = "pw%d_%s" % (self.epoch, chn)
                else:
                    chn = self.chmap.setdefault(chn, "ch%d" % len(self.chmap))
            inc_sem, inc_amt = chn, 16
            self.channels.add(inc_sem)
        elif inc is not None:
            inc_sem, inc_amt = inc
            self.channels.add(inc_sem)
        elif sig:
            inc_sem, inc_amt = eng, 1
        if inc_sem is not None:
            self.cnt[inc_sem] = self.cnt.get(inc_sem, 0) + inc_amt
            tok = (inc_sem, self.cnt[inc_sem])
        self.q[eng].append((waits, fn, inc_sem, inc_amt))
        return tok

    def dma(self, eng, out, in_, deps=(), ch=None, slow=False):
        if slow:
            return self.emit(eng, lambda E: E.dma_start(out=out, in_=in_,
                                                        allow_slow_non_contiguous=True),
                             deps, dma=True, ch=ch)
        return self.emit(eng, lambda E: E.dma_start(out=out, in_=in_), deps, dma=True, ch=ch)

    def nop(self, eng, deps=()):
        return self.emit(eng, lambda E: E.nop(), deps, sig=True)

    def pe_tok(self):
        return self.nop("pe")

    def barrier(self, engs=None):
        engs = list(engs or ENG2BLOCK)
        deps = []
        for chn in sorted(self.channels):
            if chn != "bar" and self.cnt.get(chn, 0):
                deps.append((chn, self.cnt[chn]))
        for e in engs:
            self.emit(e, lambda E: E.nop(), deps=[(e, self.cnt.get(e, 0))], inc=("bar", 1))
        val = self.cnt["bar"]
        toks = []
        for e in engs:
            toks.append(self.nop(e, deps=deps + [("bar", val)]))
        return toks

    def finalize(self):
        nc = self.nc
        names = list(ENG2BLOCK) + sorted(self.channels)
        for s in names:
            self.sem[s] = self.es.enter_context(nc.semaphore(s))
        with nc.Block() as block:
            for eng, bname in ENG2BLOCK.items():
                deco = getattr(block, bname)
                ops = self.q[eng]
                sems = self.sem

                def body(E, ops=ops):
                    for waits, fn, inc_sem, inc_amt in ops:
                        for (s, v) in waits:
                            E.wait_ge(sems[s], v)
                        inst = fn(E)
                        if inc_sem is not None:
                            inst.then_inc(sems[inc_sem], inc_amt)

                deco(body)
        return nc


class Consts:
    def __init__(self, p):
        self.p = p
        self.np_map = {}
        self._aps = {}

    def add(self, name, arr):
        import ml_dtypes
        arr = np.ascontiguousarray(arr)
        if name in self.np_map:
            return self._aps[name]
        if arr.dtype == np.float32:
            dt = F32
        elif arr.dtype == ml_dtypes.bfloat16:
            dt = BF16
        else:
            raise ValueError(arr.dtype)
        ap = self.p.nc.dram_tensor(name, list(arr.shape), dt, kind="ExternalInput").ap()
        self.np_map[name] = arr
        self._aps[name] = ap
        return ap


def load_globals(p, c):
    g = {}
    aps = {
        "ones_f": c.add("c_ones_col", np.ones((PD, 1), np.float32)),
        "ones_row": c.add("c_ones_row", np.ones((1, PD), np.float32)),
        "ident": c.add("c_ident", np.eye(PD, dtype=np.float32)),
    }
    toks = []
    for k, ap in aps.items():
        g[k] = p.sb("g_" + k, list(ap.shape), F32)
        toks.append(p.dma("sp", g[k][:], ap, ch="glob"))
    g["ready"] = [toks[-1]]
    return g



def emit_ln_stats(p, g, X, stats_sb, deps=()):
    bank = p.banks[7]
    Xr = X.rearrange("(c k) t -> k c t", k=PD)
    with p.scope():
        xt = [p.sb("lnst_x", [PD, DC, 512], F32) for _ in range(2)]
        sq = [p.sb("lnst_sq", [PD, DC, 512], F32) for _ in range(2)]
        war = [None, None]
        evac_tok = None
        done = None
        for tt in range(NT5):
            b = tt % 2
            tl = p.dma("sp", xt[b][:], Xr[:, :, tt * 512:(tt + 1) * 512],
                       deps=list(deps) + [war[b]], ch=f"lnx{b}")
            sqt = []
            for dc in range(DC):
                sqt.append(p.emit("pool", lambda E, a=xt[b], o=sq[b], dc=dc:
                                  E.tensor_mul(o[:, dc], a[:, dc], a[:, dc]),
                                  deps=[tl, war[b]], sig=True))
            pe_last = None
            for sub in range(4):
                for cc in range(2):
                    src = xt[b] if cc == 0 else sq[b]
                    for dc in range(DC):
                        pe_last = p.emit("pe", lambda E, a=src, dc=dc, sub=sub, tt=tt, cc=cc,
                                         st=(dc == 0):
                                         E.matmul(bank[:, tt * 8 + sub * 2 + cc: tt * 8 + sub * 2 + cc + 1],
                                                  a[:, dc, sub * 128:(sub + 1) * 128], g["ones_f"][:],
                                                  start=st, stop=(dc == DC - 1)),
                                         deps=[tl, sqt[dc], evac_tok] + g["ready"],
                                         sig=(sub == 3 and cc == 1 and dc == DC - 1))
            war[b] = pe_last
            done = p.emit("act", lambda E, tt=tt: E.copy(stats_sb[:, tt * 8:(tt + 1) * 8],
                                                         bank[:, tt * 8:(tt + 1) * 8]),
                          deps=[pe_last], sig=True)
            evac_tok = done
        done = p.nop("act", deps=[done])
    return done


def emit_ln_smalls(p, g, stats_sb, srow_d, brow_d, dep, kind="ln", eps=LN_EPS):
    with p.scope():
        mu = p.sb("ln_mu", [PD, 32], F32)
        var = p.sb("ln_var", [PD, 32], F32)
        scol = p.sb("ln_scol", [PD, 32], F32)
        bcol = p.sb("ln_bcol", [PD, 32], F32)
        srow_sb = p.sb("ln_srow", [32, PD], F32)
        brow_sb = p.sb("ln_brow", [32, PD], F32)
        sums = stats_sb[:].rearrange("p (t c) -> p t c", c=2)
        if kind == "ln":
            p.emit("dve", lambda E: E.tensor_scalar(mu[:], sums[:, :, 0], 1.0 / D, None, ALU.mult),
                   deps=[dep])
            p.emit("dve", lambda E: E.tensor_scalar(var[:], sums[:, :, 1], 1.0 / D, None, ALU.mult))
            p.emit("dve", lambda E: E.tensor_tensor(scol[:], mu[:], mu[:], ALU.mult))
            t = p.emit("dve", lambda E: E.tensor_tensor(var[:], var[:], scol[:], ALU.subtract),
                       sig=True)
        else:
            t = p.emit("dve", lambda E: E.tensor_scalar(var[:], sums[:, :, 1], 1.0 / D, None,
                                                        ALU.mult), deps=[dep], sig=True)
        t1b = p.emit("dve", lambda E: E.tensor_scalar(var[:], var[:], float(eps), None, ALU.add),
                     deps=[t], sig=True)
        p.emit("act", lambda E: E.activation(var[:], var[:], AF.Ln), deps=[t1b])
        t2 = p.emit("act", lambda E: E.activation(scol[:], var[:], AF.Exp, scale=-0.5), sig=True)
        tr1 = p.emit("pe", lambda E: E.transpose(p.banks[6][0:32, 0:128], scol[:, 0:32],
                                                 g["ident"][:]), deps=[t2] + g["ready"], sig=True)
        te1 = p.emit("act", lambda E: E.copy(srow_sb[:], p.banks[6][0:32, 0:128]),
                     deps=[tr1], sig=True)
        s_tok = p.dma("sp", srow_d, srow_sb[:], deps=[te1], ch="rows")
        b_tok = None
        if kind == "ln":
            tb = p.emit("dve", lambda E: E.tensor_tensor(bcol[:], mu[:], scol[:], ALU.mult),
                        deps=[t2], sig=True)
            tb2 = p.emit("dve", lambda E: E.tensor_scalar(bcol[:], bcol[:], -1.0, None, ALU.mult),
                         deps=[tb], sig=True)
            tr2 = p.emit("pe", lambda E: E.transpose(p.banks[5][0:32, 0:128], bcol[:, 0:32],
                                                     g["ident"][:]), deps=[tb2], sig=True)
            te2 = p.emit("act", lambda E: E.copy(brow_sb[:], p.banks[5][0:32, 0:128]),
                         deps=[tr2], sig=True)
            b_tok = p.dma("sp", brow_d, brow_sb[:], deps=[te2], ch="rows")
    return s_tok, b_tok


def row_bcast_ap(dram_row, t0, n):
    flat = dram_row.rearrange("r c -> (r c)")
    return flat[t0:t0 + n].rearrange("(o n) -> o n", o=1).broadcast_to([PD, n])


class OuterLN:

    def __init__(self, p, g, X, deps=()):
        self.p, self.g, self.X = p, g, X
        self.stats = p.sb("ostats", [PD, 64], F32)
        self.srow_d = p.dram("osrow_d", [32, PD], F32)
        self.brow_d = p.dram("obrow_d", [32, PD], F32)
        t = emit_ln_stats(p, g, X, self.stats, deps=deps)
        s_tok, b_tok = emit_ln_smalls(p, g, self.stats, self.srow_d, self.brow_d, t, kind="ln")
        self.ready = [s_tok, b_tok]

    def emit_xhat(self, xhat_sb, tok0, tok1, deps=()):
        p, g = self.p, self.g
        Xr = self.X.rearrange("(c k) t -> k c t", k=PD)
        with p.scope():
            xt = [p.sb("xh_x", [PD, DC, 512], F32) for _ in range(2)]
            sB = [p.sb("xh_s", [PD, 512], F32) for _ in range(2)]
            bB = [p.sb("xh_b", [PD, 512], F32) for _ in range(2)]
            tmp = [p.sb("xh_t", [PD, 512], F32) for _ in range(2)]
            war = [None, None]
            last = None
            i = 0
            for t0 in range(tok0, tok1, 512):
                w = min(512, tok1 - t0)
                b = i % 2
                i += 1
                dl = list(deps) + [war[b]] + self.ready
                tl = p.dma("sp", xt[b][:, :, 0:w], Xr[:, :, t0:t0 + w], deps=dl, ch=f"xh{b}")
                tsb = p.dma("sp", sB[b][:, 0:w], row_bcast_ap(self.srow_d, t0, w),
                            deps=dl, ch=f"xhs{b}")
                tbb = p.dma("sp", bB[b][:, 0:w], row_bcast_ap(self.brow_d, t0, w),
                            deps=dl, ch=f"xhb{b}")
                for dc in range(DC):
                    tp = p.emit("pool", lambda E, a=xt[b], o=tmp[b], sb_=sB[b], w=w, dc=dc:
                                E.tensor_mul(o[:, 0:w], a[:, dc, 0:w], sb_[:, 0:w]),
                                deps=[tl, tsb, war[b]], sig=True)
                    td = p.emit("dve", lambda E, o=tmp[b], bb_=bB[b], w=w, t0=t0, dc=dc:
                                E.tensor_tensor(xhat_sb[:, dc, t0 - tok0:t0 - tok0 + w],
                                                o[:, 0:w], bb_[:, 0:w], ALU.add),
                                deps=[tp, tbb], sig=True)
                    war[b] = td
                    last = td
        return last



def emit_x_update(p, g, X, acc_sb, tok0, tok1, acc_tok, deps=()):
    Xr = X.rearrange("(c k) t -> k c t", k=PD)
    stores = []
    ta = None
    with p.scope():
        xt = [p.sb("xu_x", [PD, DC, 512], F32) for _ in range(2)]
        war = [None, None]
        i = 0
        for t0 in range(tok0, tok1, 512):
            w = min(512, tok1 - t0)
            b = i % 2
            i += 1
            tl = p.dma("sp", xt[b][:, :, 0:w], Xr[:, :, t0:t0 + w],
                       deps=list(deps) + [war[b], acc_tok], ch=f"xu{b}")
            for dc in range(DC):
                ta = p.emit("dve", lambda E, a=xt[b], w=w, dc=dc, o=t0 - tok0:
                            E.tensor_tensor(a[:, dc, 0:w], a[:, dc, 0:w],
                                            acc_sb[:, dc, o:o + w], ALU.add),
                            deps=[tl], sig=True)
            ts = p.dma("sp", Xr[:, :, t0:t0 + w], xt[b][:, :, 0:w], deps=[ta], ch=f"xw{b}")
            war[b] = ts
            stores.append(ts)
    return stores, ta



def emit_l1ffn(p, g, X, W, deps=(), NQ_lim=None, E_lim=None):
    QT = 1024
    NQ = NQ_lim or (T // QT)
    NTC = QT // 256
    HB = 4
    E8 = 8

    ln = OuterLN(p, g, X, deps=deps)

    with p.scope():
        wg = p.sb("wgate", [PD, DC, 8], BF16)
        twg = p.dma("pool", wg[:], W["w_gate"].rearrange("(c k) m -> k c m", k=PD),
                    deps=deps, ch="wg")
        iota8 = p.sb("iota8", [PD, 8], F32)
        iotaP9 = p.sb("iotaP9", [PD, 8], F32)
        ap_i = W["c_iota8"]
        ap_i9 = W["c_iota8p9"]
        ti0 = p.dma("sp", iota8[:], ap_i, ch="l1fi0")
        ti1 = p.dma("sp", iotaP9[:], ap_i9, ch="l1fi1")

        w1b = [p.sb("w1b", [PD, DC, 512], BF16) for _ in range(2)]
        w3b = [p.sb("w3b", [PD, DC, 512], BF16) for _ in range(2)]
        w2e = p.sb("w2e", [PD, 16, 1024], BF16)
        xq = p.sb("xq", [PD, DC, QT], BF16)
        hstage = p.sb("hstage", [PD, 16, QT], BF16)
        acc = p.sb("ffacc", [PD, DC, QT], F32)
        cB = p.sb("cB", [PD, QT], F32)
        ctp = p.sb("ctp", [8, PD], F32)
        crow_d = p.dram("crow_d", [E8, QT], F32)
        NPAR = 4
        ABK = (0, 1, 6, 7)
        sA = [p.sb("sA", [PD, 256], F32) for _ in range(NPAR)]
        sA2 = [p.sb("sA2", [PD, 256], F32) for _ in range(NPAR)]
        lgs = p.sb("lgs", [PD, 8], F32)
        m1 = p.sb("rt_m1", [PD, 1], F32)
        m2 = p.sb("rt_m2", [PD, 1], F32)
        i1 = p.sb("rt_i1", [PD, 1], F32)
        eq1 = p.sb("rt_eq1", [PD, 8], F32)
        eq2 = p.sb("rt_eq2", [PD, 8], F32)
        msk = p.sb("rt_msk", [PD, 8], F32)
        c1 = p.sb("rt_c1", [PD, 1], F32)
        c2 = p.sb("rt_c2", [PD, 1], F32)
        cmat = p.sb("rt_cmat", [PD, 8], F32)

        war_w = [None, None]
        war_w2 = None
        upd_last = None
        for q in range(NQ):
            t_xh = ln.emit_xhat(xq, q * QT, (q + 1) * QT)

            rt_tok = None
            for cidx in range(QT // 128):
                gt0 = cidx * 128
                mml = None
                for kc in range(DC):
                    mml = p.emit("pe", lambda E, kc=kc, gt0=gt0, st=(kc == 0):
                                 E.matmul(p.banks[6][:, 0:8], xq[:, kc, gt0:gt0 + 128],
                                          wg[:, kc, :], start=st, stop=(kc == DC - 1)),
                                 deps=[t_xh, twg, rt_tok], sig=(kc == DC - 1))
                tev = p.emit("act", lambda E: E.copy(lgs[:], p.banks[6][:, 0:8]),
                             deps=[mml], sig=True)
                p.emit("dve", lambda E: E.tensor_reduce(m1[:], lgs[:], AX.X, ALU.max),
                       deps=[tev, ti0, ti1])
                p.emit("dve", lambda E: E.tensor_scalar(eq1[:], lgs[:], m1[:], None, ALU.is_ge))
                p.emit("dve", lambda E: E.scalar_tensor_tensor(msk[:], eq1[:], -100.0, iotaP9[:],
                                                               ALU.mult, ALU.add))
                p.emit("dve", lambda E: E.tensor_reduce(i1[:], msk[:], AX.X, ALU.min))
                p.emit("dve", lambda E: E.tensor_scalar(eq1[:], iota8[:], i1[:], None, ALU.is_equal))
                p.emit("dve", lambda E: E.scalar_tensor_tensor(msk[:], eq1[:], -1e38, lgs[:],
                                                               ALU.mult, ALU.add))
                t5 = p.emit("dve", lambda E: E.tensor_reduce(m2[:], msk[:], AX.X, ALU.max), sig=True)
                p.emit("dve", lambda E: E.tensor_scalar(eq2[:], msk[:], m2[:], None, ALU.is_ge))
                t6 = p.emit("dve", lambda E: E.tensor_tensor(c1[:], m1[:], m2[:], ALU.subtract),
                            sig=True)
                t7 = p.emit("act", lambda E: E.activation(c2[:], c1[:], AF.Tanh, scale=0.5),
                            deps=[t6], sig=True)
                p.emit("dve", lambda E: E.tensor_scalar(c1[:], c2[:], 0.5, 0.5, ALU.mult, ALU.add),
                       deps=[t7])
                p.emit("dve", lambda E: E.tensor_scalar(c2[:], c1[:], -1.0, 1.0, ALU.mult, ALU.add))
                p.emit("dve", lambda E: E.tensor_scalar(cmat[:], eq1[:], c1[:], None, ALU.mult))
                t8 = p.emit("dve", lambda E: E.scalar_tensor_tensor(cmat[:], eq2[:], c2[:], cmat[:],
                                                                    ALU.mult, ALU.add),
                            sig=True)
                ttr = p.emit("pe", lambda E: E.transpose(p.banks[5][0:8, 0:128],
                                                         cmat[:, 0:8], g["ident"][:]),
                             deps=[t8, rt_tok], sig=True)
                tcp = p.emit("act", lambda E: E.copy(ctp[:], p.banks[5][0:8, 0:128]),
                             deps=[ttr], sig=True)
                rt_tok = p.dma("sp", crow_d[:, gt0:gt0 + 128], ctp[:], deps=[tcp, rt_tok], ch="cr")

            acc_tok = None
            first_e = True
            cb_war = None
            par = 0
            prev_h = [rt_tok] * NPAR
            for e in range(E_lim or E8):
                cb_tok = p.dma("sp", cB[:],
                               crow_d[e].rearrange("(o n) -> o n", o=1).broadcast_to([PD, QT]),
                               deps=[rt_tok, cb_war], ch="cb")
                cb_war = None

                h_toks = []
                for hb in range(HB):
                    wpar = (e * HB + hb) % 2
                    tw1 = p.dma("pool", w1b[wpar][:],
                                W["w1"][e, :, hb * 512:(hb + 1) * 512].rearrange(
                                    "(c k) m -> k c m", k=PD), deps=[war_w[wpar]],
                                ch=f"w1{wpar}")
                    tw3 = p.dma("pool", w3b[wpar][:],
                                W["w3"][e, :, hb * 512:(hb + 1) * 512].rearrange(
                                    "(c k) m -> k c m", k=PD), deps=[war_w[wpar]],
                                ch=f"w3{wpar}")
                    tw2 = p.dma("pool", w2e[:, hb * 4:(hb + 1) * 4, :],
                                W["w2"][e, hb * 512:(hb + 1) * 512, :].rearrange(
                                    "(c k) m -> k c m", k=PD), deps=[war_w2],
                                ch="w2l")
                    for hc in range(4):
                        for tc in range(NTC):
                            bk = p.banks[ABK[par]]
                            for kc in range(DC):
                                p.emit("pe", lambda E, kc=kc, hc=hc, tc=tc, bk=bk, wp=wpar,
                                       st=(kc == 0):
                                       E.matmul(bk[:, 0:256], w1b[wp][:, kc, hc * 128:(hc + 1) * 128],
                                                xq[:, kc, tc * 256:(tc + 1) * 256],
                                                start=st, stop=(kc == DC - 1)),
                                       deps=[tw1, t_xh, prev_h[par]])
                            tpe = None
                            for kc in range(DC):
                                tpe = p.emit("pe", lambda E, kc=kc, hc=hc, tc=tc, bk=bk, wp=wpar,
                                             st=(kc == 0):
                                             E.matmul(bk[:, 256:512], w3b[wp][:, kc, hc * 128:(hc + 1) * 128],
                                                      xq[:, kc, tc * 256:(tc + 1) * 256],
                                                      start=st, stop=(kc == DC - 1)),
                                             deps=[tw3], sig=(kc == DC - 1))
                            tsl = p.emit("act", lambda E, bk=bk, par=par: E.activation(
                                sA[par][:], bk[:, 0:256], AF.Silu), deps=[tpe], sig=True)
                            tcm = p.emit("pool", lambda E, par=par, tc=tc: E.tensor_mul(
                                sA2[par][:], sA[par][:], cB[:, tc * 256:(tc + 1) * 256]),
                                deps=[tsl, cb_tok], sig=True)
                            cb_war = tcm
                            th = p.emit("dve", lambda E, bk=bk, par=par, hb=hb, hc=hc, tc=tc:
                                        E.tensor_tensor(hstage[:, hb * 4 + hc, tc * 256:(tc + 1) * 256],
                                                        sA2[par][:], bk[:, 256:512], ALU.mult),
                                        deps=[tcm], sig=True)
                            prev_h[par] = th
                            h_toks.append(th)
                            par = (par + 1) % NPAR
                    war_w[wpar] = th
                h_all = h_toks[-1]

                ev_prev = None
                for tc in range(NTC):
                    mm_last = None
                    for dc2 in range(DC):
                        bk = p.banks[2 + dc2 // 2]
                        half = (dc2 % 2) * 256
                        for hh in range(16):
                            mm_last = p.emit("pe", lambda E, hh=hh, dc2=dc2, tc=tc, bk=bk,
                                             half=half, st=(hh == 0):
                                             E.matmul(bk[:, half:half + 256],
                                                      w2e[:, hh, dc2 * 128:(dc2 + 1) * 128],
                                                      hstage[:, hh, tc * 256:(tc + 1) * 256],
                                                      start=st, stop=(hh == 15)),
                                             deps=[h_all, tw2, ev_prev],
                                             sig=(hh == 15 and dc2 == DC - 1))
                    ev_toks = []
                    for dc2 in range(DC):
                        bk = p.banks[2 + dc2 // 2]
                        half = (dc2 % 2) * 256
                        if first_e:
                            tev = p.emit("dve", lambda E, bk=bk, half=half, dc2=dc2, tc=tc:
                                         E.tensor_copy(acc[:, dc2, tc * 256:(tc + 1) * 256],
                                                       bk[:, half:half + 256]),
                                         deps=[mm_last], sig=True)
                        else:
                            tev = p.emit("dve", lambda E, bk=bk, half=half, dc2=dc2, tc=tc:
                                         E.tensor_tensor(acc[:, dc2, tc * 256:(tc + 1) * 256],
                                                         acc[:, dc2, tc * 256:(tc + 1) * 256],
                                                         bk[:, half:half + 256], ALU.add),
                                         deps=[mm_last], sig=True)
                        ev_toks.append(tev)
                    ev_prev = ev_toks[-1]
                acc_tok = ev_toks[-1]
                war_w2 = acc_tok
                first_e = False

            stores, upd_last = emit_x_update(p, g, X, acc, q * QT, (q + 1) * QT, acc_tok)
            ln.ready = ln.ready + [h_all, rt_tok, upd_last]
    return stores



def prep_weights_l1ffn(inputs):
    return {
        "w_gate": np.ascontiguousarray(inputs["L1_ffn_w_gate"]),
        "w1": np.ascontiguousarray(inputs["L1_ffn_w1"]),
        "w3": np.ascontiguousarray(inputs["L1_ffn_w3"]),
        "w2": np.ascontiguousarray(inputs["L1_ffn_w2"]),
    }


def build_l1ffn_prog(NQ_lim=None, E_lim=None):
    p = Prog()
    c = Consts(p)
    p.alloc_banks()
    X = p.nc.dram_tensor("Xio", [D, T], F32, kind="ExternalInput").ap()
    Xout = p.nc.dram_tensor("Xout", [D, T], F32, kind="ExternalOutput").ap()
    W = {
        "w_gate": p.nc.dram_tensor("w_gate", [D, 8], F32, kind="ExternalInput").ap(),
        "w1": p.nc.dram_tensor("w1", [8, D, 2048], F32, kind="ExternalInput").ap(),
        "w3": p.nc.dram_tensor("w3", [8, D, 2048], F32, kind="ExternalInput").ap(),
        "w2": p.nc.dram_tensor("w2", [8, 2048, D], F32, kind="ExternalInput").ap(),
    }
    iota = np.arange(8, dtype=np.float32)
    W["c_iota8"] = c.add("c_iota8", np.broadcast_to(iota, (PD, 8)).copy())
    W["c_iota8p9"] = c.add("c_iota8p9", np.broadcast_to(iota + 100.0, (PD, 8)).copy())
    g = load_globals(p, c)
    Xs = p.dram("Xs", [D, T], F32)
    cp = []
    for dc in range(DC):
        cp.append(p.dma("sp", Xs[dc * PD:(dc + 1) * PD, :], X[dc * PD:(dc + 1) * PD, :],
                        ch="st"))
    cp = cp[-1:]
    stores = emit_l1ffn(p, g, Xs, W, deps=cp, NQ_lim=NQ_lim, E_lim=E_lim)
    out_toks = []
    for dc in range(DC):
        out_toks.append(p.dma("sp", Xout[dc * PD:(dc + 1) * PD, :], Xs[dc * PD:(dc + 1) * PD, :],
                              deps=stores[-2:], ch="out"))
    p.emit("sp", lambda E: E.nop(), deps=out_toks[-1:], sig=True)
    nc = p.finalize()
    return p, c, nc


def build_test_ln_prog():
    p = Prog()
    c = Consts(p)
    p.alloc_banks()
    X = p.nc.dram_tensor("Xio", [D, T], F32, kind="ExternalInput").ap()
    XH = p.nc.dram_tensor("XH", [D, 1024], F32, kind="ExternalOutput").ap()
    g = load_globals(p, c)
    ln = OuterLN(p, g, X)
    xq = p.sb("xq_t", [PD, DC, 1024], BF16)
    t = ln.emit_xhat(xq, 0, 1024)
    outs = []
    xf = p.sb("xf_t", [PD, 1024], F32)
    prev = None
    for dc in range(DC):
        tc_ = p.emit("dve", lambda E, dc=dc: E.tensor_copy(xf[:], xq[:, dc, :]),
                     deps=[t, prev], sig=True)
        prev = p.dma("sp", XH[dc * PD:(dc + 1) * PD, :], xf[:], deps=[tc_])
        outs.append(prev)
    p.emit("sp", lambda E: E.nop(), deps=outs, sig=True)
    nc = p.finalize()
    return p, c, nc



def load_bias_chunks(p, ap, name, deps=(), ch=None):
    sb = p.sb(name, [PD, DC], F32)
    tok = p.dma("sp", sb[:], ap.rearrange("(c k) -> k c", k=PD), deps=deps,
                ch=ch or ("ldb_" + name), slow=True)
    return sb, tok


def emit_proj_fm(p, g, name, W_ap, Kdim, Mcols, rhs_fn, evac, deps=(), tt=512,
                 banks=(0, 1), jc_block=8, extra_mm_deps=()):
    KC = Kdim // PD
    NB = Mcols // (jc_block * PD)
    assert Mcols % PD == 0
    wsb = [p.sb(f"{name}_w", [PD, KC, jc_block * PD], BF16) for _ in range(2)]
    war_w = [None, None]
    bank_war = {b: None for b in banks}
    par = 0
    last = None
    for jb in range(NB):
        wp = jb % 2
        tw = p.dma("pool", wsb[wp][:], W_ap[:, jb * jc_block * PD:(jb + 1) * jc_block * PD]
                   .rearrange("(c k) m -> k c m", k=PD), deps=list(deps) + [war_w[wp]],
                   ch=f"{name}w{wp}")
        for jcl in range(jc_block):
            j0 = (jb * jc_block + jcl) * PD
            for t0 in range(0, T, tt):
                bk = p.banks[banks[par % len(banks)]]
                par += 1
                tok = None
                for kc in range(KC):
                    tok = p.emit("pe", lambda E, kc=kc, jcl=jcl, t0=t0, bk=bk, wp=wp,
                                 st=(kc == 0):
                                 E.matmul(bk[:, 0:tt], wsb[wp][:, kc, jcl * PD:(jcl + 1) * PD],
                                          rhs_fn(kc, t0, tt), start=st, stop=(kc == KC - 1)),
                                 deps=[tw, bank_war[banks[(par - 1) % len(banks)]]]
                                 + list(extra_mm_deps), sig=(kc == KC - 1))
                ev = evac(j0, t0, tt, bk, tok)
                bank_war[banks[(par - 1) % len(banks)]] = ev
                war_w[wp] = ev
                last = ev
    return last



def emit_l0ffn(p, g, X, W, deps=()):
    QT = 1024
    NQ = T // QT
    NTC = QT // 256
    E8 = 8
    SCALE = 0.0625
    NTK = T // PD

    ln = OuterLN(p, g, X, deps=deps)
    Yh = p.dram("l0f_Y", [D, T], F32)
    Yr = Yh.rearrange("(c k) t -> k c t", k=PD)
    Xr = X.rearrange("(c k) t -> k c t", k=PD)
    bank7 = p.banks[7]

    with p.scope():
        g1sb, tg1 = load_bias_chunks(p, W["g1"], "l0f_g1")
        g2sb, tg2 = load_bias_chunks(p, W["g2"], "l0f_g2")
        stats1 = p.sb("rms1_stats", [PD, 64], F32)

        ph_helpers = {}
        with p.scope():
            xt = [p.sb("l0f_x", [PD, DC, 512], F32) for _ in range(2)]
            sB = [p.sb("l0f_s", [PD, 512], F32) for _ in range(2)]
            bB = [p.sb("l0f_b", [PD, 512], F32) for _ in range(2)]
            yt = [p.sb("l0f_yt", [PD, DC, 512], F32) for _ in range(2)]
            sq = [p.sb("l0f_sq", [PD, DC, 512], F32) for _ in range(2)]
            war = [None, None]
            ystore = [None, None]
            evac_tok = None
            st1 = None
            for tt in range(NT5):
                b = tt % 2
                dl = list(deps) + [war[b]] + ln.ready
                tl = p.dma("sp", xt[b][:], Xr[:, :, tt * 512:(tt + 1) * 512], deps=dl, ch=f"l0fx{b}")
                tsb = p.dma("sp", sB[b][:], row_bcast_ap(ln.srow_d, tt * 512, 512), deps=dl, ch=f"l0fs{b}")
                tbb = p.dma("sp", bB[b][:], row_bcast_ap(ln.brow_d, tt * 512, 512), deps=dl, ch=f"l0fb{b}")
                sq_toks = []
                ylast = None
                for dc in range(DC):
                    tp = p.emit("pool", lambda E, a=xt[b], o=yt[b], s_=sB[b], dc=dc:
                                E.tensor_mul(o[:, dc], a[:, dc], s_[:]),
                                deps=[tl, tsb, war[b], ystore[b]], sig=True)
                    ylast = p.emit("dve", lambda E, o=yt[b], b_=bB[b], dc=dc:
                                   E.tensor_tensor(o[:, dc], o[:, dc], b_[:], ALU.add),
                                   deps=[tp, tbb], sig=True)
                    sq_toks.append(p.emit("pool", lambda E, a=yt[b], o=sq[b], dc=dc:
                                          E.tensor_mul(o[:, dc], a[:, dc], a[:, dc]),
                                          deps=[ylast], sig=True))
                ystore[b] = p.dma("sp", Yr[:, :, tt * 512:(tt + 1) * 512], yt[b][:],
                                  deps=[ylast], ch=f"l0fy{b}")
                pe_last = None
                for sub in range(4):
                    for dc in range(DC):
                        pe_last = p.emit("pe", lambda E, a=sq[b], dc=dc, sub=sub, tt=tt,
                                         st=(dc == 0):
                                         E.matmul(bank7[:, tt * 8 + sub * 2 + 1: tt * 8 + sub * 2 + 2],
                                                  a[:, dc, sub * 128:(sub + 1) * 128],
                                                  g["ones_f"][:], start=st, stop=(dc == DC - 1)),
                                         deps=[sq_toks[dc], evac_tok] + g["ready"],
                                         sig=(sub == 3 and dc == DC - 1))
                war[b] = pe_last
                st1 = p.emit("act", lambda E, tt=tt: E.copy(stats1[:, tt * 8:(tt + 1) * 8],
                                                            bank7[:, tt * 8:(tt + 1) * 8]),
                             deps=[pe_last], sig=True)
                evac_tok = st1
        ydone = [("l0fy0", p.cnt.get("l0fy0", 0)), ("l0fy1", p.cnt.get("l0fy1", 0))]
        r1row_d = p.dram("r1row_d", [32, PD], F32)
        s1_tok, _ = emit_ln_smalls(p, g, stats1, r1row_d, None, st1, kind="rms", eps=RMS_EPS)

        stats2 = p.sb("rms2_stats", [PD, 64], F32)
        bc_scope = p.scope()
        bc_scope.__enter__()
        attnO = p.sb("l0f_attnO", [PD, DC, T], BF16)
        nx_scope = p.scope()
        nx_scope.__enter__()
        nx1 = p.sb("l0f_nx1", [PD, DC, T], BF16)
        nx1_tok = None
        with p.scope():
            ytl = [p.sb("l0fb_y", [PD, DC, 512], F32) for _ in range(2)]
            sB = [p.sb("l0fb_s", [PD, 512], F32) for _ in range(2)]
            war = [None, None]
            for tt in range(NT5):
                b = tt % 2
                dl = [s1_tok, war[b]] + ydone
                tl = p.dma("sp", ytl[b][:], Yr[:, :, tt * 512:(tt + 1) * 512], deps=dl, ch=f"l0fby{b}")
                tsb = p.dma("sp", sB[b][:], row_bcast_ap(r1row_d, tt * 512, 512), deps=dl, ch=f"l0fbs{b}")
                for dc in range(DC):
                    nx1_tok = p.emit("dve", lambda E, a=ytl[b], s_=sB[b], dc=dc, tt=tt:
                                     E.scalar_tensor_tensor(nx1[:, dc, tt * 512:(tt + 1) * 512],
                                                            a[:, dc], g1sb[:, dc:dc + 1], s_[:],
                                                            ALU.mult, ALU.mult),
                                     deps=[tl, tsb, tg1, war[b]], sig=True)
                war[b] = nx1_tok

        attn_done = None
        with p.scope():
            kc_fm = p.sb("l0f_kc", [PD, 2, T], BF16)
            vc_tm = p.sb("l0f_vc", [PD, NTK, 264], BF16)
            shift_sb = p.sb("l0f_shift", [PD, 1], F32)
            wkc = p.sb("l0f_wkc", [PD, DC, 256], BF16)
            wvc = p.sb("l0f_wvc", [PD, DC, 256], BF16)
            wq = p.sb("l0f_wq", [PD, DC, 256], BF16)
            q_fm = p.sb("l0f_q", [PD, 2, T], BF16)
            qq = p.sb("l0f_qq", [PD, 2, 512], F32)
            qn = p.sb("l0f_qn", [1, 16], F32)
            nrm = p.sb("l0f_nrm", [1, 16], F32)
            rsum = p.sb("l0f_rs", [1, 512], F32)
            rB = p.sb("l0f_rB", [PD, 512], F32)
            Ptile = [p.sb("l0f_P", [PD, 512], BF16) for _ in range(2)]
            sh_d = p.dram("l0f_sh_d", [1, 1], F32)

            twk = p.dma("pool", wkc[:], W["Wkc"].rearrange("(c k) m -> k c m", k=PD), ch="l0fwk")
            twv = p.dma("pool", wvc[:], W["Wvc"].rearrange("(c k) m -> k c m", k=PD), ch="l0fwv")
            kc_toks = []
            for jc in range(2):
                for t5 in range(NT5):
                    bkp = p.banks[(jc * NT5 + t5) % 2]
                    tok = None
                    for kc in range(DC):
                        tok = p.emit("pe", lambda E, kc=kc, jc=jc, t5=t5, bkp=bkp, st=(kc == 0):
                                     E.matmul(bkp[:, 0:512],
                                              wkc[:, kc, jc * PD:(jc + 1) * PD],
                                              nx1[:, kc, t5 * 512:(t5 + 1) * 512],
                                              start=st, stop=(kc == DC - 1)),
                                     deps=[twk, nx1_tok,
                                           kc_toks[-2] if len(kc_toks) >= 2 else None],
                                     sig=(kc == DC - 1))
                    kc_toks.append(p.emit("act", lambda E, jc=jc, t5=t5, bkp=bkp: E.copy(
                        kc_fm[:, jc, t5 * 512:(t5 + 1) * 512], bkp[:, 0:512]),
                        deps=[tok], sig=True))
            kc_done = kc_toks[-1]
            ones_tok = p.emit("dve", lambda E: E.memset(vc_tm[:, :, 256:264], 0.0), sig=True)
            ones_tok = p.emit("dve", lambda E: E.memset(vc_tm[:, :, 256:257], 1.0),
                              deps=[ones_tok], sig=True)
            vc_toks = []
            for tc in range(NTK):
                tok = None
                for kc in range(DC):
                    tok = p.emit("pe", lambda E, kc=kc, tc=tc, st=(kc == 0):
                                 E.matmul(p.banks[2 + tc % 2][:, 0:256],
                                          nx1[:, kc, tc * PD:(tc + 1) * PD],
                                          wvc[:, kc, :], start=st, stop=(kc == DC - 1)),
                                 deps=[twv, nx1_tok,
                                       vc_toks[-2] if len(vc_toks) >= 2 else None],
                                 sig=(kc == DC - 1))
                vc_toks.append(p.emit("act", lambda E, tc=tc: E.copy(
                    vc_tm[:, tc, 0:256], p.banks[2 + tc % 2][:, 0:256]),
                    deps=[tok], sig=True))
            vc_done = vc_toks[-1]
            nrm_toks = []
            for t5 in range(NT5):
                sq_t = []
                for jc in range(2):
                    sq_t.append(p.emit("pool", lambda E, jc=jc, t5=t5: E.tensor_mul(
                        qq[:, jc, :], kc_fm[:, jc, t5 * 512:(t5 + 1) * 512],
                        kc_fm[:, jc, t5 * 512:(t5 + 1) * 512]),
                        deps=[kc_done, nrm_toks[-1] if nrm_toks else None], sig=True))
                mt = None
                for jc in range(2):
                    mt = p.emit("pe", lambda E, jc=jc, st=(jc == 0):
                                E.matmul(p.banks[4][0:1, 0:512], g["ones_f"][:], qq[:, jc, :],
                                         start=st, stop=(jc == 1)),
                                deps=[sq_t[jc], nrm_toks[-1] if nrm_toks else None],
                                sig=(jc == 1))
                nrm_toks.append(p.emit("dve", lambda E, t5=t5: E.tensor_reduce(
                    nrm[0:1, t5:t5 + 1], p.banks[4][0:1, 0:512], AX.X, ALU.max),
                    deps=[mt], sig=True))
            kn_tok = p.emit("dve", lambda E: E.tensor_reduce(nrm[0:1, 8:9], nrm[0:1, 0:8],
                                                             AX.X, ALU.max),
                            deps=[nrm_toks[-1]], sig=True)
            for h in range(4):
                twq = p.dma("pool", wq[:], W["Wq"][:, h * 256:(h + 1) * 256]
                            .rearrange("(c k) m -> k c m", k=PD), deps=[attn_done], ch="l0fwq")
                q_toks = []
                for jc in range(2):
                    for t5 in range(NT5):
                        bkp = p.banks[(jc * NT5 + t5) % 2]
                        tok = None
                        for kc in range(DC):
                            tok = p.emit("pe", lambda E, kc=kc, jc=jc, t5=t5, bkp=bkp, st=(kc == 0):
                                         E.matmul(bkp[:, 0:512],
                                                  wq[:, kc, jc * PD:(jc + 1) * PD],
                                                  nx1[:, kc, t5 * 512:(t5 + 1) * 512],
                                                  start=st, stop=(kc == DC - 1)),
                                         deps=[twq, nx1_tok, attn_done,
                                               q_toks[-2] if len(q_toks) >= 2 else None],
                                         sig=(kc == DC - 1))
                        q_toks.append(p.emit("act", lambda E, jc=jc, t5=t5, bkp=bkp: E.copy(
                            q_fm[:, jc, t5 * 512:(t5 + 1) * 512], bkp[:, 0:512]),
                            deps=[tok], sig=True))
                q_done = q_toks[-1]
                qn_toks = []
                for t5 in range(NT5):
                    sq_t = []
                    for jc in range(2):
                        sq_t.append(p.emit("pool", lambda E, jc=jc, t5=t5: E.tensor_mul(
                            qq[:, jc, :], q_fm[:, jc, t5 * 512:(t5 + 1) * 512],
                            q_fm[:, jc, t5 * 512:(t5 + 1) * 512]),
                            deps=[q_done, kn_tok,
                                  qn_toks[-1] if qn_toks else None], sig=True))
                    mt = None
                    for jc in range(2):
                        mt = p.emit("pe", lambda E, jc=jc, st=(jc == 0):
                                    E.matmul(p.banks[4][0:1, 0:512], g["ones_f"][:], qq[:, jc, :],
                                             start=st, stop=(jc == 1)),
                                    deps=[sq_t[jc], kn_tok,
                                          qn_toks[-1] if qn_toks else None],
                                    sig=(jc == 1))
                    qn_toks.append(p.emit("dve", lambda E, t5=t5: E.tensor_reduce(
                        qn[0:1, t5:t5 + 1], p.banks[4][0:1, 0:512], AX.X, ALU.max),
                        deps=[mt], sig=True))
                t1 = p.emit("dve", lambda E: E.tensor_reduce(qn[0:1, 8:9], qn[0:1, 0:8],
                                                             AX.X, ALU.max),
                            deps=[qn_toks[-1], kn_tok], sig=True)
                t2 = p.emit("dve", lambda E: E.tensor_tensor(qn[0:1, 9:10], qn[0:1, 8:9],
                                                             nrm[0:1, 8:9], ALU.mult),
                            deps=[t2 if False else t1], sig=True)
                t3 = p.emit("act", lambda E: E.activation(qn[0:1, 10:11], qn[0:1, 9:10], AF.Ln),
                            deps=[t2], sig=True)
                t4 = p.emit("act", lambda E: E.activation(qn[0:1, 11:12], qn[0:1, 10:11],
                                                          AF.Exp, scale=0.5), deps=[t3], sig=True)
                t5_ = p.emit("dve", lambda E: E.tensor_scalar(qn[0:1, 12:13], qn[0:1, 11:12],
                                                              -SCALE, None, ALU.mult),
                             deps=[t4], sig=True)
                t6 = p.dma("sp", sh_d, qn[0:1, 12:13], deps=[t5_], ch="l0fsh")
                t7 = p.dma("sp", shift_sb[:], sh_d.broadcast_to([PD, 1]), deps=[t6], ch="l0fsh2")
                prev_tr = None
                for tq in range(NT5):
                    ev_dep = attn_done
                    p_war = [None, None]
                    s_war = [None, None]
                    sum_tok = None
                    for tkc in range(NTK):
                        pr = tkc % 2
                        bkS = p.banks[4 + pr]
                        tok = None
                        for jc in range(2):
                            tok = p.emit("pe", lambda E, jc=jc, tkc=tkc, tq=tq, bkS=bkS,
                                         st=(jc == 0):
                                         E.matmul(bkS[:, 0:512],
                                                  kc_fm[:, jc, tkc * PD:(tkc + 1) * PD],
                                                  q_fm[:, jc, tq * 512:(tq + 1) * 512],
                                                  start=st, stop=(jc == 1)),
                                         deps=[kc_done, q_done, s_war[pr], t5_],
                                         sig=(jc == 1))
                        texp = p.emit("act", lambda E, pr=pr, bkS=bkS: E.activation(
                            Ptile[pr][:], bkS[:], AF.Exp, bias=shift_sb[:], scale=SCALE),
                            deps=[tok, t7, p_war[pr]], sig=True)
                        s_war[pr] = texp
                        for jx in range(2):
                            p.emit("pe", lambda E, jx=jx, tkc=tkc, pr=pr, st=(tkc == 0):
                                   E.matmul(p.banks[jx][:, 0:512],
                                            vc_tm[:, tkc, jx * 128:(jx + 1) * 128],
                                            Ptile[pr][:], start=st, stop=(tkc == NTK - 1)),
                                   deps=[texp, vc_done, ones_tok, ev_dep])
                        sum_tok = p.emit("pe", lambda E, tkc=tkc, pr=pr, st=(tkc == 0):
                                         E.matmul(p.banks[2][0:1, 0:512],
                                                  vc_tm[:, tkc, 256:257], Ptile[pr][:],
                                                  start=st, stop=(tkc == NTK - 1)),
                                         deps=[texp, prev_tr], sig=True)
                        p_war[pr] = sum_tok
                    tr = p.emit("dve", lambda E: E.reciprocal(rsum[:], p.banks[2][0:1, 0:512]),
                                deps=[sum_tok], sig=True)
                    prev_tr = tr
                    tb = p.emit("pe", lambda E: E.matmul(p.banks[3][:, 0:512], g["ones_row"][:],
                                                         rsum[:], start=True, stop=True),
                                deps=[tr], sig=True)
                    tc_ = p.emit("dve", lambda E: E.tensor_copy(rB[:], p.banks[3][:, 0:512]),
                                 deps=[tb], sig=True)
                    for jx in range(2):
                        attn_done = p.emit("dve", lambda E, jx=jx, tq=tq, h=h: E.tensor_tensor(
                            attnO[:, h * 2 + jx, tq * 512:(tq + 1) * 512],
                            p.banks[jx][:, 0:512], rB[:], ALU.mult),
                            deps=[tc_], sig=True)

        nx_scope.__exit__(None, None, None)
        st2 = None
        with p.scope():
            wo = p.sb("l0f_wo", [PD, DC, D], BF16)
            two = p.dma("pool", wo[:], W["Wo"].rearrange("(c k) m -> k c m", k=PD), ch="l0fwo")
            ytl = [p.sb("l0fd_y", [PD, DC, 512], F32) for _ in range(2)]
            sq = [p.sb("l0fd_sq", [PD, DC, 512], F32) for _ in range(2)]
            war = [None, None]
            ystore = [None, None]
            bank_war = [None, None]
            evac_tok = None
            for tt in range(NT5):
                b = tt % 2
                tl = p.dma("sp", ytl[b][:], Yr[:, :, tt * 512:(tt + 1) * 512],
                           deps=[war[b], ystore[b]] + ydone, ch=f"l0fdy{b}")
                sq_toks = []
                add_last = None
                for jc in range(DC):
                    bk = p.banks[jc % 2]
                    tok = None
                    for kc in range(DC):
                        tok = p.emit("pe", lambda E, kc=kc, jc=jc, tt=tt, bk=bk, st=(kc == 0):
                                     E.matmul(bk[:, 0:512], wo[:, kc, jc * PD:(jc + 1) * PD],
                                              attnO[:, kc, tt * 512:(tt + 1) * 512],
                                              start=st, stop=(kc == DC - 1)),
                                     deps=[two, attn_done, war[b], bank_war[jc % 2]],
                                     sig=(kc == DC - 1))
                    ta = p.emit("dve", lambda E, jc=jc, bk=bk, y_=ytl[b]: E.tensor_tensor(
                        y_[:, jc], y_[:, jc], bk[:, 0:512], ALU.add),
                        deps=[tok, tl], sig=True)
                    bank_war[jc % 2] = ta
                    tq = p.emit("pool", lambda E, jc=jc, y_=ytl[b], q_=sq[b]: E.tensor_mul(
                        q_[:, jc], y_[:, jc], y_[:, jc]), deps=[ta], sig=True)
                    sq_toks.append((ta, tq))
                    add_last = ta
                ystore[b] = p.dma("sp", Yr[:, :, tt * 512:(tt + 1) * 512], ytl[b][:],
                                  deps=[add_last], ch=f"l0fdw{b}")
                pe_last = None
                for sub in range(4):
                    for dc in range(DC):
                        pe_last = p.emit("pe", lambda E, a=sq[b], dc=dc, sub=sub, tt=tt,
                                         st=(dc == 0):
                                         E.matmul(bank7[:, tt * 8 + sub * 2 + 1: tt * 8 + sub * 2 + 2],
                                                  a[:, dc, sub * 128:(sub + 1) * 128],
                                                  g["ones_f"][:], start=st, stop=(dc == DC - 1)),
                                         deps=[sq_toks[dc][1], evac_tok] + g["ready"],
                                         sig=(sub == 3 and dc == DC - 1))
                war[b] = pe_last
                st2 = p.emit("act", lambda E, tt=tt: E.copy(stats2[:, tt * 8:(tt + 1) * 8],
                                                            bank7[:, tt * 8:(tt + 1) * 8]),
                             deps=[pe_last], sig=True)
                evac_tok = st2
        bc_scope.__exit__(None, None, None)
        y2done = [("l0fdw0", p.cnt.get("l0fdw0", 0)), ("l0fdw1", p.cnt.get("l0fdw1", 0))]
        r2row_d = p.dram("r2row_d", [32, PD], F32)
        s2_tok, _ = emit_ln_smalls(p, g, stats2, r2row_d, None, st2, kind="rms", eps=RMS_EPS)

        wr = p.sb("l0f_wr", [PD, DC, 8], BF16)
        twr = p.dma("pool", wr[:], W["Wr"].rearrange("(c k) m -> k c m", k=PD), ch="l0fwr")
        ebB = p.sb("l0f_ebB", [PD, 8], F32)
        teb = p.dma("sp", ebB[:], W["expert_bias"].rearrange("(o n) -> o n", o=1)
                    .broadcast_to([PD, 8]), ch="l0feb")
        iota8 = p.sb("l0f_iota8", [PD, 8], F32)
        iotaP = p.sb("l0f_iotaP", [PD, 8], F32)
        ti0 = p.dma("sp", iota8[:], W["c_iota8"], ch="l0fi0")
        ti1 = p.dma("sp", iotaP[:], W["c_iota8p9"], ch="l0fi1")

        w1b = [p.sb("l0f_w1b", [PD, DC, 512], BF16) for _ in range(2)]
        w3b = [p.sb("l0f_w3b", [PD, DC, 512], BF16) for _ in range(2)]
        w2e = p.sb("l0f_w2e", [PD, 8, 1024], BF16)
        xq = p.sb("l0f_xq", [PD, DC, QT], BF16)
        hstage = p.sb("l0f_hst", [PD, 8, QT], BF16)
        acc = p.sb("l0f_acc", [PD, DC, QT], F32)
        cB = p.sb("l0f_cB", [PD, QT], F32)
        ctp = p.sb("l0f_ctp", [8, PD], F32)
        crow_d = p.dram("l0f_crow", [E8, QT], F32)
        NPAR = 4
        ABK = (0, 1, 6, 7)
        sA = [p.sb("l0f_sA", [PD, 256], F32) for _ in range(NPAR)]
        sA2 = [p.sb("l0f_sA2", [PD, 256], F32) for _ in range(NPAR)]
        lgs = p.sb("l0f_lgs", [PD, 8], F32)
        prb = p.sb("l0f_prb", [PD, 8], F32)
        m1 = p.sb("l0f_m1", [PD, 1], F32)
        m2 = p.sb("l0f_m2", [PD, 1], F32)
        i1 = p.sb("l0f_i1", [PD, 1], F32)
        ssum = p.sb("l0f_ssum", [PD, 1], F32)
        eq1 = p.sb("l0f_eq1", [PD, 8], F32)
        eq2 = p.sb("l0f_eq2", [PD, 8], F32)
        msk = p.sb("l0f_msk", [PD, 8], F32)
        c1 = p.sb("l0f_c1", [PD, 1], F32)
        c2 = p.sb("l0f_c2", [PD, 1], F32)
        cmat = p.sb("l0f_cmat", [PD, 8], F32)

        war_w = [None, None]
        war_w2 = None
        upd_dep = []
        for q in range(NQ):
            xq_tok = None
            with p.scope():
                ytl = [p.sb("l0fe_y", [PD, DC, 512], F32) for _ in range(2)]
                sB = [p.sb("l0fe_s", [PD, 512], F32) for _ in range(2)]
                war = [None, None]
                for i5, t0 in enumerate(range(q * QT, (q + 1) * QT, 512)):
                    b = i5 % 2
                    dl = [s2_tok, war[b]] + y2done + upd_dep
                    tl = p.dma("sp", ytl[b][:], Yr[:, :, t0:t0 + 512], deps=dl, ch=f"l0fey{b}")
                    tsb = p.dma("sp", sB[b][:], row_bcast_ap(r2row_d, t0, 512), deps=dl,
                                ch=f"l0fes{b}")
                    for dc in range(DC):
                        xq_tok = p.emit("dve", lambda E, a=ytl[b], s_=sB[b], dc=dc, t0=t0, q=q:
                                        E.scalar_tensor_tensor(
                                            xq[:, dc, t0 - q * QT:t0 - q * QT + 512],
                                            a[:, dc], g2sb[:, dc:dc + 1], s_[:],
                                            ALU.mult, ALU.mult),
                                        deps=[tl, tsb, tg2, war[b]], sig=True)
                    war[b] = xq_tok
            rt_tok = None
            for cidx in range(QT // 128):
                gt0 = cidx * 128
                mml = None
                for kc in range(DC):
                    mml = p.emit("pe", lambda E, kc=kc, gt0=gt0, st=(kc == 0):
                                 E.matmul(p.banks[6][:, 0:8], xq[:, kc, gt0:gt0 + 128],
                                          wr[:, kc, :], start=st, stop=(kc == DC - 1)),
                                 deps=[xq_tok, twr, rt_tok], sig=(kc == DC - 1))
                tev = p.emit("act", lambda E: E.copy(lgs[:], p.banks[6][:, 0:8]),
                             deps=[mml], sig=True)
                p.emit("dve", lambda E: E.tensor_tensor(lgs[:], lgs[:], ebB[:], ALU.add),
                       deps=[tev, teb, ti0, ti1])
                p.emit("dve", lambda E: E.tensor_reduce(m1[:], lgs[:], AX.X, ALU.max))
                p.emit("dve", lambda E: E.tensor_scalar(lgs[:], lgs[:], m1[:], None, ALU.subtract))
                t_e = p.emit("act", lambda E: E.activation(prb[:], lgs[:], AF.Exp),
                             deps=[("dve", p.cnt["dve"])], sig=True)
                p.emit("dve", lambda E: E.tensor_reduce(ssum[:], prb[:], AX.X, ALU.add),
                       deps=[t_e])
                p.emit("dve", lambda E: E.reciprocal(ssum[:], ssum[:]))
                p.emit("dve", lambda E: E.tensor_scalar(prb[:], prb[:], ssum[:], None, ALU.mult))
                p.emit("dve", lambda E: E.tensor_reduce(m1[:], prb[:], AX.X, ALU.max))
                p.emit("dve", lambda E: E.tensor_scalar(eq1[:], prb[:], m1[:], None, ALU.is_ge))
                p.emit("dve", lambda E: E.scalar_tensor_tensor(msk[:], eq1[:], -100.0, iotaP[:],
                                                               ALU.mult, ALU.add))
                p.emit("dve", lambda E: E.tensor_reduce(i1[:], msk[:], AX.X, ALU.min))
                p.emit("dve", lambda E: E.tensor_scalar(eq1[:], iota8[:], i1[:], None, ALU.is_equal))
                p.emit("dve", lambda E: E.scalar_tensor_tensor(msk[:], eq1[:], -1e38, prb[:],
                                                               ALU.mult, ALU.add))
                p.emit("dve", lambda E: E.tensor_reduce(m2[:], msk[:], AX.X, ALU.max))
                p.emit("dve", lambda E: E.tensor_scalar(eq2[:], msk[:], m2[:], None, ALU.is_ge))
                t6 = p.emit("dve", lambda E: E.tensor_tensor(c1[:], m1[:], m2[:], ALU.subtract),
                            sig=True)
                t7 = p.emit("act", lambda E: E.activation(c2[:], c1[:], AF.Tanh, scale=0.5),
                            deps=[t6], sig=True)
                p.emit("dve", lambda E: E.tensor_scalar(c1[:], c2[:], 0.5, 0.5, ALU.mult, ALU.add),
                       deps=[t7])
                p.emit("dve", lambda E: E.tensor_scalar(c2[:], c1[:], -1.0, 1.0, ALU.mult, ALU.add))
                p.emit("dve", lambda E: E.tensor_scalar(cmat[:], eq1[:], c1[:], None, ALU.mult))
                t8 = p.emit("dve", lambda E: E.scalar_tensor_tensor(cmat[:], eq2[:], c2[:], cmat[:],
                                                                    ALU.mult, ALU.add), sig=True)
                ttr = p.emit("pe", lambda E: E.transpose(p.banks[5][0:8, 0:128],
                                                         cmat[:, 0:8], g["ident"][:]),
                             deps=[t8, rt_tok], sig=True)
                tcp = p.emit("act", lambda E: E.copy(ctp[:], p.banks[5][0:8, 0:128]),
                             deps=[ttr], sig=True)
                rt_tok = p.dma("sp", crow_d[:, gt0:gt0 + 128], ctp[:], deps=[tcp, rt_tok],
                               ch="l0fcr")
            acc_tok = None
            first_e = True
            cb_war = None
            par = 0
            prev_h = [rt_tok] * NPAR
            for e in range(E8 + 1):
                shared = (e == E8)
                if not shared:
                    cb_tok = p.dma("sp", cB[:],
                                   crow_d[e].rearrange("(o n) -> o n", o=1).broadcast_to([PD, QT]),
                                   deps=[rt_tok, cb_war], ch="l0fcb")
                    cb_war = None
                h_all = None
                for hb in range(2):
                    wpar = (e * 2 + hb) % 2
                    if shared:
                        src1 = W["sW1"][:, hb * 512:(hb + 1) * 512]
                        src3 = W["sW3"][:, hb * 512:(hb + 1) * 512]
                        src2 = W["sW2"][hb * 512:(hb + 1) * 512, :]
                    else:
                        src1 = W["eW1"][e, :, hb * 512:(hb + 1) * 512]
                        src3 = W["eW3"][e, :, hb * 512:(hb + 1) * 512]
                        src2 = W["eW2"][e, hb * 512:(hb + 1) * 512, :]
                    tw1 = p.dma("pool", w1b[wpar][:], src1.rearrange("(c k) m -> k c m", k=PD),
                                deps=[war_w[wpar]], ch=f"l0fw1{wpar}")
                    tw3 = p.dma("pool", w3b[wpar][:], src3.rearrange("(c k) m -> k c m", k=PD),
                                deps=[war_w[wpar]], ch=f"l0fw3{wpar}")
                    tw2 = p.dma("pool", w2e[:, hb * 4:(hb + 1) * 4, :],
                                src2.rearrange("(c k) m -> k c m", k=PD),
                                deps=[war_w2], ch="l0fw2l")
                    for hc in range(4):
                        for tc in range(NTC):
                            bk = p.banks[ABK[par]]
                            for kc in range(DC):
                                p.emit("pe", lambda E, kc=kc, hc=hc, tc=tc, bk=bk, wp=wpar,
                                       st=(kc == 0):
                                       E.matmul(bk[:, 0:256], w1b[wp][:, kc, hc * 128:(hc + 1) * 128],
                                                xq[:, kc, tc * 256:(tc + 1) * 256],
                                                start=st, stop=(kc == DC - 1)),
                                       deps=[tw1, xq_tok, prev_h[par]])
                            tpe = None
                            for kc in range(DC):
                                tpe = p.emit("pe", lambda E, kc=kc, hc=hc, tc=tc, bk=bk, wp=wpar,
                                             st=(kc == 0):
                                             E.matmul(bk[:, 256:512],
                                                      w3b[wp][:, kc, hc * 128:(hc + 1) * 128],
                                                      xq[:, kc, tc * 256:(tc + 1) * 256],
                                                      start=st, stop=(kc == DC - 1)),
                                             deps=[tw3], sig=(kc == DC - 1))
                            tsl = p.emit("act", lambda E, bk=bk, par=par: E.activation(
                                sA[par][:], bk[:, 0:256], AF.Silu), deps=[tpe], sig=True)
                            if shared:
                                tcm = p.emit("pool", lambda E, par=par: E.tensor_copy(
                                    sA2[par][:], sA[par][:]), deps=[tsl], sig=True)
                            else:
                                tcm = p.emit("pool", lambda E, par=par, tc=tc: E.tensor_mul(
                                    sA2[par][:], sA[par][:], cB[:, tc * 256:(tc + 1) * 256]),
                                    deps=[tsl, cb_tok], sig=True)
                                cb_war = tcm
                            th = p.emit("dve", lambda E, bk=bk, par=par, hb=hb, hc=hc, tc=tc:
                                        E.tensor_tensor(hstage[:, hb * 4 + hc, tc * 256:(tc + 1) * 256],
                                                        sA2[par][:], bk[:, 256:512], ALU.mult),
                                        deps=[tcm], sig=True)
                            prev_h[par] = th
                            h_all = th
                            par = (par + 1) % NPAR
                    war_w[wpar] = h_all
                ev_prev = None
                for tc in range(NTC):
                    mm_last = None
                    for dc2 in range(DC):
                        bk = p.banks[2 + dc2 // 2]
                        half = (dc2 % 2) * 256
                        for hh in range(8):
                            mm_last = p.emit("pe", lambda E, hh=hh, dc2=dc2, tc=tc, bk=bk,
                                             half=half, st=(hh == 0):
                                             E.matmul(bk[:, half:half + 256],
                                                      w2e[:, hh, dc2 * 128:(dc2 + 1) * 128],
                                                      hstage[:, hh, tc * 256:(tc + 1) * 256],
                                                      start=st, stop=(hh == 7)),
                                             deps=[h_all, tw2, ev_prev],
                                             sig=(hh == 7 and dc2 == DC - 1))
                    ev_toks = []
                    for dc2 in range(DC):
                        bk = p.banks[2 + dc2 // 2]
                        half = (dc2 % 2) * 256
                        if first_e:
                            tev = p.emit("dve", lambda E, bk=bk, half=half, dc2=dc2, tc=tc:
                                         E.tensor_copy(acc[:, dc2, tc * 256:(tc + 1) * 256],
                                                       bk[:, half:half + 256]),
                                         deps=[mm_last], sig=True)
                        else:
                            tev = p.emit("dve", lambda E, bk=bk, half=half, dc2=dc2, tc=tc:
                                         E.tensor_tensor(acc[:, dc2, tc * 256:(tc + 1) * 256],
                                                         acc[:, dc2, tc * 256:(tc + 1) * 256],
                                                         bk[:, half:half + 256], ALU.add),
                                         deps=[mm_last], sig=True)
                        ev_toks.append(tev)
                    ev_prev = ev_toks[-1]
                acc_tok = ev_prev
                war_w2 = acc_tok
                first_e = False
            with p.scope():
                xt = [p.sb("l0fu_x", [PD, DC, 512], F32) for _ in range(2)]
                yt2 = [p.sb("l0fu_y", [PD, DC, 512], F32) for _ in range(2)]
                war = [None, None]
                st_toks = []
                for i5, t0 in enumerate(range(q * QT, (q + 1) * QT, 512)):
                    b = i5 % 2
                    tl = p.dma("sp", xt[b][:], Xr[:, :, t0:t0 + 512],
                               deps=[war[b], acc_tok], ch=f"l0fux{b}")
                    tly = p.dma("sp", yt2[b][:], Yr[:, :, t0:t0 + 512],
                                deps=[war[b], acc_tok] + y2done, ch=f"l0fuy{b}")
                    ta = None
                    for dc in range(DC):
                        p.emit("dve", lambda E, x_=xt[b], y_=yt2[b], dc=dc:
                               E.tensor_tensor(x_[:, dc], x_[:, dc], y_[:, dc], ALU.add),
                               deps=[tl, tly])
                        ta = p.emit("dve", lambda E, x_=xt[b], dc=dc, t0=t0, q=q:
                                    E.tensor_tensor(x_[:, dc], x_[:, dc],
                                                    acc[:, dc, t0 - q * QT:t0 - q * QT + 512],
                                                    ALU.add), sig=True)
                    ts_ = p.dma("sp", Xr[:, :, t0:t0 + 512], xt[b][:], deps=[ta], ch=f"l0fuw{b}")
                    war[b] = ts_
                    st_toks.append(ts_)
            upd_dep = st_toks[-2:]
    return st_toks


def build_l0ffn_prog(NQ_lim=None, E_lim=None):
    p = Prog()
    c = Consts(p)
    p.alloc_banks()
    X = p.nc.dram_tensor("Xio", [D, T], F32, kind="ExternalInput").ap()
    Xout = p.nc.dram_tensor("Xout", [D, T], F32, kind="ExternalOutput").ap()
    W = {}
    for nm, shp in [("g1", [D]), ("g2", [D]), ("Wq", [D, D]), ("Wkc", [D, 256]),
                    ("Wvc", [D, 256]), ("Wo", [D, D]), ("Wr", [D, 8]),
                    ("expert_bias", [8]), ("sW1", [D, D]), ("sW2", [D, D]),
                    ("sW3", [D, D]), ("eW1", [8, D, D]), ("eW2", [8, D, D]),
                    ("eW3", [8, D, D])]:
        W[nm] = p.nc.dram_tensor(nm, shp, F32, kind="ExternalInput").ap()
    iota = np.arange(8, dtype=np.float32)
    W["c_iota8"] = c.add("c_iota8", np.broadcast_to(iota, (PD, 8)).copy())
    W["c_iota8p9"] = c.add("c_iota8p9", np.broadcast_to(iota + 100.0, (PD, 8)).copy())
    g = load_globals(p, c)
    Xs = p.dram("Xs", [D, T], F32)
    cp = []
    for dc in range(DC):
        cp.append(p.dma("sp", Xs[dc * PD:(dc + 1) * PD, :], X[dc * PD:(dc + 1) * PD, :], ch="st"))
    cp = cp[-1:]
    stores = emit_l0ffn(p, g, Xs, W, deps=cp)
    out_toks = []
    for dc in range(DC):
        out_toks.append(p.dma("sp", Xout[dc * PD:(dc + 1) * PD, :], Xs[dc * PD:(dc + 1) * PD, :],
                              deps=stores[-2:], ch="out"))
    p.emit("sp", lambda E: E.nop(), deps=out_toks[-1:], sig=True)
    nc = p.finalize()
    return p, c, nc


def prep_l0ffn_weights(ins):
    return {
        "g1": ins["L0_ffn_g1"], "g2": ins["L0_ffn_g2"],
        "Wq": np.ascontiguousarray(ins["L0_ffn_Wq"].T),
        "Wkc": np.ascontiguousarray(ins["L0_ffn_Wkc"].T),
        "Wvc": np.ascontiguousarray(ins["L0_ffn_Wvc"].T),
        "Wo": np.ascontiguousarray(ins["L0_ffn_Wo"].T),
        "Wr": np.ascontiguousarray(ins["L0_ffn_Wr"].T),
        "expert_bias": ins["L0_ffn_expert_bias"],
        "sW1": np.ascontiguousarray(ins["L0_ffn_sW1"].T),
        "sW2": np.ascontiguousarray(ins["L0_ffn_sW2"].T),
        "sW3": np.ascontiguousarray(ins["L0_ffn_sW3"].T),
        "eW1": np.ascontiguousarray(ins["L0_ffn_eW1"].transpose(0, 2, 1)),
        "eW2": np.ascontiguousarray(ins["L0_ffn_eW2"].transpose(0, 2, 1)),
        "eW3": np.ascontiguousarray(ins["L0_ffn_eW3"].transpose(0, 2, 1)),
    }



def emit_y_bf16(p, g, ln, X, Ybf_d, deps=()):
    Xr = X.rearrange("(c k) t -> k c t", k=PD)
    Yw = Ybf_d.rearrange("(c k) t -> k c t", k=PD)
    with p.scope():
        xt = [p.sb("yb_x", [PD, DC, 512], F32) for _ in range(2)]
        sB = [p.sb("yb_s", [PD, 512], F32) for _ in range(2)]
        bB = [p.sb("yb_b", [PD, 512], F32) for _ in range(2)]
        yb = [p.sb("yb_y", [PD, DC, 512], BF16) for _ in range(2)]
        tmp = [p.sb("yb_t", [PD, 512], F32) for _ in range(2)]
        war = [None, None]
        ystore = [None, None]
        for tt in range(NT5):
            b = tt % 2
            dl = list(deps) + [war[b]] + ln.ready
            tl = p.dma("sp", xt[b][:], Xr[:, :, tt * 512:(tt + 1) * 512], deps=dl, ch=f"ybx{b}")
            tsb = p.dma("sp", sB[b][:], row_bcast_ap(ln.srow_d, tt * 512, 512), deps=dl, ch=f"ybs{b}")
            tbb = p.dma("sp", bB[b][:], row_bcast_ap(ln.brow_d, tt * 512, 512), deps=dl, ch=f"ybb{b}")
            last = None
            for dc in range(DC):
                tp = p.emit("pool", lambda E, a=xt[b], o=tmp[b], s_=sB[b], dc=dc:
                            E.tensor_mul(o[:], a[:, dc], s_[:]),
                            deps=[tl, tsb, war[b], ystore[b], last], sig=True)
                last = p.emit("dve", lambda E, o=tmp[b], y_=yb[b], b_=bB[b], dc=dc:
                              E.tensor_tensor(y_[:, dc], o[:], b_[:], ALU.add),
                              deps=[tp, tbb], sig=True)
            war[b] = last
            ystore[b] = p.dma("sp", Yw[:, :, tt * 512:(tt + 1) * 512], yb[b][:],
                              deps=[last], ch=f"ybw{b}")
    return [("ybw0", p.cnt.get("ybw0", 0)), ("ybw1", p.cnt.get("ybw1", 0))]


def emit_proj_from_hbm(p, g, name, W_ap, src_d, Mcols, evac, deps=(), tt=512,
                       banks=(0, 1), cast=False):
    KC = DC
    Sr = src_d.rearrange("(c k) t -> k c t", k=PD)
    with p.scope():
        wsb = p.sb(f"{name}_w", [PD, KC, Mcols], BF16)
        tw = p.dma("pool", wsb[:], W_ap.rearrange("(c k) m -> k c m", k=PD),
                   deps=deps, ch=f"{name}w")
        st = [p.sb(f"{name}_s", [PD, DC, tt], BF16) for _ in range(2)]
        war = [None, None]
        bank_war = {b: None for b in banks}
        last = None
        pi = 0
        for t0 in range(0, T, tt):
            b = (t0 // tt) % 2
            tl = p.dma("sp", st[b][:], Sr[:, :, t0:t0 + tt], deps=list(deps) + [war[b]],
                       ch=f"{name}s{b}")
            for jc in range(Mcols // PD):
                bk = p.banks[banks[pi % len(banks)]]
                bw = bank_war[banks[pi % len(banks)]]
                pi += 1
                tok = None
                for kc in range(KC):
                    tok = p.emit("pe", lambda E, kc=kc, jc=jc, b=b, bk=bk, st_=(kc == 0):
                                 E.matmul(bk[:, 0:tt], wsb[:, kc, jc * PD:(jc + 1) * PD],
                                          st[b][:, kc, :], start=st_, stop=(kc == KC - 1)),
                                 deps=[tw, tl, bw], sig=(kc == KC - 1))
                ev = evac(jc * PD, t0, tt, bk, tok)
                bank_war[banks[(pi - 1) % len(banks)]] = ev
                war[b] = ev
                last = ev
    return last


def emit_norm_max(p, g, src_d, rows, out_sb, out_col, deps=(), bank_i=6):
    RC = rows // PD
    Sr = src_d.rearrange("(c k) t -> k c t", k=PD)
    bank = p.banks[bank_i]
    with p.scope():
        st = [p.sb("nm_s", [PD, RC, 512], BF16) for _ in range(2)]
        sq = [p.sb("nm_q", [PD, RC, 512], F32) for _ in range(2)]
        mx = p.sb("nm_m", [1, NT5], F32)
        war = [None, None]
        red = None
        for t5 in range(NT5):
            b = t5 % 2
            tl = p.dma("sp", st[b][:], Sr[:, :, t5 * 512:(t5 + 1) * 512],
                       deps=list(deps) + [war[b]], ch=f"nms{b}")
            sq_t = []
            for rc in range(RC):
                sq_t.append(p.emit("pool", lambda E, b=b, rc=rc: E.tensor_mul(
                    sq[b][:, rc], st[b][:, rc], st[b][:, rc]),
                    deps=[tl, war[b]], sig=True))
            mt = None
            for rc in range(RC):
                mt = p.emit("pe", lambda E, b=b, rc=rc, st_=(rc == 0):
                            E.matmul(bank[0:1, 0:512], g["ones_f"][:], sq[b][:, rc],
                                     start=st_, stop=(rc == RC - 1)),
                            deps=[sq_t[rc], red] + g["ready"], sig=(rc == RC - 1))
            red = p.emit("dve", lambda E, t5=t5: E.tensor_reduce(
                mx[0:1, t5:t5 + 1], bank[0:1, 0:512], AX.X, ALU.max), deps=[mt], sig=True)
            war[b] = mt
        tok = p.emit("dve", lambda E: E.tensor_reduce(out_sb[0:1, out_col:out_col + 1],
                                                      mx[:], AX.X, ALU.max),
                     deps=[red], sig=True)
    return tok



def emit_l0ab(p, g, X, W, deps=()):
    SCALE = 0.03125
    WIN = 1024
    ln = OuterLN(p, g, X, deps=deps)
    Ybf = p.dram("l0ab_Y", [D, T], BF16)
    Qd = p.dram("l0ab_Q", [D, T], BF16)
    Vd = p.dram("l0ab_V", [T // PD, PD, 1032], BF16)
    Xr = X.rearrange("(c k) t -> k c t", k=PD)
    ytok = emit_y_bf16(p, g, ln, X, Ybf, deps=deps)
    with p.scope():
        def ev_q(j0, t0, w, bk, tok, _s=[None, None], _st=[None, None]):
            b = (t0 // 512) % 2
            if _s[b] is None:
                _s[b] = p.sb("l0ab_qe", [PD, 512], BF16)
            t1 = p.emit("act", lambda E, s_=_s[b], w=w, bk=bk: E.copy(s_[:, 0:w], bk[:, 0:w]),
                        deps=[tok, _st[b]], sig=True)
            t2 = p.dma("sp", Qd[j0:j0 + PD, t0:t0 + w], _s[b][:, 0:w], deps=[t1], ch=f"l0abq{b}")
            _st[b] = t2
            return t2
        q_tok = emit_proj_from_hbm(p, g, "l0abQ", W["W_qk"], Ybf, D, ev_q, deps=ytok)
        qdone = [("l0abq0", p.cnt.get("l0abq0", 0)), ("l0abq1", p.cnt.get("l0abq1", 0)), q_tok]
        vv = p.sb("l0ab_vt", [PD, 1032], BF16)
        wvt = p.sb("l0ab_wv", [PD, DC, D], BF16)
        twv = p.dma("pool", wvt[:], W["WvT"].rearrange("(c k) m -> k c m", k=PD), ch="l0abwv")
        yst = [p.sb("l0ab_ys", [PD, DC, 512], BF16) for _ in range(2)]
        war = [None, None]
        vstore = None
        mtok = p.emit("dve", lambda E: E.memset(vv[:, 1024:1032], 0.0), sig=True)
        mtok = p.emit("dve", lambda E: E.memset(vv[:, 1024:1025], 1.0), deps=[mtok], sig=True)
        Yr = Ybf.rearrange("(c k) t -> k c t", k=PD)
        for t5 in range(NT5):
            b = t5 % 2
            tl = p.dma("sp", yst[b][:], Yr[:, :, t5 * 512:(t5 + 1) * 512],
                       deps=ytok + [war[b]], ch=f"l0abv{b}")
            for tc4 in range(4):
                tc = t5 * 4 + tc4
                ev = None
                for half in range(2):
                    tok = None
                    for kc in range(DC):
                        tok = p.emit("pe", lambda E, kc=kc, b=b, tc4=tc4, half=half,
                                     st=(kc == 0):
                                     E.matmul(p.banks[2 + half][:, 0:512],
                                              yst[b][:, kc, tc4 * 128:(tc4 + 1) * 128],
                                              wvt[:, kc, half * 512:(half + 1) * 512],
                                              start=st, stop=(kc == DC - 1)),
                                     deps=[twv, tl, vstore], sig=(kc == DC - 1))
                    ev = p.emit("dve", lambda E, half=half: E.tensor_copy(
                        vv[:, half * 512:(half + 1) * 512], p.banks[2 + half][:, 0:512]),
                        deps=[tok, mtok], sig=True)
                vstore = p.dma("sp", Vd[tc], vv[:], deps=[ev], ch="l0abvw")
                war[b] = vstore
        vdone = [("l0abvw", p.cnt.get("l0abvw", 0))]
        sm = p.sb("l0ab_sm", [1, 4], F32)
        nt = emit_norm_max(p, g, Qd, D, sm, 0, deps=qdone)
        t3 = p.emit("act", lambda E: E.activation(sm[0:1, 1:2], sm[0:1, 0:1], AF.Ln),
                    deps=[nt], sig=True)
        t4 = p.emit("act", lambda E: E.activation(sm[0:1, 2:3], sm[0:1, 1:2], AF.Exp, scale=0.5),
                    deps=[t3], sig=True)
        t5_ = p.emit("dve", lambda E: E.tensor_scalar(sm[0:1, 3:4], sm[0:1, 2:3],
                                                      32.01 * SCALE, None, ALU.mult),
                     deps=[t4], sig=True)
        sh_d = p.dram("l0ab_sh", [1, 1], F32)
        t6 = p.dma("sp", sh_d, sm[0:1, 3:4], deps=[t5_], ch="l0absh")
        shB = p.sb("l0ab_shB", [PD, 1], F32)
        t7 = p.dma("sp", shB[:], sh_d.broadcast_to([PD, 1]), deps=[t6], ch="l0absh2")
        slB = p.sb("l0ab_slB", [PD, 1], F32)
        t8 = p.dma("sp", slB[:], W["alibi_slope"].rearrange("(o n) -> o n", o=1)
                   .broadcast_to([PD, 1]), ch="l0absl")
        iqc = p.sb("l0ab_iq", [PD, 1], F32)
        t9 = p.dma("sp", iqc[:], W["c_iota128"], ch="l0abio")
        trim = p.sb("l0ab_tri", [PD, PD], BF16)
        t10 = p.dma("sp", trim[:], W["c_trimask"], ch="l0abtm")
        qt = [p.sb("l0ab_qt", [PD, DC, 128], BF16) for _ in range(2)]
        kt = [p.sb("l0ab_kt", [PD, DC, 128], BF16) for _ in range(2)]
        NW = 9
        Vw = p.sb("l0ab_Vw", [PD, NW, 1032], BF16)
        Pw = p.sb("l0ab_Pw", [PD, NW, 128], BF16)
        rbias = p.sb("l0ab_rb", [PD, 4], F32)
        rsum = p.sb("l0ab_rsum", [1, 128], F32)
        rB = p.sb("l0ab_rB", [PD, 128], F32)
        delta = p.sb("l0ab_dl", [PD, DC, 512], F32)
        Qr = Qd.rearrange("(c k) t -> k c t", k=PD)
        qwar = [None, None]
        kwar = [None, None]
        vwar = None
        pwar = None
        prev_tr = None
        prev_texp = None
        del_tok = None
        upd = []
        for tq in range(T // PD):
            bq = tq % 2
            tql = p.dma("sp", qt[bq][:], Qr[:, :, tq * 128:(tq + 1) * 128],
                        deps=qdone + [qwar[bq]], ch=f"l0abqt{bq}")
            tk_lo = max(0, tq - 8)
            nw = tq + 1 - tk_lo
            texp = None
            vls = []
            for i in range(nw):
                tkc = tk_lo + i
                bk_ = i % 2
                pr = i % 2
                bkS = p.banks[4 + pr]
                tkl = p.dma("sp", kt[bk_][:], Yr[:, :, tkc * 128:(tkc + 1) * 128],
                            deps=ytok + [kwar[bk_]], ch=f"l0abkt{bk_}")
                tvl = p.dma("sp", Vw[:, i, :], Vd[tkc], deps=vdone + [vwar],
                            ch=f"l0abvt{i}")
                vls.append(tvl)
                tok = None
                for kc in range(DC):
                    tok = p.emit("pe", lambda E, kc=kc, bk_=bk_, bq=bq, bkS=bkS, st=(kc == 0):
                                 E.matmul(bkS[:, 0:128], kt[bk_][:, kc, :], qt[bq][:, kc, :],
                                          start=st, stop=(kc == DC - 1)),
                                 deps=[tkl, tql, prev_texp], sig=(kc == DC - 1))
                kwar[bk_] = tok
                tb1 = p.emit("dve", lambda E, d=float(tkc * 128 - tq * 128): E.tensor_scalar(
                    rbias[:, 0:1], iqc[:], d, None, ALU.add),
                    deps=[t9, prev_tr, prev_texp], sig=True)
                tb2 = p.emit("dve", lambda E: E.tensor_tensor(rbias[:, 1:2], rbias[:, 0:1],
                                                              slB[:], ALU.mult),
                             deps=[tb1, t8], sig=True)
                tb3 = p.emit("dve", lambda E: E.tensor_tensor(rbias[:, 2:3], rbias[:, 1:2],
                                                              shB[:], ALU.add),
                             deps=[tb2, t7], sig=True)
                texp = p.emit("act", lambda E, i=i, bkS=bkS: E.activation(
                    Pw[:, i, :], bkS[:, 0:128], AF.Exp, bias=rbias[:, 2:3], scale=SCALE),
                    deps=[tok, tb3, pwar], sig=True)
                prev_texp = texp
                if tkc == tq:
                    texp = p.emit("dve", lambda E, i=i: E.tensor_tensor(
                        Pw[:, i, :], Pw[:, i, :], trim[:], ALU.mult),
                        deps=[texp, t10], sig=True)
            sum_tok = None
            for i in range(nw):
                sum_tok = p.emit("pe", lambda E, i=i, st=(i == 0), lk=(i == nw - 1):
                                 E.matmul(p.banks[2][0:1, 0:128], Vw[:, i, 1024:1025],
                                          Pw[:, i, :], start=st, stop=lk),
                                 deps=[texp, vls[i], prev_tr, del_tok], sig=(i == nw - 1))
            tr = p.emit("dve", lambda E: E.reciprocal(rsum[:], p.banks[2][0:1, 0:128]),
                        deps=[sum_tok], sig=True)
            prev_tr = tr
            tb = p.emit("pe", lambda E: E.matmul(p.banks[3][:, 0:128], g["ones_row"][:],
                                                 rsum[:], start=True, stop=True),
                        deps=[tr], sig=True)
            tc_ = p.emit("dve", lambda E: E.tensor_copy(rB[:], p.banks[3][:, 0:128]),
                         deps=[tb], sig=True)
            jx_ev = {0: None, 1: None}
            for jx in range(8):
                bko = p.banks[jx % 2]
                mm_last = None
                for i in range(nw):
                    mm_last = p.emit("pe", lambda E, i=i, jx=jx, bko=bko, st=(i == 0),
                                     lk=(i == nw - 1):
                                     E.matmul(bko[:, 0:128], Vw[:, i, jx * 128:(jx + 1) * 128],
                                              Pw[:, i, :], start=st, stop=lk),
                                     deps=[texp, vls[i], jx_ev[jx % 2], del_tok],
                                     sig=(i == nw - 1))
                ev = p.emit("dve", lambda E, jx=jx, tq=tq, bko=bko: E.tensor_tensor(
                    delta[:, jx, (tq % 4) * 128:(tq % 4 + 1) * 128],
                    bko[:, 0:128], rB[:], ALU.mult),
                    deps=[mm_last, tc_], sig=True)
                jx_ev[jx % 2] = ev
                del_tok = ev
            pwar = mm_last
            vwar = mm_last
            qwar[bq] = mm_last
            if tq % 4 == 3:
                sts, _ = emit_x_update(p, g, X, delta, (tq - 3) * 128, (tq + 1) * 128, del_tok)
                upd = sts
    return upd



def emit_l1ab(p, g, X, W, deps=()):
    SCALE = 0.125
    NTK = T // PD
    ln = OuterLN(p, g, X, deps=deps)
    Ybf = p.dram("l1ab_Y", [D, T], BF16)
    Qd = p.dram("l1ab_Q", [D, T], BF16)
    Kd = p.dram("l1ab_K", [D, T], BF16)
    Vd = p.dram("l1ab_V", [NTK, PD, 16 * 66], BF16)
    ytok = emit_y_bf16(p, g, ln, X, Ybf, deps=deps)
    with p.scope():
        bq, tbq = load_bias_chunks(p, W["bq"], "l1ab_bq")
        bk, tbk = load_bias_chunks(p, W["bk"], "l1ab_bk")
        bv, tbv = load_bias_chunks(p, W["bv"], "l1ab_bv")
        bp, tbp = load_bias_chunks(p, W["bp"], "l1ab_bp")

        def mk_ev(dst, bias_sb, bias_tok, chname):
            bufs = [None, None]
            stoks = [None, None]
            def ev(j0, t0, w, bk_, tok):
                b = (t0 // 512) % 2
                if bufs[b] is None:
                    bufs[b] = p.sb(f"{chname}_e", [PD, 512], BF16)
                t1 = p.emit("dve", lambda E, s_=bufs[b], w=w, bk_=bk_, j0=j0: E.tensor_scalar(
                    s_[:, 0:w], bk_[:, 0:w], bias_sb[:, j0 // PD:j0 // PD + 1], None, ALU.add),
                    deps=[tok, bias_tok, stoks[b]], sig=True)
                t2 = p.dma("sp", dst[j0:j0 + PD, t0:t0 + w], bufs[b][:, 0:w],
                           deps=[t1], ch=f"{chname}{b}")
                stoks[b] = t2
                return t2
            return ev
        q_tok = emit_proj_from_hbm(p, g, "l1abQ", W["WqT"], Ybf, D,
                                   mk_ev(Qd, bq, tbq, "l1abq"), deps=ytok)
        k_tok = emit_proj_from_hbm(p, g, "l1abK", W["WkT"], Ybf, D,
                                   mk_ev(Kd, bk, tbk, "l1abk"), deps=ytok)
        qdone = [("l1abq0", p.cnt.get("l1abq0", 0)), ("l1abq1", p.cnt.get("l1abq1", 0)), q_tok]
        kdone = [("l1abk0", p.cnt.get("l1abk0", 0)), ("l1abk1", p.cnt.get("l1abk1", 0)), k_tok]
        wvt = p.sb("l1ab_wv", [PD, DC, D], BF16)
        twv = p.dma("pool", wvt[:], W["WvT"].rearrange("(c k) m -> k c m", k=PD), ch="l1abwv")
        bvB = p.sb("l1ab_bvB", [PD, D], F32)
        tbvB = p.dma("sp", bvB[:], W["bv"].rearrange("(o n) -> o n", o=1).broadcast_to([PD, D]),
                     ch="l1abbv")
        vv = p.sb("l1ab_vt", [PD, 16 * 66], BF16)
        yst = [p.sb("l1ab_ys", [PD, DC, 512], BF16) for _ in range(2)]
        war = [None, None]
        vstore = None
        mtok = p.emit("dve", lambda E: E.memset(vv[:], 0.0), sig=True)
        mtok = p.emit("dve", lambda E: E.memset(
            vv[:].rearrange("p (h c) -> p h c", c=66)[:, :, 64:65], 1.0), deps=[mtok], sig=True)
        Yr = Ybf.rearrange("(c k) t -> k c t", k=PD)
        for t5 in range(NT5):
            b = t5 % 2
            tl = p.dma("sp", yst[b][:], Yr[:, :, t5 * 512:(t5 + 1) * 512],
                       deps=ytok + [war[b]], ch=f"l1abv{b}")
            for tc4 in range(4):
                tc = t5 * 4 + tc4
                ev = None
                for half in range(2):
                    tok = None
                    for kc in range(DC):
                        tok = p.emit("pe", lambda E, kc=kc, b=b, tc4=tc4, half=half,
                                     st=(kc == 0):
                                     E.matmul(p.banks[2 + half][:, 0:512],
                                              yst[b][:, kc, tc4 * 128:(tc4 + 1) * 128],
                                              wvt[:, kc, half * 512:(half + 1) * 512],
                                              start=st, stop=(kc == DC - 1)),
                                     deps=[twv, tl, vstore], sig=(kc == DC - 1))
                    t1 = p.emit("dve", lambda E, half=half: E.scalar_tensor_tensor(
                        vv[:].rearrange("p (h c) -> p h c", c=66)[:, half * 8:(half + 1) * 8, 0:64],
                        p.banks[2 + half][:, 0:512].rearrange("p (h c) -> p h c", c=64),
                        1.0,
                        bvB[:, half * 512:(half + 1) * 512].rearrange("p (h c) -> p h c", c=64),
                        ALU.mult, ALU.add),
                        deps=[tok, tbvB, mtok], sig=True)
                    ev = t1
                vstore = p.dma("sp", Vd[tc], vv[:], deps=[ev], ch="l1abvw")
                war[b] = vstore
        vdone = [("l1abvw", p.cnt.get("l1abvw", 0))]
        lam = p.sb("l1ab_lam", [1, 512], F32)
        lsc = p.sb("l1ab_lsc", [1, 16], F32)
        tl1 = p.dma("sp", lam[0:1, 0:512], W["lambda_q1"].rearrange("h o c -> (h o c)")
                    .rearrange("(o n) -> o n", o=1), ch="l1abl1")
        lq2 = p.sb("l1ab_l2", [1, 512], F32)
        tmp5 = p.sb("l1ab_tmp5", [1, 512], F32)
        tl3 = p.dma("sp", lq2[0:1, 0:512], W["lambda_k1"].rearrange("h o c -> (h o c)")
                    .rearrange("(o n) -> o n", o=1), ch="l1abl2")
        t1 = p.emit("dve", lambda E: E.tensor_tensor(tmp5[:], lam[:], lq2[:], ALU.mult),
                    deps=[tl1, tl3], sig=True)
        t2 = p.emit("dve", lambda E: E.tensor_reduce(
            lsc[0:1, 0:8], tmp5[0:1, :].rearrange("o (h c) -> o h c", c=64), AX.X, ALU.add),
            deps=[t1], sig=True)
        tl4 = p.dma("sp", lam[0:1, 0:512], W["lambda_q2"].rearrange("h o c -> (h o c)")
                    .rearrange("(o n) -> o n", o=1), deps=[t1], ch="l1abl1")
        tl5 = p.dma("sp", lq2[0:1, 0:512], W["lambda_k2"].rearrange("h o c -> (h o c)")
                    .rearrange("(o n) -> o n", o=1), deps=[t1], ch="l1abl2")
        t3 = p.emit("dve", lambda E: E.tensor_tensor(tmp5[:], lam[:], lq2[:], ALU.mult),
                    deps=[tl4, tl5], sig=True)
        t4 = p.emit("dve", lambda E: E.tensor_reduce(
            lsc[0:1, 8:16], tmp5[0:1, :].rearrange("o (h c) -> o h c", c=64), AX.X, ALU.add),
            deps=[t3], sig=True)
        t5_ = p.emit("dve", lambda E: E.tensor_scalar_min(lsc[:], lsc[:], 5.0),
                     deps=[t2, t4], sig=True)
        t6 = p.emit("act", lambda E: E.activation(lsc[:], lsc[:], AF.Exp), deps=[t5_], sig=True)
        t7 = p.emit("dve", lambda E: E.tensor_tensor(lsc[0:1, 0:8], lsc[0:1, 0:8],
                                                     lsc[0:1, 8:16], ALU.subtract),
                    deps=[t6], sig=True)
        t8 = p.emit("dve", lambda E: E.tensor_reduce(lsc[0:1, 8:9], lsc[0:1, 0:8], AX.X, ALU.add),
                    deps=[t7], sig=True)
        t9 = p.emit("dve", lambda E: E.tensor_scalar(lsc[0:1, 9:10], lsc[0:1, 8:9],
                                                     0.125, 0.8, ALU.mult, ALU.add),
                    deps=[t8], sig=True)
        t10 = p.emit("dve", lambda E: E.tensor_scalar(lsc[0:1, 10:11], lsc[0:1, 9:10],
                                                      -1.0, None, ALU.mult), deps=[t9], sig=True)
        lv_d = p.dram("l1ab_lv", [1, 1], F32)
        t11 = p.dma("sp", lv_d, lsc[0:1, 10:11], deps=[t10], ch="l1ablv")
        lvB = p.sb("l1ab_lvB", [PD, 1], F32)
        t12 = p.dma("sp", lvB[:], lv_d.broadcast_to([PD, 1]), deps=[t11], ch="l1ablv2")
        sm = p.sb("l1ab_sm", [1, 6], F32)
        nq = emit_norm_max(p, g, Qd, D, sm, 0, deps=qdone)
        nk = emit_norm_max(p, g, Kd, D, sm, 1, deps=kdone)
        u1 = p.emit("dve", lambda E: E.tensor_tensor(sm[0:1, 2:3], sm[0:1, 0:1], sm[0:1, 1:2],
                                                     ALU.mult), deps=[nq, nk], sig=True)
        u2 = p.emit("act", lambda E: E.activation(sm[0:1, 3:4], sm[0:1, 2:3], AF.Ln),
                    deps=[u1], sig=True)
        u3 = p.emit("act", lambda E: E.activation(sm[0:1, 4:5], sm[0:1, 3:4], AF.Exp, scale=0.5),
                    deps=[u2], sig=True)
        u4 = p.emit("dve", lambda E: E.tensor_scalar(sm[0:1, 5:6], sm[0:1, 4:5], -SCALE, None,
                                                     ALU.mult), deps=[u3], sig=True)
        sh_d = p.dram("l1ab_sh", [1, 1], F32)
        u5 = p.dma("sp", sh_d, sm[0:1, 5:6], deps=[u4], ch="l1absh")
        shB = p.sb("l1ab_shB", [PD, 1], F32)
        u6 = p.dma("sp", shB[:], sh_d.broadcast_to([PD, 1]), deps=[u5], ch="l1absh2")
        o_fm = p.sb("l1ab_o", [PD, DC, T], BF16)
        attn_scope = p.scope()
        attn_scope.__enter__()
        kh = p.sb("l1ab_kh", [64, T], BF16)
        qh = p.sb("l1ab_qh", [64, T], BF16)
        vh = p.sb("l1ab_vh", [PD, NTK, 66], BF16)
        Pw = p.sb("l1ab_Pw", [PD, NTK, 512], BF16)
        rsum = p.sb("l1ab_rs", [1, 512], F32)
        rB = p.sb("l1ab_rB", [64, 512], F32)
        SBK = (4, 5, 6, 7)
        o_done = None
        for h in range(16):
            tkh = p.dma("sp", kh[:], Kd[h * 64:(h + 1) * 64, :], deps=kdone + [o_done],
                        ch="l1abkh")
            tqh = p.dma("sp", qh[:], Qd[h * 64:(h + 1) * 64, :], deps=qdone + [o_done],
                        ch="l1abqh")
            tvh = p.dma("sp", vh[:], Vd[:, :, h * 66:(h + 1) * 66].rearrange("a k c -> k a c"),
                        deps=vdone + [o_done], ch="l1abvh")
            prev_tr = None
            prev_phase2 = None
            for tq in range(NT5):
                s_war = [None] * 4
                texps = []
                for tkc in range(NTK):
                    pr = tkc % 4
                    bkS = p.banks[SBK[pr]]
                    tok = p.emit("pe", lambda E, tkc=tkc, tq=tq, bkS=bkS:
                                 E.matmul(bkS[:, 0:512], kh[:, tkc * PD:(tkc + 1) * PD],
                                          qh[:, tq * 512:(tq + 1) * 512],
                                          start=True, stop=True),
                                 deps=[tkh, tqh, s_war[pr]], sig=True)
                    texp = p.emit("act", lambda E, tkc=tkc, bkS=bkS: E.activation(
                        Pw[:, tkc, :], bkS[:], AF.Exp, bias=shB[:], scale=SCALE),
                        deps=[tok, u6, prev_phase2], sig=True)
                    s_war[pr] = texp
                    texps.append(texp)
                sum_tok = None
                for tkc in range(NTK):
                    sum_tok = p.emit("pe", lambda E, tkc=tkc, st=(tkc == 0):
                                     E.matmul(p.banks[0][0:65, 0:512], vh[:, tkc, 0:65],
                                              Pw[:, tkc, :], start=st, stop=(tkc == NTK - 1)),
                                     deps=[texps[tkc], tvh, prev_tr, o_done],
                                     sig=(tkc == NTK - 1))
                prev_phase2 = sum_tok
                tr = p.emit("dve", lambda E: E.reciprocal(rsum[:], p.banks[0][64:65, 0:512]),
                            deps=[sum_tok], sig=True)
                prev_tr = tr
                tb = p.emit("pe", lambda E: E.matmul(p.banks[3][0:64, 0:512],
                                                     g["ones_row"][0:1, 0:64], rsum[:],
                                                     start=True, stop=True),
                            deps=[tr], sig=True)
                tc_ = p.emit("dve", lambda E: E.tensor_copy(rB[:], p.banks[3][0:64, 0:512]),
                             deps=[tb], sig=True)
                o_done = p.emit("dve", lambda E, h=h, tq=tq: E.tensor_tensor(
                    o_fm[(h % 2) * 64:(h % 2) * 64 + 64, h // 2, tq * 512:(tq + 1) * 512],
                    p.banks[0][0:64, 0:512], rB[:], ALU.mult),
                    deps=[tc_], sig=True)
        attn_scope.__exit__(None, None, None)
        cmb = None
        for j in range(4):
            for t5 in range(NT5):
                cmb = p.emit("dve", lambda E, j=j, t5=t5: E.scalar_tensor_tensor(
                    o_fm[:, j, t5 * 512:(t5 + 1) * 512], o_fm[:, j + 4, t5 * 512:(t5 + 1) * 512],
                    lvB[:], o_fm[:, j, t5 * 512:(t5 + 1) * 512], ALU.mult, ALU.add),
                    deps=[o_done, t12], sig=True)
        wp = p.sb("l1ab_wp", [PD, DC, D], BF16)
        twp = p.dma("pool", wp[:], W["WpT"].rearrange("(c k) m -> k c m", k=PD), ch="l1abwp")
        delta = p.sb("l1ab_dl", [PD, DC, 512], F32)
        del_tok = None
        upd = []
        bank_war = [None, None]
        for t5 in range(NT5):
            for jc in range(DC):
                bk_ = p.banks[jc % 2]
                tok = None
                for kc in range(DC):
                    tok = p.emit("pe", lambda E, kc=kc, jc=jc, t5=t5, bk_=bk_, st=(kc == 0):
                                 E.matmul(bk_[:, 0:512], wp[:, kc, jc * PD:(jc + 1) * PD],
                                          o_fm[:, kc, t5 * 512:(t5 + 1) * 512],
                                          start=st, stop=(kc == DC - 1)),
                                 deps=[twp, cmb, bank_war[jc % 2]], sig=(kc == DC - 1))
                del_tok = p.emit("dve", lambda E, jc=jc, bk_=bk_: E.tensor_scalar(
                    delta[:, jc, :], bk_[:, 0:512], bp[:, jc:jc + 1], None, ALU.add),
                    deps=[tok, tbp], sig=True)
                bank_war[jc % 2] = del_tok
            sts, _ = emit_x_update(p, g, X, delta, t5 * 512, (t5 + 1) * 512, del_tok)
            upd = sts
    return upd



def emit_l1aa(p, g, X, W, deps=()):
    NH, NKV, HD, CH = 16, 4, 64, 128
    NC = T // CH
    ln = OuterLN(p, g, X, deps=deps)
    Ybf = p.dram("l1aa_Y", [D, T], BF16)
    Od = p.dram("l1aa_O", [D, T], BF16)
    ytok = emit_y_bf16(p, g, ln, X, Ybf, deps=deps)
    Yr = Ybf.rearrange("(c k) t -> k c t", k=PD)
    with p.scope():
        permT = p.sb("l1aa_perm", [PD, PD], BF16)
        tpm = p.dma("sp", permT[:], W["c_permT"], ch="l1aapm")
        cosf = p.sb("l1aa_cosf", [PD, T], BF16)
        sinf = p.sb("l1aa_sinf", [PD, T], BF16)
        tcf = p.dma("sp", cosf[:], W["c_cos_fm"], ch="l1aacf")
        tsf = p.dma("sp", sinf[:], W["c_sin_fm"], ch="l1aasf")
        cost = p.sb("l1aa_cost", [PD, NC, HD], BF16)
        sint = p.sb("l1aa_sint", [PD, NC, HD], BF16)
        tct = p.dma("sp", cost[:], W["c_cos_tm"], ch="l1aact")
        tst = p.dma("sp", sint[:], W["c_sin_tm"], ch="l1aast")
        maskT = p.sb("l1aa_mask", [PD, NH, CH], BF16)
        tmk = p.dma("sp", maskT[:], W["c_maskT"], ch="l1aamk")
        qdec = p.sb("l1aa_qdec", [PD, NH, CH], BF16)
        tqd = p.dma("sp", qdec[:], W["c_qdec"], ch="l1aaqd")
        kdec = p.sb("l1aa_kdec", [PD, NH], F32)
        tkd = p.dma("sp", kdec[:], W["c_kdec"], ch="l1aakd")

        q_fm = p.sb("l1aa_q", [PD, DC, T], BF16)

        def qmap(h):
            return ((h % 4) + 4 * (h // 8), 64 * ((h // 4) % 2))

        def ev_q(j0, t0, w, bk_, tok):
            tk2 = tok
            for hh in (j0 // 64, j0 // 64 + 1):
                pch, pb = qmap(hh)
                rb = (hh % 2) * 64
                tk2 = p.emit("act", lambda E, pch=pch, pb=pb, rb=rb, t0=t0, w=w, bk_=bk_:
                             E.copy(q_fm[pb:pb + 64, pch, t0:t0 + w], bk_[rb:rb + 64, 0:w]),
                             deps=[tok], sig=True)
            return tk2
        q_tok = emit_proj_from_hbm(p, g, "l1aaQ", W["WqT"], Ybf, D, ev_q, deps=ytok)
        rtmp = p.sb("l1aa_rt", [PD, 512], F32)
        rtmp2 = p.sb("l1aa_rt2", [PD, 512], F32)
        rope_tok = None
        for dc in range(DC):
            for t5 in range(NT5):
                mm = p.emit("pe", lambda E, dc=dc, t5=t5: E.matmul(
                    p.banks[6][:, 0:512], permT[:], q_fm[:, dc, t5 * 512:(t5 + 1) * 512],
                    start=True, stop=True), deps=[q_tok, tpm, rope_tok], sig=True)
                w1 = p.emit("dve", lambda E, t5=t5: E.tensor_tensor(
                    rtmp[:], p.banks[6][:, 0:512], sinf[:, t5 * 512:(t5 + 1) * 512], ALU.mult),
                    deps=[mm, tsf], sig=True)
                w2 = p.emit("dve", lambda E, dc=dc, t5=t5: E.tensor_tensor(
                    rtmp2[:], q_fm[:, dc, t5 * 512:(t5 + 1) * 512],
                    cosf[:, t5 * 512:(t5 + 1) * 512], ALU.mult), deps=[tcf], sig=True)
                rope_tok = p.emit("dve", lambda E, dc=dc, t5=t5: E.tensor_tensor(
                    q_fm[:, dc, t5 * 512:(t5 + 1) * 512], rtmp[:], rtmp2[:], ALU.add),
                    deps=[w1, w2], sig=True)
        k_fm = p.sb("l1aa_kf", [PD, 2, T], BF16)
        def ev_k(j0, t0, w, bk_, tok):
            return p.emit("act", lambda E, j0=j0, t0=t0, w=w, bk_=bk_: E.copy(
                k_fm[:, j0 // PD, t0:t0 + w], bk_[:, 0:w]), deps=[tok], sig=True)
        k_tok = emit_proj_from_hbm(p, g, "l1aaK", W["WkT"], Ybf, 256, ev_k, deps=ytok,
                                   banks=(2, 3))
        kr_tok = None
        for dc in range(2):
            for t5 in range(NT5):
                mm = p.emit("pe", lambda E, dc=dc, t5=t5: E.matmul(
                    p.banks[6][:, 0:512], permT[:], k_fm[:, dc, t5 * 512:(t5 + 1) * 512],
                    start=True, stop=True), deps=[k_tok, tpm, kr_tok, rope_tok], sig=True)
                w1 = p.emit("dve", lambda E, t5=t5: E.tensor_tensor(
                    rtmp[:], p.banks[6][:, 0:512], sinf[:, t5 * 512:(t5 + 1) * 512], ALU.mult),
                    deps=[mm, tsf], sig=True)
                w2 = p.emit("dve", lambda E, dc=dc, t5=t5: E.tensor_tensor(
                    rtmp2[:], k_fm[:, dc, t5 * 512:(t5 + 1) * 512],
                    cosf[:, t5 * 512:(t5 + 1) * 512], ALU.mult), deps=[tcf], sig=True)
                kr_tok = p.emit("dve", lambda E, dc=dc, t5=t5: E.tensor_tensor(
                    k_fm[:, dc, t5 * 512:(t5 + 1) * 512], rtmp[:], rtmp2[:], ALU.add),
                    deps=[w1, w2], sig=True)
        k_tm = p.sb("l1aa_ktm", [PD, NC, 256], BF16)
        v_tm = p.sb("l1aa_vtm", [PD, NC, 256], BF16)
        wkt = p.sb("l1aa_wkt", [PD, DC, 256], BF16)
        wvt = p.sb("l1aa_wvt", [PD, DC, 256], BF16)
        twk = p.dma("pool", wkt[:], W["WkT"].rearrange("(c k) m -> k c m", k=PD), ch="l1aawk")
        twv = p.dma("pool", wvt[:], W["WvT"].rearrange("(c k) m -> k c m", k=PD), ch="l1aawv")
        yst = [p.sb("l1aa_ys", [PD, DC, 512], BF16) for _ in range(2)]
        war = [None, None]
        tm_tok = None
        for t5 in range(NT5):
            b = t5 % 2
            tl = p.dma("sp", yst[b][:], Yr[:, :, t5 * 512:(t5 + 1) * 512],
                       deps=ytok + [war[b]], ch=f"l1aay{b}")
            for tc4 in range(4):
                tc = t5 * 4 + tc4
                for which in range(2):
                    wsb2 = wkt if which == 0 else wvt
                    dst = k_tm if which == 0 else v_tm
                    tok = None
                    for kc in range(DC):
                        tok = p.emit("pe", lambda E, kc=kc, b=b, tc4=tc4, which=which,
                                     st=(kc == 0):
                                     E.matmul(p.banks[2 + which][:, 0:256],
                                              yst[b][:, kc, tc4 * 128:(tc4 + 1) * 128],
                                              (wkt if which == 0 else wvt)[:, kc, :],
                                              start=st, stop=(kc == DC - 1)),
                                     deps=[twk, twv, tl, tm_tok], sig=(kc == DC - 1))
                    tm_tok = p.emit("dve", lambda E, tc=tc, which=which, dst=dst: E.tensor_copy(
                        dst[:, tc, :], p.banks[2 + which][:, 0:256]), deps=[tok], sig=True)
                war[b] = tm_tok
        ktr_tok = None
        for tc in range(NC):
            kv = k_tm[:, tc, :].rearrange("p (h c) -> p h c", c=HD)
            t1 = p.emit("dve", lambda E, kv=kv: E.tensor_scalar(
                rtmp[:, 0:256].rearrange("p (h c) -> p h c", c=HD)[:, :, 0:32],
                kv[:, :, 32:64], -1.0, None, ALU.mult), deps=[tm_tok, ktr_tok], sig=True)
            t2 = p.emit("dve", lambda E, kv=kv: E.tensor_copy(
                rtmp[:, 0:256].rearrange("p (h c) -> p h c", c=HD)[:, :, 32:64],
                kv[:, :, 0:32]), deps=[t1], sig=True)
            t3 = p.emit("dve", lambda E, tc=tc: E.tensor_tensor(
                rtmp[:, 0:256].rearrange("p (h c) -> p h c", c=HD)[:, :, :],
                rtmp[:, 0:256].rearrange("p (h c) -> p h c", c=HD)[:, :, :],
                sint[:, tc, None, :].broadcast_to([PD, 4, HD]) if False else
                sint[:, tc:tc + 1, :].broadcast_to([PD, 4, HD]), ALU.mult),
                deps=[t2, tst], sig=True)
            t4 = p.emit("dve", lambda E, kv=kv, tc=tc: E.tensor_tensor(
                rtmp2[:, 0:256].rearrange("p (h c) -> p h c", c=HD)[:, :, :], kv[:, :, :],
                cost[:, tc:tc + 1, :].broadcast_to([PD, 4, HD]), ALU.mult),
                deps=[tct], sig=True)
            ktr_tok = p.emit("dve", lambda E, kv=kv: E.tensor_tensor(
                kv[:, :, :], rtmp[:, 0:256].rearrange("p (h c) -> p h c", c=HD)[:, :, :],
                rtmp2[:, 0:256].rearrange("p (h c) -> p h c", c=HD)[:, :, :], ALU.add),
                deps=[t3, t4], sig=True)
        kvf = p.sb("l1aa_kvf", [PD, NH * 64], F32)
        kvb = p.sb("l1aa_kvb", [PD, NH * 64], BF16)
        Ptl = p.sb("l1aa_P", [PD, CH], BF16)
        qd_t = p.sb("l1aa_qdt", [PD, CH], BF16)
        kd_t = p.sb("l1aa_kdt", [PD, 64], BF16)
        ob = p.sb("l1aa_ob", [64, CH], BF16)
        kv0 = p.emit("dve", lambda E: E.memset(kvf[:], 0.0), sig=True)
        kv0 = p.emit("dve", lambda E: E.memset(kvb[:], 0.0), deps=[kv0], sig=True)
        kv_tok = [kv0] * NH
        slopes = [2.0 ** (-0.5 * (i + 1)) for i in range(NH)]
        ostore = None
        for ci in range(NC):
            for h in range(NH):
                kvh = h // 4
                krow = (kvh % 2) * 64
                kch = kvh // 2
                qch, qrow = ((h % 4) + 4 * (h // 8), 64 * ((h // 4) % 2))
                bkS = p.banks[4 + h % 2]
                smm = p.emit("pe", lambda E, ci=ci, krow=krow, kch=kch, qrow=qrow, qch=qch,
                             bkS=bkS:
                             E.matmul(bkS[:, 0:CH],
                                      k_fm[krow:krow + 64, kch, ci * CH:(ci + 1) * CH],
                                      q_fm[qrow:qrow + 64, qch, ci * CH:(ci + 1) * CH],
                                      start=True, stop=True),
                             deps=[kr_tok, rope_tok, ("dve", p.cnt["dve"])], sig=True)
                tp = p.emit("dve", lambda E, h=h, bkS=bkS: E.tensor_tensor(
                    Ptl[:], bkS[:, 0:CH], maskT[:, h, :], ALU.mult),
                    deps=[smm, tmk], sig=True)
                tqd2 = p.emit("dve", lambda E, h=h, ci=ci, qrow=qrow, qch=qch: E.tensor_tensor(
                    qd_t[qrow:qrow + 64, :], q_fm[qrow:qrow + 64, qch, ci * CH:(ci + 1) * CH],
                    qdec[qrow:qrow + 64, h, :], ALU.mult), deps=[tqd], sig=True)
                om1 = p.emit("pe", lambda E, ci=ci, kvh=kvh, h=h: E.matmul(
                    p.banks[h % 2][0:64, 0:CH], v_tm[:, ci, kvh * 64:(kvh + 1) * 64], Ptl[:],
                    start=True, stop=False), deps=[tp, tm_tok, ktr_tok], sig=False)
                om2 = p.emit("pe", lambda E, h=h, qrow=qrow: E.matmul(
                    p.banks[h % 2][0:64, 0:CH], kvb[qrow:qrow + 64, h * 64:(h + 1) * 64],
                    qd_t[qrow:qrow + 64, :], start=False, stop=True),
                    deps=[tqd2, kv_tok[h]], sig=True)
                oev = p.emit("dve", lambda E, h=h: E.tensor_copy(ob[:], p.banks[h % 2][0:64, 0:CH]),
                             deps=[om2, ostore], sig=True)
                ostore = p.dma("sp", Od[h * 64:(h + 1) * 64, ci * CH:(ci + 1) * CH], ob[:],
                               deps=[oev], ch="l1aaow")
                tkd2 = p.emit("dve", lambda E, h=h, ci=ci, kvh=kvh: E.tensor_scalar(
                    kd_t[:], k_tm[:, ci, kvh * 64:(kvh + 1) * 64], kdec[:, h:h + 1], None,
                    ALU.mult), deps=[tkd, ktr_tok], sig=True)
                dmm = p.emit("pe", lambda E, ci=ci, kvh=kvh, h=h: E.matmul(
                    p.banks[2 + h % 2][0:64, 0:64], kd_t[:], v_tm[:, ci, kvh * 64:(kvh + 1) * 64],
                    start=True, stop=True), deps=[tkd2, ("dve", p.cnt["dve"])], sig=True)
                up1 = p.emit("dve", lambda E, h=h, qrow=qrow, bd=float(slopes[h]):
                             E.scalar_tensor_tensor(kvf[qrow:qrow + 64, h * 64:(h + 1) * 64],
                                                    kvf[qrow:qrow + 64, h * 64:(h + 1) * 64],
                                                    float(np.exp(-bd * CH)),
                                                    p.banks[2 + h % 2][0:64, 0:64],
                                                    ALU.mult, ALU.add),
                             deps=[dmm], sig=True)
                kv_tok[h] = p.emit("dve", lambda E, h=h, qrow=qrow: E.tensor_copy(
                    kvb[qrow:qrow + 64, h * 64:(h + 1) * 64],
                    kvf[qrow:qrow + 64, h * 64:(h + 1) * 64]),
                    deps=[up1], sig=True)
        odone = [("l1aaow", p.cnt.get("l1aaow", 0)), ostore]
    with p.scope():
        delta = p.sb("l1aa_dl", [PD, DC, 512], F32)
        upd = []
        dtoks = {}
        def ev_o(j0, t0, w, bk_, tok):
            t1 = p.emit("act", lambda E, j0=j0, t0=t0, w=w, bk_=bk_: E.copy(
                delta[:, j0 // PD, 0:w], bk_[:, 0:w]), deps=[tok, dtoks.get("u")], sig=True)
            dtoks[j0 // PD] = t1
            if j0 // PD == DC - 1:
                sts, _ = emit_x_update(p, g, X, delta, t0, t0 + w, t1)
                dtoks.clear()
                dtoks["u"] = sts[-1] if False else ("dve", p.cnt["dve"])
                return sts[-1]
            return t1
        last = emit_proj_from_hbm(p, g, "l1aaO", W["WoT"], Od, D, ev_o, deps=odone,
                                  banks=(0, 1), tt=512)
    return [last]



def vec_chunks_ap(ap):
    flat = ap
    if len(ap.shape) == 3:
        flat = ap.rearrange("o a c -> (o a c)")
    return flat.rearrange("(c k) -> k c", k=PD)


def emit_l0aa(p, g, X, W, deps=()):
    ln = OuterLN(p, g, X, deps=deps)
    Yh = p.dram("l0aa_Y", [D, T], F32)
    Kd = p.dram("l0aa_K", [D, T], F32)
    Vd = p.dram("l0aa_V", [D, T], F32)
    Rd = p.dram("l0aa_R", [D, T], F32)
    Yr = Yh.rearrange("(c k) t -> k c t", k=PD)
    Xr = X.rearrange("(c k) t -> k c t", k=PD)
    bank7 = p.banks[7]
    with p.scope():
        vecs = {}
        vtoks = {}
        for nm in ["time_decay", "time_first", "ln1_g", "ln1_b", "ln2_g", "ln2_b",
                   "mix_k", "mix_v", "mix_r", "cmix_k", "cmix_r"]:
            vecs[nm] = p.sb("l0aa_" + nm, [PD, DC], F32)
            vtoks[nm] = p.dma("sp", vecs[nm][:], vec_chunks_ap(W[nm]), ch="v" + nm, slow=True)
        vready = list(vtoks.values())

        stats1 = p.sb("l0aa_st1", [PD, 64], F32)
        with p.scope():
            xt = [p.sb("l0aa_x", [PD, DC, 512], F32) for _ in range(2)]
            sB = [p.sb("l0aa_s", [PD, 512], F32) for _ in range(2)]
            bB = [p.sb("l0aa_b", [PD, 512], F32) for _ in range(2)]
            yt = [p.sb("l0aa_yt", [PD, DC, 512], F32) for _ in range(2)]
            sq = [p.sb("l0aa_sq", [PD, DC, 512], F32) for _ in range(2)]
            war = [None, None]
            ystore = [None, None]
            evac_tok = None
            st1 = None
            for tt in range(NT5):
                b = tt % 2
                dl = list(deps) + [war[b]] + ln.ready
                tl = p.dma("sp", xt[b][:], Xr[:, :, tt * 512:(tt + 1) * 512], deps=dl, ch=f"lax{b}")
                tsb = p.dma("sp", sB[b][:], row_bcast_ap(ln.srow_d, tt * 512, 512), deps=dl,
                            ch=f"las{b}")
                tbb = p.dma("sp", bB[b][:], row_bcast_ap(ln.brow_d, tt * 512, 512), deps=dl,
                            ch=f"lab{b}")
                sq_toks = []
                ylast = None
                for dc in range(DC):
                    tp = p.emit("pool", lambda E, a=xt[b], o=yt[b], s_=sB[b], dc=dc:
                                E.tensor_mul(o[:, dc], a[:, dc], s_[:]),
                                deps=[tl, tsb, war[b], ystore[b]], sig=True)
                    ylast = p.emit("dve", lambda E, o=yt[b], b_=bB[b], dc=dc:
                                   E.tensor_tensor(o[:, dc], o[:, dc], b_[:], ALU.add),
                                   deps=[tp, tbb], sig=True)
                    sq_toks.append(p.emit("pool", lambda E, a=yt[b], o=sq[b], dc=dc:
                                          E.tensor_mul(o[:, dc], a[:, dc], a[:, dc]),
                                          deps=[ylast], sig=True))
                ystore[b] = p.dma("sp", Yr[:, :, tt * 512:(tt + 1) * 512], yt[b][:],
                                  deps=[ylast], ch=f"law{b}")
                pe_last = None
                for sub in range(4):
                    for cc2 in range(2):
                        src = yt[b] if cc2 == 0 else sq[b]
                        dtk = [ylast] if cc2 == 0 else sq_toks
                        for dc in range(DC):
                            pe_last = p.emit("pe", lambda E, a=src, dc=dc, sub=sub, tt=tt,
                                             cc2=cc2, st=(dc == 0):
                                             E.matmul(bank7[:, tt * 8 + sub * 2 + cc2:
                                                            tt * 8 + sub * 2 + cc2 + 1],
                                                      a[:, dc, sub * 128:(sub + 1) * 128],
                                                      g["ones_f"][:], start=st,
                                                      stop=(dc == DC - 1)),
                                             deps=[dtk[dc if cc2 else 0] if cc2 else ylast,
                                                   evac_tok] + g["ready"],
                                             sig=(sub == 3 and cc2 == 1 and dc == DC - 1))
                war[b] = pe_last
                st1 = p.emit("act", lambda E, tt=tt: E.copy(stats1[:, tt * 8:(tt + 1) * 8],
                                                            bank7[:, tt * 8:(tt + 1) * 8]),
                             deps=[pe_last], sig=True)
                evac_tok = st1
        ydone = [("law0", p.cnt.get("law0", 0)), ("law1", p.cnt.get("law1", 0))]
        s1row_d = p.dram("l0aa_s1r", [32, PD], F32)
        b1row_d = p.dram("l0aa_b1r", [32, PD], F32)
        s1_tok, b1_tok = emit_ln_smalls(p, g, stats1, s1row_d, b1row_d, st1, kind="ln")

        with p.scope():
            wtk = p.sb("l0aa_wtk", [PD, DC, D], BF16)
            wtv = p.sb("l0aa_wtv", [PD, DC, D], BF16)
            wtr = p.sb("l0aa_wtr", [PD, DC, D], BF16)
            t_k = p.dma("pool", wtk[:], W["w_tk"].rearrange("(c k) m -> k c m", k=PD), ch="lawk")
            t_v = p.dma("pool", wtv[:], W["w_tv"].rearrange("(c k) m -> k c m", k=PD), ch="lawv")
            t_r = p.dma("pool", wtr[:], W["w_tr"].rearrange("(c k) m -> k c m", k=PD), ch="lawr")
            ytl = [p.sb("la2_y", [PD, DC, 512], F32) for _ in range(2)]
            sB = [p.sb("la2_s", [PD, 512], F32) for _ in range(2)]
            bB = [p.sb("la2_b", [PD, 512], F32) for _ in range(2)]
            ht = [p.sb("la2_h", [PD, DC, 513], F32) for _ in range(2)]
            xk = p.sb("la2_xk", [PD, DC, 512], BF16)
            xv = p.sb("la2_xv", [PD, DC, 512], BF16)
            xr = p.sb("la2_xr", [PD, DC, 512], BF16)
            dfs = p.sb("la2_df", [PD, DC, 512], F32)
            evb = [p.sb("la2_ev", [PD, 512], F32) for _ in range(2)]
            war = [None, None]
            hprev = [None, None]
            mix_tok = None
            carry_tok = None
            store_toks = []
            bank_war = {}
            est = {}
            for tc in range(NT5):
                b = tc % 2
                dl = [s1_tok, b1_tok, war[b]] + ydone + vready
                tl = p.dma("sp", ytl[b][:], Yr[:, :, tc * 512:(tc + 1) * 512], deps=dl,
                           ch=f"la2y{b}")
                tsb = p.dma("sp", sB[b][:], row_bcast_ap(s1row_d, tc * 512, 512), deps=dl,
                            ch=f"la2s{b}")
                tbb = p.dma("sp", bB[b][:], row_bcast_ap(b1row_d, tc * 512, 512), deps=dl,
                            ch=f"la2b{b}")
                h_toks = []
                for dc in range(DC):
                    t1 = p.emit("dve", lambda E, a=ytl[b], o=ht[b], s_=sB[b], dc=dc, vecs=vecs:
                                E.scalar_tensor_tensor(o[:, dc, 1:513], a[:, dc],
                                                       vecs["ln1_g"][:, dc:dc + 1], s_[:],
                                                       ALU.mult, ALU.mult),
                                deps=[tl, tsb, war[b], hprev[b], mix_tok], sig=True)
                    t2 = p.emit("dve", lambda E, o=ht[b], b_=bB[b], dc=dc, vecs=vecs:
                                E.scalar_tensor_tensor(o[:, dc, 1:513], b_[:],
                                                       vecs["ln1_g"][:, dc:dc + 1],
                                                       o[:, dc, 1:513], ALU.mult, ALU.add),
                                deps=[tbb], sig=True)
                    t3 = p.emit("dve", lambda E, o=ht[b], dc=dc, vecs=vecs:
                                E.tensor_scalar(o[:, dc, 1:513], o[:, dc, 1:513],
                                                vecs["ln1_b"][:, dc:dc + 1], None, ALU.add),
                                sig=True)
                    h_toks.append(t3)
                if tc == 0:
                    carry_tok = p.emit("dve", lambda E, o=ht[b]: E.memset(o[:, :, 0:1], 0.0),
                                       deps=[hprev[b]], sig=True)
                else:
                    carry_tok = p.emit("dve", lambda E, o=ht[b], pv=ht[1 - b]:
                                       E.tensor_copy(o[:, :, 0:1], pv[:, :, 512:513]),
                                       deps=[h_toks[-1], hprev[1 - b]], sig=True)
                mix_last = None
                for dc in range(DC):
                    td = p.emit("dve", lambda E, o=ht[b], dc=dc, dfs=dfs:
                                E.tensor_tensor(dfs[:, dc], o[:, dc, 1:513], o[:, dc, 0:512],
                                                ALU.subtract),
                                deps=[h_toks[dc], carry_tok], sig=True)
                    for nm, dst in [("mix_k", xk), ("mix_v", xv), ("mix_r", xr)]:
                        mix_last = p.emit("dve", lambda E, o=ht[b], dc=dc, nm=nm, dst=dst, dfs=dfs, vecs=vecs:
                                          E.scalar_tensor_tensor(dst[:, dc], dfs[:, dc],
                                                                 vecs[nm][:, dc:dc + 1],
                                                                 o[:, dc, 0:512],
                                                                 ALU.mult, ALU.add),
                                          deps=[td], sig=True)
                mix_tok = mix_last
                pi = 0
                for which, (src, wsb2, dst, tw) in enumerate(
                        [(xk, wtk, Kd, t_k), (xv, wtv, Vd, t_v), (xr, wtr, Rd, t_r)]):
                    src, wsb2, dst, tw = [(xk, wtk, Kd, t_k), (xv, wtv, Vd, t_v),
                                          (xr, wtr, Rd, t_r)][which]
                    for jc in range(DC):
                        bkp = p.banks[pi % 2]
                        bw = bank_war.get(pi % 2)
                        pi += 1
                        tok = None
                        for kc in range(DC):
                            tok = p.emit("pe", lambda E, kc=kc, jc=jc, src=src, wsb2=wsb2,
                                         bkp=bkp, st=(kc == 0):
                                         E.matmul(bkp[:, 0:512],
                                                  wsb2[:, kc, jc * PD:(jc + 1) * PD],
                                                  src[:, kc, :], start=st, stop=(kc == DC - 1)),
                                         deps=[tw, mix_tok, bw], sig=(kc == DC - 1))
                        b2 = pi % 2
                        if which == 2:
                            tev = p.emit("act", lambda E, bkp=bkp, b2=b2: E.activation(
                                evb[b2][:], bkp[:, 0:512], AF.Sigmoid),
                                deps=[tok, est.get(b2)], sig=True)
                        else:
                            tev = p.emit("act", lambda E, bkp=bkp, b2=b2: E.copy(
                                evb[b2][:], bkp[:, 0:512]), deps=[tok, est.get(b2)], sig=True)
                        bank_war[(pi - 1) % 2] = tev
                        tst = p.dma("sp", dst[jc * PD:(jc + 1) * PD, tc * 512:(tc + 1) * 512],
                                    evb[b2][:], deps=[tev], ch=f"lakv{b2}")
                        est[b2] = tst
                        store_toks.append(tst)
                war[b] = mix_tok
                hprev[b] = mix_tok
            kvr_done = [("lakv0", p.cnt.get("lakv0", 0)), ("lakv1", p.cnt.get("lakv1", 0))]

        with p.scope():
            kt = [p.sb("lb_k", [PD, 512], F32) for _ in range(2)]
            vt = [p.sb("lb_v", [PD, 512], F32) for _ in range(2)]
            rt = [p.sb("lb_r", [PD, 512], F32) for _ in range(2)]
            ybt = [p.sb("lb_y", [PD, 512], F32) for _ in range(2)]
            ekm = p.sb("lb_ekm", [PD, 512], F32)
            euk = p.sb("lb_euk", [PD, 512], F32)
            ewk = p.sb("lb_ewk", [PD, 512], F32)
            gt = p.sb("lb_g", [PD, 512], F32)
            gkv = p.sb("lb_gkv", [PD, 512], F32)
            num = p.sb("lb_num", [PD, 512], F32)
            den = p.sb("lb_den", [PD, 512], F32)
            Dbuf = p.sb("lb_D", [PD, 513], F32)
            Mbuf = p.sb("lb_M", [PD, 513], F32)
            sm = p.sb("lb_sm", [PD, 8], F32)
            war = [None, None]
            ystw = [None, None]
            for cc in range(DC):
                i0 = p.emit("dve", lambda E: E.memset(sm[:], 0.0), sig=True)
                i1 = p.emit("dve", lambda E, cc=cc, vecs=vecs: E.tensor_copy(sm[:, 2:3],
                                                         vecs["time_first"][:, cc:cc + 1]),
                            deps=[i0] + vready, sig=True)
                i2 = p.emit("act", lambda E, cc=cc, vecs=vecs: E.activation(sm[:, 3:4],
                                                        vecs["time_decay"][:, cc:cc + 1],
                                                        AF.Exp), deps=[i1], sig=True)
                i3 = p.emit("dve", lambda E: E.memset(Dbuf[:, 0:1], 0.0), sig=True)
                i4 = p.emit("dve", lambda E: E.memset(Mbuf[:, 0:1], 0.0), sig=True)
                scan_tok = i4
                for tc in range(NT5):
                    b = tc % 2
                    dl = kvr_done + [war[b]]
                    tkl = p.dma("sp", kt[b][:], Kd[cc * PD:(cc + 1) * PD, tc * 512:(tc + 1) * 512],
                                deps=dl, ch=f"lbk{b}")
                    tvl = p.dma("sp", vt[b][:], Vd[cc * PD:(cc + 1) * PD, tc * 512:(tc + 1) * 512],
                                deps=dl, ch=f"lbv{b}")
                    trl = p.dma("sp", rt[b][:], Rd[cc * PD:(cc + 1) * PD, tc * 512:(tc + 1) * 512],
                                deps=dl, ch=f"lbr{b}")
                    tyl = p.dma("sp", ybt[b][:], Yr[:, cc, tc * 512:(tc + 1) * 512],
                                deps=[war[b], ystw[b]] + ydone, ch=f"lby{b}")
                    e1 = p.emit("act", lambda E, b=b: E.activation(ekm[:], kt[b][:], AF.Exp,
                                                                   bias=sm[:, 1:2]),
                                deps=[tkl, i2, scan_tok], sig=True)
                    e2 = p.emit("act", lambda E, b=b: E.activation(euk[:], kt[b][:], AF.Exp,
                                                                   bias=sm[:, 2:3]),
                                deps=[tkl], sig=True)
                    e3 = p.emit("dve", lambda E: E.tensor_scalar(ewk[:], ekm[:], sm[:, 3:4],
                                                                 None, ALU.mult),
                                deps=[e1, i2], sig=True)
                    s1 = p.emit("dve", lambda E: E.tensor_tensor_scan(
                        Dbuf[:, 1:513], sm[:, 3:4].broadcast_to([PD, 512]), ewk[:],
                        Dbuf[:, 0:1], ALU.mult, ALU.add), deps=[e3], sig=True)
                    g1 = p.emit("dve", lambda E: E.tensor_tensor(gt[:], Dbuf[:, 0:512], ekm[:],
                                                                 ALU.max), deps=[s1], sig=True)
                    g2 = p.emit("dve", lambda E: E.reciprocal(gt[:], gt[:]), sig=True)
                    g3 = p.emit("dve", lambda E: E.tensor_tensor(gt[:], gt[:], Dbuf[:, 1:513],
                                                                 ALU.mult), sig=True)
                    g4 = p.emit("dve", lambda E: E.tensor_tensor(gkv[:], gt[:], ekm[:], ALU.mult),
                                sig=True)
                    g5 = p.emit("dve", lambda E, b=b: E.tensor_tensor(gkv[:], gkv[:], vt[b][:],
                                                                      ALU.mult),
                                deps=[tvl], sig=True)
                    s2 = p.emit("dve", lambda E: E.tensor_tensor_scan(
                        Mbuf[:, 1:513], gt[:], gkv[:], Mbuf[:, 0:1], ALU.mult, ALU.add),
                        deps=[g5], sig=True)
                    y1 = p.emit("dve", lambda E, b=b: E.tensor_tensor(num[:], euk[:], vt[b][:],
                                                                      ALU.mult),
                                deps=[e2, tvl], sig=True)
                    y2 = p.emit("dve", lambda E: E.tensor_tensor(num[:], num[:], Mbuf[:, 0:512],
                                                                 ALU.add), deps=[s2], sig=True)
                    y3 = p.emit("dve", lambda E: E.tensor_tensor(den[:], Dbuf[:, 0:512], euk[:],
                                                                 ALU.add), sig=True)
                    y4 = p.emit("dve", lambda E: E.reciprocal(den[:], den[:]), sig=True)
                    y5 = p.emit("dve", lambda E: E.tensor_tensor(num[:], num[:], den[:],
                                                                 ALU.mult), sig=True)
                    u1 = p.emit("dve", lambda E, b=b: E.tensor_tensor(num[:], num[:], rt[b][:],
                                                                      ALU.mult),
                                deps=[trl], sig=True)
                    u2 = p.emit("dve", lambda E, b=b: E.tensor_tensor(ybt[b][:], ybt[b][:],
                                                                      num[:], ALU.add),
                                deps=[tyl], sig=True)
                    ystw[b] = p.dma("sp", Yr[:, cc, tc * 512:(tc + 1) * 512], ybt[b][:],
                                    deps=[u2], ch=f"lbyw{b}")
                    war[b] = u2
                    r1 = p.emit("act", lambda E: E.activation(sm[:, 4:5], Dbuf[:, 512:513],
                                                              AF.Ln), deps=[s2], sig=True)
                    r2 = p.emit("dve", lambda E: E.tensor_tensor(sm[:, 0:1], sm[:, 0:1],
                                                                 sm[:, 4:5], ALU.add),
                                deps=[r1], sig=True)
                    r3 = p.emit("dve", lambda E: E.tensor_scalar(sm[:, 1:2], sm[:, 0:1], -1.0,
                                                                 None, ALU.mult), sig=True)
                    r4 = p.emit("dve", lambda E, cc=cc, vecs=vecs: E.tensor_tensor(
                        sm[:, 2:3], vecs["time_first"][:, cc:cc + 1], sm[:, 0:1], ALU.subtract),
                        sig=True)
                    r5 = p.emit("dve", lambda E: E.reciprocal(sm[:, 5:6], Dbuf[:, 512:513]),
                                sig=True)
                    r6 = p.emit("dve", lambda E: E.tensor_tensor(Mbuf[:, 0:1], Mbuf[:, 512:513],
                                                                 sm[:, 5:6], ALU.mult), sig=True)
                    scan_tok = p.emit("dve", lambda E: E.memset(Dbuf[:, 0:1], 1.0), sig=True)
            yw_done = [("lbyw0", p.cnt.get("lbyw0", 0)), ("lbyw1", p.cnt.get("lbyw1", 0))]

        stats2 = p.sb("l0aa_st2", [PD, 64], F32)
        st2 = emit_ln_stats(p, g, Yh, stats2, deps=yw_done)
        s2row_d = p.dram("l0aa_s2r", [32, PD], F32)
        b2row_d = p.dram("l0aa_b2r", [32, PD], F32)
        s2_tok, b2_tok = emit_ln_smalls(p, g, stats2, s2row_d, b2row_d, st2, kind="ln")

        with p.scope():
            wcv_all = p.sb("lc_wcv", [PD, 32, D], BF16)
            tcv = p.dma("pool", wcv_all[:], W["w_cv"].rearrange("(c k) m -> k c m", k=PD),
                        ch="lcwcv")
            wcr = p.sb("lc_wcr", [PD, DC, D], BF16)
            tcr = p.dma("pool", wcr[:], W["w_cr"].rearrange("(c k) m -> k c m", k=PD), ch="lcwcr")
            wckb = [p.sb("lc_wck", [PD, DC, 512], BF16)]
            war_ck = [None]
            yct = p.sb("lc_y", [PD, DC, 512], F32)
            sB = p.sb("lc_s", [PD, 512], F32)
            bB = p.sb("lc_b", [PD, 512], F32)
            ht = p.sb("lc_h", [PD, DC, 513], F32)
            carryT = p.sb("lc_carry", [PD, DC, 1], F32)
            ck = p.sb("lc_ck", [PD, DC, 512], BF16)
            cr = p.sb("lc_cr", [PD, DC, 512], BF16)
            dfc = p.sb("lc_df", [PD, 512], F32)
            kst = p.sb("lc_kst", [PD, 32, 512], BF16)
            scr = p.sb("lc_scr", [PD, DC, 512], BF16)
            rl = p.sb("lc_rl", [PD, 512], F32)
            xt2 = p.sb("lc_x", [PD, DC, 512], F32)
            yload = None
            xstore = None
            mix_tok = None
            kst_war = None
            scr_war = None
            upd = []
            for tc in range(NT5):
                dl = [s2_tok, b2_tok, yload] + yw_done
                tl = p.dma("sp", yct[:], Yr[:, :, tc * 512:(tc + 1) * 512], deps=dl, ch="lcy")
                tsb = p.dma("sp", sB[:], row_bcast_ap(s2row_d, tc * 512, 512), deps=dl, ch="lcs")
                tbb = p.dma("sp", bB[:], row_bcast_ap(b2row_d, tc * 512, 512), deps=dl, ch="lcb")
                if tc == 0:
                    carry_tok = p.emit("dve", lambda E: E.memset(carryT[:], 0.0), sig=True)
                else:
                    carry_tok = p.emit("dve", lambda E, ht=ht: E.tensor_copy(carryT[:], ht[:, :, 512:513]),
                                       deps=[mix_tok], sig=True)
                h_toks = []
                for dc in range(DC):
                    t1 = p.emit("dve", lambda E, o=ht, dc=dc, vecs=vecs, yct=yct, sB=sB:
                                E.scalar_tensor_tensor(o[:, dc, 1:513], yct[:, dc],
                                                       vecs["ln2_g"][:, dc:dc + 1], sB[:],
                                                       ALU.mult, ALU.mult),
                                deps=[tl, tsb, carry_tok, mix_tok], sig=True)
                    t2 = p.emit("dve", lambda E, o=ht, dc=dc, vecs=vecs, bB=bB:
                                E.scalar_tensor_tensor(o[:, dc, 1:513], bB[:],
                                                       vecs["ln2_g"][:, dc:dc + 1],
                                                       o[:, dc, 1:513], ALU.mult, ALU.add),
                                deps=[tbb], sig=True)
                    t3 = p.emit("dve", lambda E, o=ht, dc=dc, vecs=vecs:
                                E.tensor_scalar(o[:, dc, 1:513], o[:, dc, 1:513],
                                                vecs["ln2_b"][:, dc:dc + 1], None, ALU.add),
                                sig=True)
                    h_toks.append(t3)
                ccar = p.emit("dve", lambda E, ht=ht: E.tensor_copy(ht[:, :, 0:1], carryT[:]),
                              deps=[h_toks[-1]], sig=True)
                mix_last = None
                for dc in range(DC):
                    td = p.emit("dve", lambda E, o=ht, dc=dc, dfc=dfc:
                                E.tensor_tensor(dfc[:], o[:, dc, 1:513], o[:, dc, 0:512],
                                                ALU.subtract),
                                deps=[h_toks[dc], ccar], sig=True)
                    for nm, dst in [("cmix_k", ck), ("cmix_r", cr)]:
                        mix_last = p.emit("dve", lambda E, o=ht, dc=dc, nm=nm, dst=dst,
                                          dfc=dfc, vecs=vecs:
                                          E.scalar_tensor_tensor(dst[:, dc], dfc[:],
                                                                 vecs[nm][:, dc:dc + 1],
                                                                 o[:, dc, 0:512],
                                                                 ALU.mult, ALU.add),
                                          deps=[td], sig=True)
                mix_tok = mix_last
                yload = mix_tok
                pi = 0
                bank_war2 = {}
                for jc in range(DC):
                    bkp = p.banks[pi % 2]
                    bw = bank_war2.get(pi % 2)
                    pi += 1
                    tok = None
                    for kc in range(DC):
                        tok = p.emit("pe", lambda E, kc=kc, jc=jc, bkp=bkp, st=(kc == 0):
                                     E.matmul(bkp[:, 0:512], wcr[:, kc, jc * PD:(jc + 1) * PD],
                                              cr[:, kc, :], start=st, stop=(kc == DC - 1)),
                                     deps=[tcr, mix_tok, bw, scr_war], sig=(kc == DC - 1))
                    scr_tok = p.emit("act", lambda E, jc=jc, bkp=bkp: E.activation(
                        scr[:, jc, :], bkp[:, 0:512], AF.Sigmoid), deps=[tok], sig=True)
                    bank_war2[(pi - 1) % 2] = scr_tok
                scr_war = scr_tok
                kst_tok = None
                for ffb in range(8):
                    wb = 0
                    twc = p.dma("pool", wckb[wb][:],
                                W["w_ck"][:, ffb * 512:(ffb + 1) * 512]
                                .rearrange("(c k) m -> k c m", k=PD),
                                deps=[war_ck[wb]], ch=f"lcck{wb}")
                    for ffc in range(4):
                        bkp = p.banks[pi % 2]
                        bw = bank_war2.get(pi % 2)
                        pi += 1
                        tok = None
                        for kc in range(DC):
                            tok = p.emit("pe", lambda E, kc=kc, ffc=ffc, wb=wb, bkp=bkp,
                                         st=(kc == 0):
                                         E.matmul(bkp[:, 0:512],
                                                  wckb[wb][:, kc, ffc * 128:(ffc + 1) * 128],
                                                  ck[:, kc, :], start=st, stop=(kc == DC - 1)),
                                         deps=[twc, mix_tok, bw, kst_war], sig=(kc == DC - 1))
                        tr1 = p.emit("dve", lambda E, bkp=bkp: E.tensor_scalar_max(
                            rl[:], bkp[:, 0:512], 0.0), deps=[tok, kst_tok], sig=True)
                        kst_tok = p.emit("act", lambda E, ffb=ffb, ffc=ffc: E.activation(
                            kst[:, ffb * 4 + ffc, :], rl[:], AF.Square), deps=[tr1], sig=True)
                        bank_war2[(pi - 1) % 2] = tr1
                    war_ck[wb] = kst_tok
                kst_war = kst_tok
                tlx = p.dma("sp", xt2[:], Xr[:, :, tc * 512:(tc + 1) * 512],
                            deps=[xstore, kst_tok], ch="lcx")
                add_last = None
                for dc2 in range(DC):
                    bko = p.banks[2 + dc2 // 2]
                    half = (dc2 % 2) * 256
                    for hf in range(2):
                        mm_last = None
                        for ffi in range(32):
                            mm_last = p.emit("pe", lambda E, ffi=ffi, dc2=dc2, bko=bko,
                                             half=half, hf=hf, st=(ffi == 0):
                                             E.matmul(bko[:, half:half + 256],
                                                      wcv_all[:, ffi, dc2 * PD:(dc2 + 1) * PD],
                                                      kst[:, ffi, hf * 256:(hf + 1) * 256],
                                                      start=st, stop=(ffi == 31)),
                                             deps=[kst_tok, tcv, add_last],
                                             sig=(ffi == 31))
                        e1 = p.emit("dve", lambda E, dc2=dc2, bko=bko, half=half, hf=hf:
                                    E.tensor_tensor(rl[:, 0:256], bko[:, half:half + 256],
                                                    scr[:, dc2, hf * 256:(hf + 1) * 256],
                                                    ALU.mult),
                                    deps=[mm_last, tlx], sig=True)
                        e2 = p.emit("dve", lambda E, dc2=dc2, hf=hf:
                                    E.tensor_tensor(xt2[:, dc2, hf * 256:(hf + 1) * 256],
                                                    xt2[:, dc2, hf * 256:(hf + 1) * 256],
                                                    yct[:, dc2, hf * 256:(hf + 1) * 256],
                                                    ALU.add), sig=True)
                        add_last = p.emit("dve", lambda E, dc2=dc2, hf=hf:
                                          E.tensor_tensor(xt2[:, dc2, hf * 256:(hf + 1) * 256],
                                                          xt2[:, dc2, hf * 256:(hf + 1) * 256],
                                                          rl[:, 0:256], ALU.add), sig=True)
                xstore = p.dma("sp", Xr[:, :, tc * 512:(tc + 1) * 512], xt2[:],
                               deps=[add_last], ch="lcxw")
                upd = [xstore]
    return [("lcxw", p.cnt.get("lcxw", 0))]



def _std_prog(wspec, consts_fn, emitfn):
    p = Prog()
    c = Consts(p)
    p.alloc_banks()
    X = p.nc.dram_tensor("Xio", [D, T], F32, kind="ExternalInput").ap()
    Xout = p.nc.dram_tensor("Xout", [D, T], F32, kind="ExternalOutput").ap()
    W = {}
    for nm, shp in wspec:
        W[nm] = p.nc.dram_tensor(nm, list(shp), F32, kind="ExternalInput").ap()
    if consts_fn is not None:
        consts_fn(c, W)
    g = load_globals(p, c)
    Xs = p.dram("Xs", [D, T], F32)
    cp = []
    for dc in range(DC):
        cp.append(p.dma("sp", Xs[dc * PD:(dc + 1) * PD, :], X[dc * PD:(dc + 1) * PD, :], ch="st"))
    cp = cp[-1:]
    fin = emitfn(p, g, Xs, W, deps=cp)
    out_toks = []
    for dc in range(DC):
        out_toks.append(p.dma("sp", Xout[dc * PD:(dc + 1) * PD, :], Xs[dc * PD:(dc + 1) * PD, :],
                              deps=list(fin), ch="out"))
    p.emit("sp", lambda E: E.nop(), deps=out_toks[-1:], sig=True)
    nc = p.finalize()
    return p, c, nc


def _l1aa_consts(c, W):
    NH, HD, CH = 16, 64, 128
    inv = 1.0 / 10000.0 ** (np.arange(0, HD, 2, dtype=np.float64) / HD)
    ang = np.outer(np.arange(T, dtype=np.float64), inv)
    emb = np.concatenate([ang, ang], axis=1)
    cos = np.cos(emb).astype(np.float32)
    sin = np.sin(emb).astype(np.float32)
    import ml_dtypes
    cos_fm = np.tile(cos.T, (2, 1)).astype(ml_dtypes.bfloat16)
    sin_fm = np.tile(sin.T, (2, 1)).astype(ml_dtypes.bfloat16)
    cos_tm = cos.reshape(T // PD, PD, HD).transpose(1, 0, 2).astype(ml_dtypes.bfloat16)
    sin_tm = sin.reshape(T // PD, PD, HD).transpose(1, 0, 2).astype(ml_dtypes.bfloat16)
    perm = np.zeros((HD, HD), np.float32)
    for i in range(32):
        perm[i, 32 + i] = -1.0
        perm[32 + i, i] = 1.0
    permT = np.kron(np.eye(2, dtype=np.float32), perm.T).astype(ml_dtypes.bfloat16)
    sl = np.array([2.0 ** (-0.5 * (i + 1)) for i in range(NH)], np.float64)
    t_ = np.arange(CH, dtype=np.float64)
    m = np.where(t_[:, None] >= t_[None, :], np.exp(-sl[:, None, None] * (t_[:, None] - t_[None, :])), 0.0)
    maskT = m.transpose(2, 0, 1).astype(np.float32)
    maskT = np.ascontiguousarray(maskT.transpose(0, 1, 2)).astype(ml_dtypes.bfloat16)
    qdec = np.exp(-sl[:, None] * (t_ + 1.0))
    qdec = np.broadcast_to(qdec[None, :, :], (128, NH, CH)).astype(ml_dtypes.bfloat16)
    kdec = np.exp(-sl[None, :] * (CH - 1.0 - t_[:, None])).astype(np.float32)
    W["c_permT"] = c.add("c_permT", np.ascontiguousarray(permT))
    W["c_cos_fm"] = c.add("c_cos_fm", np.ascontiguousarray(cos_fm))
    W["c_sin_fm"] = c.add("c_sin_fm", np.ascontiguousarray(sin_fm))
    W["c_cos_tm"] = c.add("c_cos_tm", np.ascontiguousarray(cos_tm))
    W["c_sin_tm"] = c.add("c_sin_tm", np.ascontiguousarray(sin_tm))
    W["c_maskT"] = c.add("c_maskT", np.ascontiguousarray(maskT))
    W["c_qdec"] = c.add("c_qdec", np.ascontiguousarray(qdec))
    W["c_kdec"] = c.add("c_kdec", np.ascontiguousarray(kdec))


def _l0ab_consts(c, W):
    import ml_dtypes
    W["c_iota128"] = c.add("c_iota128", np.arange(PD, dtype=np.float32).reshape(PD, 1))
    tri = (np.arange(PD)[None, :] >= np.arange(PD)[:, None]).astype(ml_dtypes.bfloat16)
    W["c_trimask"] = c.add("c_trimask", np.ascontiguousarray(tri))


SUBS = {
    "l0aa": dict(
        wspec=[("time_decay", [D]), ("time_first", [D]), ("w_tk", [D, D]),
               ("w_tv", [D, D]), ("w_tr", [D, D]), ("w_ck", [D, 4 * D]),
               ("w_cv", [4 * D, D]), ("w_cr", [D, D]), ("ln1_g", [D]), ("ln1_b", [D]),
               ("ln2_g", [D]), ("ln2_b", [D]), ("mix_k", [1, 1, D]), ("mix_v", [1, 1, D]),
               ("mix_r", [1, 1, D]), ("cmix_k", [1, 1, D]), ("cmix_r", [1, 1, D])],
        consts=None, emit=None,
        prep=lambda ins: {
            "time_decay": ins["L0_attn_a_time_decay"], "time_first": ins["L0_attn_a_time_first"],
            "w_tk": ins["L0_attn_a_w_tk"], "w_tv": ins["L0_attn_a_w_tv"],
            "w_tr": ins["L0_attn_a_w_tr"], "w_ck": ins["L0_attn_a_w_ck"],
            "w_cv": ins["L0_attn_a_w_cv"], "w_cr": ins["L0_attn_a_w_cr"],
            "ln1_g": ins["L0_attn_a_ln1_g"], "ln1_b": ins["L0_attn_a_ln1_b"],
            "ln2_g": ins["L0_attn_a_ln2_g"], "ln2_b": ins["L0_attn_a_ln2_b"],
            "mix_k": ins["L0_attn_a_mix_k"], "mix_v": ins["L0_attn_a_mix_v"],
            "mix_r": ins["L0_attn_a_mix_r"], "cmix_k": ins["L0_attn_a_cmix_k"],
            "cmix_r": ins["L0_attn_a_cmix_r"]}),
    "l0ab": dict(
        wspec=[("W_qk", [D, D]), ("WvT", [D, D]), ("alibi_slope", [1])],
        consts=_l0ab_consts, emit=None,
        prep=lambda ins: {
            "W_qk": ins["L0_attn_b_W_qk"],
            "WvT": np.ascontiguousarray(ins["L0_attn_b_W_v"].T),
            "alibi_slope": ins["L0_attn_b_alibi_slope"]}),
    "l1aa": dict(
        wspec=[("WqT", [D, D]), ("WkT", [D, 256]), ("WvT", [D, 256]), ("WoT", [D, D])],
        consts=_l1aa_consts, emit=None,
        prep=lambda ins: {
            "WqT": np.ascontiguousarray(ins["L1_attn_a_Wq"].T),
            "WkT": np.ascontiguousarray(ins["L1_attn_a_Wk"].T),
            "WvT": np.ascontiguousarray(ins["L1_attn_a_Wv"].T),
            "WoT": np.ascontiguousarray(ins["L1_attn_a_Wo"].T)}),
    "l1ab": dict(
        wspec=[("WqT", [D, D]), ("WkT", [D, D]), ("WvT", [D, D]), ("WpT", [D, D]),
               ("bq", [D]), ("bk", [D]), ("bv", [D]), ("bp", [D]),
               ("lambda_q1", [8, 1, 64]), ("lambda_k1", [8, 1, 64]),
               ("lambda_q2", [8, 1, 64]), ("lambda_k2", [8, 1, 64])],
        consts=None, emit=None,
        prep=lambda ins: {
            "WqT": np.ascontiguousarray(ins["L1_attn_b_Wq"].T),
            "WkT": np.ascontiguousarray(ins["L1_attn_b_Wk"].T),
            "WvT": np.ascontiguousarray(ins["L1_attn_b_Wv"].T),
            "WpT": np.ascontiguousarray(ins["L1_attn_b_Wp"].T),
            "bq": ins["L1_attn_b_bq"], "bk": ins["L1_attn_b_bk"],
            "bv": ins["L1_attn_b_bv"], "bp": ins["L1_attn_b_bp"],
            "lambda_q1": ins["L1_attn_b_lambda_q1"], "lambda_k1": ins["L1_attn_b_lambda_k1"],
            "lambda_q2": ins["L1_attn_b_lambda_q2"], "lambda_k2": ins["L1_attn_b_lambda_k2"]}),
}


def build_sub_prog(name):
    spec = SUBS[name]
    emit = {"l0aa": emit_l0aa, "l0ab": emit_l0ab, "l1aa": emit_l1aa, "l1ab": emit_l1ab}[name]
    return _std_prog(spec["wspec"], spec["consts"], emit)


def prep_l1ffn_weights(ins):
    return {"w_gate": ins["L1_ffn_w_gate"], "w1": ins["L1_ffn_w1"],
            "w3": ins["L1_ffn_w3"], "w2": ins["L1_ffn_w2"]}


_PROG_CACHE = {}


def _get_prog(name):
    if name not in _PROG_CACHE:
        if name == "l1ffn":
            _PROG_CACHE[name] = build_l1ffn_prog()
        elif name == "l0ffn":
            _PROG_CACHE[name] = build_l0ffn_prog()
        else:
            _PROG_CACHE[name] = build_sub_prog(name)
    return _PROG_CACHE[name]


CHAIN = ["l0aa", "l0ffn", "l0ab", "l0ffn", "l1aa", "l1ffn", "l1ab", "l1ffn"]


def kernel(**inputs):
    x = np.asarray(inputs["L0_attn_a_x"], dtype=np.float32)
    B = x.shape[0]
    Xs = [np.ascontiguousarray(x[b].T) for b in range(B)]
    for name in CHAIN:
        p, c, nc = _get_prog(name)
        if name == "l1ffn":
            Wnp = prep_l1ffn_weights(inputs)
        elif name == "l0ffn":
            Wnp = prep_l0ffn_weights(inputs)
        else:
            Wnp = SUBS[name]["prep"](inputs)
        in_maps = []
        for b in range(B):
            m = dict(c.np_map)
            m.update(Wnp)
            m["Xio"] = Xs[b]
            in_maps.append(m)
        res = run_bass_kernel_spmd(nc, in_maps, core_ids=list(range(B)))
        Xs = [np.ascontiguousarray(r["Xout"]) for r in res.results]
    out = np.stack([Xs[b].T for b in range(B)], axis=0)
    return out.astype(np.float32)


EMITS = {"l0aa": emit_l0aa, "l0ffn": emit_l0ffn, "l0ab": emit_l0ab,
         "l1aa": emit_l1aa, "l1ffn": emit_l1ffn, "l1ab": emit_l1ab}

WSPECS = {
    "l0ffn": [("g1", [D]), ("g2", [D]), ("Wq", [D, D]), ("Wkc", [D, 256]),
              ("Wvc", [D, 256]), ("Wo", [D, D]), ("Wr", [D, 8]), ("expert_bias", [8]),
              ("sW1", [D, D]), ("sW2", [D, D]), ("sW3", [D, D]),
              ("eW1", [8, D, D]), ("eW2", [8, D, D]), ("eW3", [8, D, D])],
    "l1ffn": [("w_gate", [D, 8]), ("w1", [8, D, 2048]), ("w3", [8, D, 2048]),
              ("w2", [8, 2048, D])],
}


def build_fused_prog():
    p = Prog()
    c = Consts(p)
    p.alloc_banks()
    X = p.nc.dram_tensor("Xio", [D, T], F32, kind="ExternalInput").ap()
    Xout = p.nc.dram_tensor("Xout", [D, T], F32, kind="ExternalOutput").ap()
    iota = np.arange(8, dtype=np.float32)
    Wall = {}
    for name in set(CHAIN):
        wspec = SUBS[name]["wspec"] if name in SUBS else WSPECS[name]
        W = {}
        for nm, shp in wspec:
            W[nm] = p.nc.dram_tensor(f"W{name}_{nm}", list(shp), F32,
                                     kind="ExternalInput").ap()
        W["c_iota8"] = c.add("c_iota8", np.broadcast_to(iota, (PD, 8)).copy())
        W["c_iota8p9"] = c.add("c_iota8p9", np.broadcast_to(iota + 100.0, (PD, 8)).copy())
        if name in SUBS and SUBS[name]["consts"] is not None:
            SUBS[name]["consts"](c, W)
        Wall[name] = W
    g = load_globals(p, c)
    Xs = p.dram("Xs", [D, T], F32)
    cp = []
    for dc in range(DC):
        cp.append(p.dma("sp", Xs[dc * PD:(dc + 1) * PD, :], X[dc * PD:(dc + 1) * PD, :], ch="st"))
    deps = cp[-1:]
    for name in CHAIN:
        p.reset_channels()
        fin = EMITS[name](p, g, Xs, Wall[name], deps=deps)
        deps = list(fin)
    out_toks = []
    for dc in range(DC):
        out_toks.append(p.dma("sp", Xout[dc * PD:(dc + 1) * PD, :],
                              Xs[dc * PD:(dc + 1) * PD, :], deps=deps, ch="out"))
    p.emit("sp", lambda E: E.nop(), deps=out_toks[-1:], sig=True)
    nc = p.finalize()
    return p, c, nc


def fused_in_map(c, inputs):
    m = dict(c.np_map)
    for name in set(CHAIN):
        if name == "l1ffn":
            Wnp = prep_l1ffn_weights(inputs)
        elif name == "l0ffn":
            Wnp = prep_l0ffn_weights(inputs)
        else:
            Wnp = SUBS[name]["prep"](inputs)
        for k, v in Wnp.items():
            m[f"W{name}_{k}"] = np.ascontiguousarray(np.asarray(v, dtype=np.float32))
    return m


FUSED = True


def kernel(**inputs):
    x = np.asarray(inputs["L0_attn_a_x"], dtype=np.float32)
    B = x.shape[0]
    if FUSED:
        if "fused" not in _PROG_CACHE:
            _PROG_CACHE["fused"] = build_fused_prog()
        p, c, nc = _PROG_CACHE["fused"]
        base = fused_in_map(c, inputs)
        in_maps = []
        for b in range(B):
            m = dict(base)
            m["Xio"] = np.ascontiguousarray(x[b].T)
            in_maps.append(m)
        res = run_bass_kernel_spmd(nc, in_maps, core_ids=list(range(B)))
        out = np.stack([res.results[b]["Xout"].T for b in range(B)], axis=0)
        return out.astype(np.float32)
    Xs = [np.ascontiguousarray(x[b].T) for b in range(B)]
    for name in CHAIN:
        p, c, nc = _get_prog(name)
        if name == "l1ffn":
            Wnp = prep_l1ffn_weights(inputs)
        elif name == "l0ffn":
            Wnp = prep_l0ffn_weights(inputs)
        else:
            Wnp = SUBS[name]["prep"](inputs)
        in_maps = []
        for b in range(B):
            m = dict(c.np_map)
            m.update(Wnp)
            m["Xio"] = Xs[b]
            in_maps.append(m)
        res = run_bass_kernel_spmd(nc, in_maps, core_ids=list(range(B)))
        Xs = [np.ascontiguousarray(r["Xout"]) for r in res.results]
    out = np.stack([Xs[b].T for b in range(B)], axis=0)
    return out.astype(np.float32)


# version 07
# speedup vs baseline: 1.0627x; 1.0627x over previous
import numpy as np
from contextlib import ExitStack

import concourse.bass as bass
import concourse.mybir as mybir
from concourse.bass_utils import run_bass_kernel_spmd

DEDUP_WAITS = True

F32 = mybir.dt.float32
BF16 = mybir.dt.bfloat16
AF = mybir.ActivationFunctionType
ALU = mybir.AluOpType
AX = mybir.AxisListType

T = 4096
D = 1024
PD = 128
DC = D // PD
NT5 = T // 512
NT2 = T // 256
LN_EPS = 1e-5
RMS_EPS = 1e-6

ENG2BLOCK = dict(pe="tensor", act="scalar", dve="vector", pool="gpsimd", sp="sync")


class Prog:
    def __init__(self):
        self.nc = bass.Bass("TRN2", target_bir_lowering=False)
        self.es = ExitStack()
        self.scopes = [self.es]
        self.q = {e: [] for e in ENG2BLOCK}
        self.cnt = {}
        self.sem = {}
        self.lastwait = {}
        self._names = {}
        self.banks = None
        self.channels = set()
        self.chmap = {}
        self.epoch = 0

    def _nm(self, base):
        i = self._names.get(base, 0)
        self._names[base] = i + 1
        return f"{base}_{i}" if i else base

    def sb(self, base, shape, dt=F32):
        return self.scopes[-1].enter_context(self.nc.sbuf_tensor(self._nm(base), shape, dt))

    def dram(self, base, shape, dt=F32, kind="Internal"):
        return self.nc.dram_tensor(self._nm(base), shape, dt, kind=kind).ap()

    def scope(self, barrier=True):
        prog = self

        class _S:
            def __enter__(sf):
                sf.stk = ExitStack()
                prog.scopes.append(sf.stk)
                return sf.stk

            def __exit__(sf, *a):
                if barrier:
                    prog.barrier()
                prog.scopes.pop()
                return sf.stk.__exit__(*a)

        return _S()

    def reset_channels(self):
        self.chmap = {}
        self.epoch += 1

    def alloc_banks(self):
        self.banks = [self.es.enter_context(self.nc.psum_tensor(f"bank{i}", [PD, 512], F32))
                      for i in range(8)]

    OOO = ("dve", "act", "pool")

    def emit(self, eng, fn, deps=(), sig=False, dma=False, ch=None, inc=None):
        deps = list(deps)
        if not dma and inc is None:
            if eng in self.OOO:
                sig = True
            if sig:
                prev = self.cnt.get(eng, 0)
                if prev:
                    deps.append((eng, prev))
        waits = {}
        for d in deps:
            if d is None:
                continue
            s, v = d
            if v <= 0:
                continue
            assert v <= self.cnt.get(s, 0), f"dep ({s},{v}) not yet emitted (have {self.cnt.get(s, 0)})"
            if (s == eng and eng not in self.OOO
                    and not ((sig or inc is not None) and v == self.cnt.get(eng, 0))):
                continue
            if waits.get(s, 0) < v:
                waits[s] = v
        if DEDUP_WAITS:
            ww = {}
            for s2, v2 in waits.items():
                k = (eng, s2)
                if self.lastwait.get(k, 0) >= v2:
                    continue
                self.lastwait[k] = v2
                ww[s2] = v2
            waits = ww
        waits = list(waits.items())
        inc_sem, inc_amt, tok = None, 0, None
        if dma:
            chn = ch or (eng + "d")
            if chn not in ("glob", "bar"):
                if eng == "pool":
                    chn = "pw%d_%s" % (self.epoch, chn)
                else:
                    chn = self.chmap.setdefault(chn, "ch%d" % len(self.chmap))
            inc_sem, inc_amt = chn, 16
            self.channels.add(inc_sem)
        elif inc is not None:
            inc_sem, inc_amt = inc
            self.channels.add(inc_sem)
        elif sig:
            inc_sem, inc_amt = eng, 1
        if inc_sem is not None:
            self.cnt[inc_sem] = self.cnt.get(inc_sem, 0) + inc_amt
            tok = (inc_sem, self.cnt[inc_sem])
        self.q[eng].append((waits, fn, inc_sem, inc_amt))
        return tok

    def dma(self, eng, out, in_, deps=(), ch=None, slow=False):
        if slow:
            return self.emit(eng, lambda E: E.dma_start(out=out, in_=in_,
                                                        allow_slow_non_contiguous=True),
                             deps, dma=True, ch=ch)
        return self.emit(eng, lambda E: E.dma_start(out=out, in_=in_), deps, dma=True, ch=ch)

    def nop(self, eng, deps=()):
        return self.emit(eng, lambda E: E.nop(), deps, sig=True)

    def pe_tok(self):
        return self.nop("pe")

    def barrier(self, engs=None):
        engs = list(engs or ENG2BLOCK)
        deps = []
        for chn in sorted(self.channels):
            if chn != "bar" and self.cnt.get(chn, 0):
                deps.append((chn, self.cnt[chn]))
        for e in engs:
            self.emit(e, lambda E: E.nop(), deps=[(e, self.cnt.get(e, 0))], inc=("bar", 1))
        val = self.cnt["bar"]
        toks = []
        for e in engs:
            toks.append(self.nop(e, deps=deps + [("bar", val)]))
        return toks

    def finalize(self):
        nc = self.nc
        names = list(ENG2BLOCK) + sorted(self.channels)
        for s in names:
            self.sem[s] = self.es.enter_context(nc.semaphore(s))
        with nc.Block() as block:
            for eng, bname in ENG2BLOCK.items():
                deco = getattr(block, bname)
                ops = self.q[eng]
                sems = self.sem

                def body(E, ops=ops):
                    for waits, fn, inc_sem, inc_amt in ops:
                        for (s, v) in waits:
                            E.wait_ge(sems[s], v)
                        inst = fn(E)
                        if inc_sem is not None:
                            inst.then_inc(sems[inc_sem], inc_amt)

                deco(body)
        return nc


class Consts:
    def __init__(self, p):
        self.p = p
        self.np_map = {}
        self._aps = {}

    def add(self, name, arr):
        import ml_dtypes
        arr = np.ascontiguousarray(arr)
        if name in self.np_map:
            return self._aps[name]
        if arr.dtype == np.float32:
            dt = F32
        elif arr.dtype == ml_dtypes.bfloat16:
            dt = BF16
        else:
            raise ValueError(arr.dtype)
        ap = self.p.nc.dram_tensor(name, list(arr.shape), dt, kind="ExternalInput").ap()
        self.np_map[name] = arr
        self._aps[name] = ap
        return ap


def load_globals(p, c):
    g = {}
    aps = {
        "ones_f": c.add("c_ones_col", np.ones((PD, 1), np.float32)),
        "ones_row": c.add("c_ones_row", np.ones((1, PD), np.float32)),
        "ident": c.add("c_ident", np.eye(PD, dtype=np.float32)),
    }
    toks = []
    for k, ap in aps.items():
        g[k] = p.sb("g_" + k, list(ap.shape), F32)
        toks.append(p.dma("sp", g[k][:], ap, ch="glob"))
    g["ready"] = [toks[-1]]
    return g



def emit_ln_stats(p, g, X, stats_sb, deps=()):
    bank = p.banks[7]
    Xr = X.rearrange("(c k) t -> k c t", k=PD)
    with p.scope():
        xt = [p.sb("lnst_x", [PD, DC, 512], F32) for _ in range(2)]
        sq = [p.sb("lnst_sq", [PD, DC, 512], F32) for _ in range(2)]
        war = [None, None]
        evac_tok = None
        done = None
        for tt in range(NT5):
            b = tt % 2
            tl = p.dma("sp", xt[b][:], Xr[:, :, tt * 512:(tt + 1) * 512],
                       deps=list(deps) + [war[b]], ch=f"lnx{b}")
            sqt = []
            for dc in range(DC):
                sqt.append(p.emit("pool", lambda E, a=xt[b], o=sq[b], dc=dc:
                                  E.tensor_mul(o[:, dc], a[:, dc], a[:, dc]),
                                  deps=[tl, war[b]], sig=True))
            pe_last = None
            for sub in range(4):
                for cc in range(2):
                    src = xt[b] if cc == 0 else sq[b]
                    for dc in range(DC):
                        pe_last = p.emit("pe", lambda E, a=src, dc=dc, sub=sub, tt=tt, cc=cc,
                                         st=(dc == 0):
                                         E.matmul(bank[:, tt * 8 + sub * 2 + cc: tt * 8 + sub * 2 + cc + 1],
                                                  a[:, dc, sub * 128:(sub + 1) * 128], g["ones_f"][:],
                                                  start=st, stop=(dc == DC - 1)),
                                         deps=[tl, sqt[dc], evac_tok] + g["ready"],
                                         sig=(sub == 3 and cc == 1 and dc == DC - 1))
            war[b] = pe_last
            done = p.emit("act", lambda E, tt=tt: E.copy(stats_sb[:, tt * 8:(tt + 1) * 8],
                                                         bank[:, tt * 8:(tt + 1) * 8]),
                          deps=[pe_last], sig=True)
            evac_tok = done
        done = p.nop("act", deps=[done])
    return done


def emit_ln_smalls(p, g, stats_sb, srow_d, brow_d, dep, kind="ln", eps=LN_EPS):
    with p.scope():
        mu = p.sb("ln_mu", [PD, 32], F32)
        var = p.sb("ln_var", [PD, 32], F32)
        scol = p.sb("ln_scol", [PD, 32], F32)
        bcol = p.sb("ln_bcol", [PD, 32], F32)
        srow_sb = p.sb("ln_srow", [32, PD], F32)
        brow_sb = p.sb("ln_brow", [32, PD], F32)
        sums = stats_sb[:].rearrange("p (t c) -> p t c", c=2)
        if kind == "ln":
            p.emit("dve", lambda E: E.tensor_scalar(mu[:], sums[:, :, 0], 1.0 / D, None, ALU.mult),
                   deps=[dep])
            p.emit("dve", lambda E: E.tensor_scalar(var[:], sums[:, :, 1], 1.0 / D, None, ALU.mult))
            p.emit("dve", lambda E: E.tensor_tensor(scol[:], mu[:], mu[:], ALU.mult))
            t = p.emit("dve", lambda E: E.tensor_tensor(var[:], var[:], scol[:], ALU.subtract),
                       sig=True)
        else:
            t = p.emit("dve", lambda E: E.tensor_scalar(var[:], sums[:, :, 1], 1.0 / D, None,
                                                        ALU.mult), deps=[dep], sig=True)
        t1b = p.emit("dve", lambda E: E.tensor_scalar(var[:], var[:], float(eps), None, ALU.add),
                     deps=[t], sig=True)
        p.emit("act", lambda E: E.activation(var[:], var[:], AF.Ln), deps=[t1b])
        t2 = p.emit("act", lambda E: E.activation(scol[:], var[:], AF.Exp, scale=-0.5), sig=True)
        tr1 = p.emit("pe", lambda E: E.transpose(p.banks[6][0:32, 0:128], scol[:, 0:32],
                                                 g["ident"][:]), deps=[t2] + g["ready"], sig=True)
        te1 = p.emit("act", lambda E: E.copy(srow_sb[:], p.banks[6][0:32, 0:128]),
                     deps=[tr1], sig=True)
        s_tok = p.dma("sp", srow_d, srow_sb[:], deps=[te1], ch="rows")
        b_tok = None
        if kind == "ln":
            tb = p.emit("dve", lambda E: E.tensor_tensor(bcol[:], mu[:], scol[:], ALU.mult),
                        deps=[t2], sig=True)
            tb2 = p.emit("dve", lambda E: E.tensor_scalar(bcol[:], bcol[:], -1.0, None, ALU.mult),
                         deps=[tb], sig=True)
            tr2 = p.emit("pe", lambda E: E.transpose(p.banks[5][0:32, 0:128], bcol[:, 0:32],
                                                     g["ident"][:]), deps=[tb2], sig=True)
            te2 = p.emit("act", lambda E: E.copy(brow_sb[:], p.banks[5][0:32, 0:128]),
                         deps=[tr2], sig=True)
            b_tok = p.dma("sp", brow_d, brow_sb[:], deps=[te2], ch="rows")
    return s_tok, b_tok


def row_bcast_ap(dram_row, t0, n):
    flat = dram_row.rearrange("r c -> (r c)")
    return flat[t0:t0 + n].rearrange("(o n) -> o n", o=1).broadcast_to([PD, n])


class OuterLN:

    def __init__(self, p, g, X, deps=()):
        self.p, self.g, self.X = p, g, X
        self.stats = p.sb("ostats", [PD, 64], F32)
        self.srow_d = p.dram("osrow_d", [32, PD], F32)
        self.brow_d = p.dram("obrow_d", [32, PD], F32)
        t = emit_ln_stats(p, g, X, self.stats, deps=deps)
        s_tok, b_tok = emit_ln_smalls(p, g, self.stats, self.srow_d, self.brow_d, t, kind="ln")
        self.ready = [s_tok, b_tok]

    def emit_xhat(self, xhat_sb, tok0, tok1, deps=()):
        p, g = self.p, self.g
        Xr = self.X.rearrange("(c k) t -> k c t", k=PD)
        with p.scope():
            xt = [p.sb("xh_x", [PD, DC, 512], F32) for _ in range(2)]
            sB = [p.sb("xh_s", [PD, 512], F32) for _ in range(2)]
            bB = [p.sb("xh_b", [PD, 512], F32) for _ in range(2)]
            tmp = [p.sb("xh_t", [PD, 512], F32) for _ in range(2)]
            war = [None, None]
            last = None
            i = 0
            for t0 in range(tok0, tok1, 512):
                w = min(512, tok1 - t0)
                b = i % 2
                i += 1
                dl = list(deps) + [war[b]] + self.ready
                tl = p.dma("sp", xt[b][:, :, 0:w], Xr[:, :, t0:t0 + w], deps=dl, ch=f"xh{b}")
                tsb = p.dma("sp", sB[b][:, 0:w], row_bcast_ap(self.srow_d, t0, w),
                            deps=dl, ch=f"xhs{b}")
                tbb = p.dma("sp", bB[b][:, 0:w], row_bcast_ap(self.brow_d, t0, w),
                            deps=dl, ch=f"xhb{b}")
                for dc in range(DC):
                    tp = p.emit("pool", lambda E, a=xt[b], o=tmp[b], sb_=sB[b], w=w, dc=dc:
                                E.tensor_mul(o[:, 0:w], a[:, dc, 0:w], sb_[:, 0:w]),
                                deps=[tl, tsb, war[b]], sig=True)
                    td = p.emit("dve", lambda E, o=tmp[b], bb_=bB[b], w=w, t0=t0, dc=dc:
                                E.tensor_tensor(xhat_sb[:, dc, t0 - tok0:t0 - tok0 + w],
                                                o[:, 0:w], bb_[:, 0:w], ALU.add),
                                deps=[tp, tbb], sig=True)
                    war[b] = td
                    last = td
        return last



def emit_x_update(p, g, X, acc_sb, tok0, tok1, acc_tok, deps=()):
    Xr = X.rearrange("(c k) t -> k c t", k=PD)
    stores = []
    ta = None
    with p.scope():
        xt = [p.sb("xu_x", [PD, DC, 512], F32) for _ in range(2)]
        war = [None, None]
        i = 0
        for t0 in range(tok0, tok1, 512):
            w = min(512, tok1 - t0)
            b = i % 2
            i += 1
            tl = p.dma("sp", xt[b][:, :, 0:w], Xr[:, :, t0:t0 + w],
                       deps=list(deps) + [war[b], acc_tok], ch=f"xu{b}")
            for dc in range(DC):
                ta = p.emit("dve", lambda E, a=xt[b], w=w, dc=dc, o=t0 - tok0:
                            E.tensor_tensor(a[:, dc, 0:w], a[:, dc, 0:w],
                                            acc_sb[:, dc, o:o + w], ALU.add),
                            deps=[tl], sig=True)
            ts = p.dma("sp", Xr[:, :, t0:t0 + w], xt[b][:, :, 0:w], deps=[ta], ch=f"xw{b}")
            war[b] = ts
            stores.append(ts)
    return stores, ta



def emit_l1ffn(p, g, X, W, deps=(), NQ_lim=None, E_lim=None):
    QT = 1024
    NQ = NQ_lim or (T // QT)
    NTC = QT // 256
    HB = 4
    E8 = 8

    ln = OuterLN(p, g, X, deps=deps)

    with p.scope():
        wg = p.sb("wgate", [PD, DC, 8], BF16)
        twg = p.dma("pool", wg[:], W["w_gate"].rearrange("(c k) m -> k c m", k=PD),
                    deps=deps, ch="wg")
        iota8 = p.sb("iota8", [PD, 8], F32)
        iotaP9 = p.sb("iotaP9", [PD, 8], F32)
        ap_i = W["c_iota8"]
        ap_i9 = W["c_iota8p9"]
        ti0 = p.dma("sp", iota8[:], ap_i, ch="l1fi0")
        ti1 = p.dma("sp", iotaP9[:], ap_i9, ch="l1fi1")

        w1b = [p.sb("w1b", [PD, DC, 512], BF16) for _ in range(2)]
        w3b = [p.sb("w3b", [PD, DC, 512], BF16) for _ in range(2)]
        w2e = p.sb("w2e", [PD, 16, 1024], BF16)
        xq = p.sb("xq", [PD, DC, QT], BF16)
        hstage = p.sb("hstage", [PD, 16, QT], BF16)
        acc = p.sb("ffacc", [PD, DC, QT], F32)
        cB = p.sb("cB", [PD, QT], F32)
        ctp = p.sb("ctp", [8, PD], F32)
        crow_d = p.dram("crow_d", [E8, QT], F32)
        NPAR = 4
        ABK = (0, 1, 6, 7)
        sA = [p.sb("sA", [PD, 256], F32) for _ in range(NPAR)]
        sA2 = [p.sb("sA2", [PD, 256], F32) for _ in range(NPAR)]
        lgs = p.sb("lgs", [PD, 8], F32)
        m1 = p.sb("rt_m1", [PD, 1], F32)
        m2 = p.sb("rt_m2", [PD, 1], F32)
        i1 = p.sb("rt_i1", [PD, 1], F32)
        eq1 = p.sb("rt_eq1", [PD, 8], F32)
        eq2 = p.sb("rt_eq2", [PD, 8], F32)
        msk = p.sb("rt_msk", [PD, 8], F32)
        c1 = p.sb("rt_c1", [PD, 1], F32)
        c2 = p.sb("rt_c2", [PD, 1], F32)
        cmat = p.sb("rt_cmat", [PD, 8], F32)

        war_w = [None, None]
        war_w2 = None
        upd_last = None
        for q in range(NQ):
            t_xh = ln.emit_xhat(xq, q * QT, (q + 1) * QT)

            rt_tok = None
            for cidx in range(QT // 128):
                gt0 = cidx * 128
                mml = None
                for kc in range(DC):
                    mml = p.emit("pe", lambda E, kc=kc, gt0=gt0, st=(kc == 0):
                                 E.matmul(p.banks[6][:, 0:8], xq[:, kc, gt0:gt0 + 128],
                                          wg[:, kc, :], start=st, stop=(kc == DC - 1)),
                                 deps=[t_xh, twg, rt_tok], sig=(kc == DC - 1))
                tev = p.emit("act", lambda E: E.copy(lgs[:], p.banks[6][:, 0:8]),
                             deps=[mml], sig=True)
                p.emit("dve", lambda E: E.tensor_reduce(m1[:], lgs[:], AX.X, ALU.max),
                       deps=[tev, ti0, ti1])
                p.emit("dve", lambda E: E.tensor_scalar(eq1[:], lgs[:], m1[:], None, ALU.is_ge))
                p.emit("dve", lambda E: E.scalar_tensor_tensor(msk[:], eq1[:], -100.0, iotaP9[:],
                                                               ALU.mult, ALU.add))
                p.emit("dve", lambda E: E.tensor_reduce(i1[:], msk[:], AX.X, ALU.min))
                p.emit("dve", lambda E: E.tensor_scalar(eq1[:], iota8[:], i1[:], None, ALU.is_equal))
                p.emit("dve", lambda E: E.scalar_tensor_tensor(msk[:], eq1[:], -1e38, lgs[:],
                                                               ALU.mult, ALU.add))
                t5 = p.emit("dve", lambda E: E.tensor_reduce(m2[:], msk[:], AX.X, ALU.max), sig=True)
                p.emit("dve", lambda E: E.tensor_scalar(eq2[:], msk[:], m2[:], None, ALU.is_ge))
                t6 = p.emit("dve", lambda E: E.tensor_tensor(c1[:], m1[:], m2[:], ALU.subtract),
                            sig=True)
                t7 = p.emit("act", lambda E: E.activation(c2[:], c1[:], AF.Tanh, scale=0.5),
                            deps=[t6], sig=True)
                p.emit("dve", lambda E: E.tensor_scalar(c1[:], c2[:], 0.5, 0.5, ALU.mult, ALU.add),
                       deps=[t7])
                p.emit("dve", lambda E: E.tensor_scalar(c2[:], c1[:], -1.0, 1.0, ALU.mult, ALU.add))
                p.emit("dve", lambda E: E.tensor_scalar(cmat[:], eq1[:], c1[:], None, ALU.mult))
                t8 = p.emit("dve", lambda E: E.scalar_tensor_tensor(cmat[:], eq2[:], c2[:], cmat[:],
                                                                    ALU.mult, ALU.add),
                            sig=True)
                ttr = p.emit("pe", lambda E: E.transpose(p.banks[5][0:8, 0:128],
                                                         cmat[:, 0:8], g["ident"][:]),
                             deps=[t8, rt_tok], sig=True)
                tcp = p.emit("act", lambda E: E.copy(ctp[:], p.banks[5][0:8, 0:128]),
                             deps=[ttr], sig=True)
                rt_tok = p.dma("sp", crow_d[:, gt0:gt0 + 128], ctp[:], deps=[tcp, rt_tok], ch="cr")

            acc_tok = None
            first_e = True
            cb_war = None
            par = 0
            prev_h = [rt_tok] * NPAR
            for e in range(E_lim or E8):
                cb_tok = p.dma("sp", cB[:],
                               crow_d[e].rearrange("(o n) -> o n", o=1).broadcast_to([PD, QT]),
                               deps=[rt_tok, cb_war], ch="cb")
                cb_war = None

                h_toks = []
                for hb in range(HB):
                    wpar = (e * HB + hb) % 2
                    tw1 = p.dma("pool", w1b[wpar][:],
                                W["w1"][e, :, hb * 512:(hb + 1) * 512].rearrange(
                                    "(c k) m -> k c m", k=PD), deps=[war_w[wpar]],
                                ch=f"w1{wpar}")
                    tw3 = p.dma("pool", w3b[wpar][:],
                                W["w3"][e, :, hb * 512:(hb + 1) * 512].rearrange(
                                    "(c k) m -> k c m", k=PD), deps=[war_w[wpar]],
                                ch=f"w3{wpar}")
                    tw2 = p.dma("pool", w2e[:, hb * 4:(hb + 1) * 4, :],
                                W["w2"][e, hb * 512:(hb + 1) * 512, :].rearrange(
                                    "(c k) m -> k c m", k=PD), deps=[war_w2],
                                ch="w2l")
                    for hc in range(4):
                        for tc in range(NTC):
                            bk = p.banks[ABK[par]]
                            for kc in range(DC):
                                p.emit("pe", lambda E, kc=kc, hc=hc, tc=tc, bk=bk, wp=wpar,
                                       st=(kc == 0):
                                       E.matmul(bk[:, 0:256], w1b[wp][:, kc, hc * 128:(hc + 1) * 128],
                                                xq[:, kc, tc * 256:(tc + 1) * 256],
                                                start=st, stop=(kc == DC - 1)),
                                       deps=[tw1, t_xh, prev_h[par]])
                            tpe = None
                            for kc in range(DC):
                                tpe = p.emit("pe", lambda E, kc=kc, hc=hc, tc=tc, bk=bk, wp=wpar,
                                             st=(kc == 0):
                                             E.matmul(bk[:, 256:512], w3b[wp][:, kc, hc * 128:(hc + 1) * 128],
                                                      xq[:, kc, tc * 256:(tc + 1) * 256],
                                                      start=st, stop=(kc == DC - 1)),
                                             deps=[tw3], sig=(kc == DC - 1))
                            tsl = p.emit("act", lambda E, bk=bk, par=par: E.activation(
                                sA[par][:], bk[:, 0:256], AF.Silu), deps=[tpe], sig=True)
                            tcm = p.emit("pool", lambda E, par=par, tc=tc: E.tensor_mul(
                                sA2[par][:], sA[par][:], cB[:, tc * 256:(tc + 1) * 256]),
                                deps=[tsl, cb_tok], sig=True)
                            cb_war = tcm
                            th = p.emit("dve", lambda E, bk=bk, par=par, hb=hb, hc=hc, tc=tc:
                                        E.tensor_tensor(hstage[:, hb * 4 + hc, tc * 256:(tc + 1) * 256],
                                                        sA2[par][:], bk[:, 256:512], ALU.mult),
                                        deps=[tcm], sig=True)
                            prev_h[par] = th
                            h_toks.append(th)
                            par = (par + 1) % NPAR
                    war_w[wpar] = th
                h_all = h_toks[-1]

                ev_prev = None
                for tc in range(NTC):
                    mm_last = None
                    for dc2 in range(DC):
                        bk = p.banks[2 + dc2 // 2]
                        half = (dc2 % 2) * 256
                        for hh in range(16):
                            mm_last = p.emit("pe", lambda E, hh=hh, dc2=dc2, tc=tc, bk=bk,
                                             half=half, st=(hh == 0):
                                             E.matmul(bk[:, half:half + 256],
                                                      w2e[:, hh, dc2 * 128:(dc2 + 1) * 128],
                                                      hstage[:, hh, tc * 256:(tc + 1) * 256],
                                                      start=st, stop=(hh == 15)),
                                             deps=[h_all, tw2, ev_prev],
                                             sig=(hh == 15 and dc2 == DC - 1))
                    ev_toks = []
                    for dc2 in range(DC):
                        bk = p.banks[2 + dc2 // 2]
                        half = (dc2 % 2) * 256
                        if first_e:
                            tev = p.emit("dve", lambda E, bk=bk, half=half, dc2=dc2, tc=tc:
                                         E.tensor_copy(acc[:, dc2, tc * 256:(tc + 1) * 256],
                                                       bk[:, half:half + 256]),
                                         deps=[mm_last], sig=True)
                        else:
                            tev = p.emit("dve", lambda E, bk=bk, half=half, dc2=dc2, tc=tc:
                                         E.tensor_tensor(acc[:, dc2, tc * 256:(tc + 1) * 256],
                                                         acc[:, dc2, tc * 256:(tc + 1) * 256],
                                                         bk[:, half:half + 256], ALU.add),
                                         deps=[mm_last], sig=True)
                        ev_toks.append(tev)
                    ev_prev = ev_toks[-1]
                acc_tok = ev_toks[-1]
                war_w2 = acc_tok
                first_e = False

            stores, upd_last = emit_x_update(p, g, X, acc, q * QT, (q + 1) * QT, acc_tok)
            ln.ready = ln.ready + [h_all, rt_tok, upd_last]
    return stores



def prep_weights_l1ffn(inputs):
    return {
        "w_gate": np.ascontiguousarray(inputs["L1_ffn_w_gate"]),
        "w1": np.ascontiguousarray(inputs["L1_ffn_w1"]),
        "w3": np.ascontiguousarray(inputs["L1_ffn_w3"]),
        "w2": np.ascontiguousarray(inputs["L1_ffn_w2"]),
    }


def build_l1ffn_prog(NQ_lim=None, E_lim=None):
    p = Prog()
    c = Consts(p)
    p.alloc_banks()
    X = p.nc.dram_tensor("Xio", [D, T], F32, kind="ExternalInput").ap()
    Xout = p.nc.dram_tensor("Xout", [D, T], F32, kind="ExternalOutput").ap()
    W = {
        "w_gate": p.nc.dram_tensor("w_gate", [D, 8], F32, kind="ExternalInput").ap(),
        "w1": p.nc.dram_tensor("w1", [8, D, 2048], F32, kind="ExternalInput").ap(),
        "w3": p.nc.dram_tensor("w3", [8, D, 2048], F32, kind="ExternalInput").ap(),
        "w2": p.nc.dram_tensor("w2", [8, 2048, D], F32, kind="ExternalInput").ap(),
    }
    iota = np.arange(8, dtype=np.float32)
    W["c_iota8"] = c.add("c_iota8", np.broadcast_to(iota, (PD, 8)).copy())
    W["c_iota8p9"] = c.add("c_iota8p9", np.broadcast_to(iota + 100.0, (PD, 8)).copy())
    g = load_globals(p, c)
    Xs = p.dram("Xs", [D, T], F32)
    cp = []
    for dc in range(DC):
        cp.append(p.dma("sp", Xs[dc * PD:(dc + 1) * PD, :], X[dc * PD:(dc + 1) * PD, :],
                        ch="st"))
    cp = cp[-1:]
    stores = emit_l1ffn(p, g, Xs, W, deps=cp, NQ_lim=NQ_lim, E_lim=E_lim)
    out_toks = []
    for dc in range(DC):
        out_toks.append(p.dma("sp", Xout[dc * PD:(dc + 1) * PD, :], Xs[dc * PD:(dc + 1) * PD, :],
                              deps=stores[-2:], ch="out"))
    p.emit("sp", lambda E: E.nop(), deps=out_toks[-1:], sig=True)
    nc = p.finalize()
    return p, c, nc


def build_test_ln_prog():
    p = Prog()
    c = Consts(p)
    p.alloc_banks()
    X = p.nc.dram_tensor("Xio", [D, T], F32, kind="ExternalInput").ap()
    XH = p.nc.dram_tensor("XH", [D, 1024], F32, kind="ExternalOutput").ap()
    g = load_globals(p, c)
    ln = OuterLN(p, g, X)
    xq = p.sb("xq_t", [PD, DC, 1024], BF16)
    t = ln.emit_xhat(xq, 0, 1024)
    outs = []
    xf = p.sb("xf_t", [PD, 1024], F32)
    prev = None
    for dc in range(DC):
        tc_ = p.emit("dve", lambda E, dc=dc: E.tensor_copy(xf[:], xq[:, dc, :]),
                     deps=[t, prev], sig=True)
        prev = p.dma("sp", XH[dc * PD:(dc + 1) * PD, :], xf[:], deps=[tc_])
        outs.append(prev)
    p.emit("sp", lambda E: E.nop(), deps=outs, sig=True)
    nc = p.finalize()
    return p, c, nc



def load_bias_chunks(p, ap, name, deps=(), ch=None):
    sb = p.sb(name, [PD, DC], F32)
    tok = p.dma("sp", sb[:], ap.rearrange("(c k) -> k c", k=PD), deps=deps,
                ch=ch or ("ldb_" + name), slow=True)
    return sb, tok


def emit_proj_fm(p, g, name, W_ap, Kdim, Mcols, rhs_fn, evac, deps=(), tt=512,
                 banks=(0, 1), jc_block=8, extra_mm_deps=()):
    KC = Kdim // PD
    NB = Mcols // (jc_block * PD)
    assert Mcols % PD == 0
    wsb = [p.sb(f"{name}_w", [PD, KC, jc_block * PD], BF16) for _ in range(2)]
    war_w = [None, None]
    bank_war = {b: None for b in banks}
    par = 0
    last = None
    for jb in range(NB):
        wp = jb % 2
        tw = p.dma("pool", wsb[wp][:], W_ap[:, jb * jc_block * PD:(jb + 1) * jc_block * PD]
                   .rearrange("(c k) m -> k c m", k=PD), deps=list(deps) + [war_w[wp]],
                   ch=f"{name}w{wp}")
        for jcl in range(jc_block):
            j0 = (jb * jc_block + jcl) * PD
            for t0 in range(0, T, tt):
                bk = p.banks[banks[par % len(banks)]]
                par += 1
                tok = None
                for kc in range(KC):
                    tok = p.emit("pe", lambda E, kc=kc, jcl=jcl, t0=t0, bk=bk, wp=wp,
                                 st=(kc == 0):
                                 E.matmul(bk[:, 0:tt], wsb[wp][:, kc, jcl * PD:(jcl + 1) * PD],
                                          rhs_fn(kc, t0, tt), start=st, stop=(kc == KC - 1)),
                                 deps=[tw, bank_war[banks[(par - 1) % len(banks)]]]
                                 + list(extra_mm_deps), sig=(kc == KC - 1))
                ev = evac(j0, t0, tt, bk, tok)
                bank_war[banks[(par - 1) % len(banks)]] = ev
                war_w[wp] = ev
                last = ev
    return last



def emit_l0ffn(p, g, X, W, deps=()):
    QT = 1024
    NQ = T // QT
    NTC = QT // 256
    E8 = 8
    SCALE = 0.0625
    NTK = T // PD

    ln = OuterLN(p, g, X, deps=deps)
    Yh = p.dram("l0f_Y", [D, T], F32)
    Yr = Yh.rearrange("(c k) t -> k c t", k=PD)
    Xr = X.rearrange("(c k) t -> k c t", k=PD)
    bank7 = p.banks[7]

    with p.scope():
        g1sb, tg1 = load_bias_chunks(p, W["g1"], "l0f_g1")
        g2sb, tg2 = load_bias_chunks(p, W["g2"], "l0f_g2")
        stats1 = p.sb("rms1_stats", [PD, 64], F32)

        ph_helpers = {}
        with p.scope():
            xt = [p.sb("l0f_x", [PD, DC, 512], F32) for _ in range(2)]
            sB = [p.sb("l0f_s", [PD, 512], F32) for _ in range(2)]
            bB = [p.sb("l0f_b", [PD, 512], F32) for _ in range(2)]
            yt = [p.sb("l0f_yt", [PD, DC, 512], F32) for _ in range(2)]
            sq = [p.sb("l0f_sq", [PD, DC, 512], F32) for _ in range(2)]
            war = [None, None]
            ystore = [None, None]
            evac_tok = None
            st1 = None
            for tt in range(NT5):
                b = tt % 2
                dl = list(deps) + [war[b]] + ln.ready
                tl = p.dma("sp", xt[b][:], Xr[:, :, tt * 512:(tt + 1) * 512], deps=dl, ch=f"l0fx{b}")
                tsb = p.dma("sp", sB[b][:], row_bcast_ap(ln.srow_d, tt * 512, 512), deps=dl, ch=f"l0fs{b}")
                tbb = p.dma("sp", bB[b][:], row_bcast_ap(ln.brow_d, tt * 512, 512), deps=dl, ch=f"l0fb{b}")
                sq_toks = []
                ylast = None
                for dc in range(DC):
                    tp = p.emit("pool", lambda E, a=xt[b], o=yt[b], s_=sB[b], dc=dc:
                                E.tensor_mul(o[:, dc], a[:, dc], s_[:]),
                                deps=[tl, tsb, war[b], ystore[b]], sig=True)
                    ylast = p.emit("dve", lambda E, o=yt[b], b_=bB[b], dc=dc:
                                   E.tensor_tensor(o[:, dc], o[:, dc], b_[:], ALU.add),
                                   deps=[tp, tbb], sig=True)
                    sq_toks.append(p.emit("pool", lambda E, a=yt[b], o=sq[b], dc=dc:
                                          E.tensor_mul(o[:, dc], a[:, dc], a[:, dc]),
                                          deps=[ylast], sig=True))
                ystore[b] = p.dma("sp", Yr[:, :, tt * 512:(tt + 1) * 512], yt[b][:],
                                  deps=[ylast], ch=f"l0fy{b}")
                pe_last = None
                for sub in range(4):
                    for dc in range(DC):
                        pe_last = p.emit("pe", lambda E, a=sq[b], dc=dc, sub=sub, tt=tt,
                                         st=(dc == 0):
                                         E.matmul(bank7[:, tt * 8 + sub * 2 + 1: tt * 8 + sub * 2 + 2],
                                                  a[:, dc, sub * 128:(sub + 1) * 128],
                                                  g["ones_f"][:], start=st, stop=(dc == DC - 1)),
                                         deps=[sq_toks[dc], evac_tok] + g["ready"],
                                         sig=(sub == 3 and dc == DC - 1))
                war[b] = pe_last
                st1 = p.emit("act", lambda E, tt=tt: E.copy(stats1[:, tt * 8:(tt + 1) * 8],
                                                            bank7[:, tt * 8:(tt + 1) * 8]),
                             deps=[pe_last], sig=True)
                evac_tok = st1
        ydone = [("l0fy0", p.cnt.get("l0fy0", 0)), ("l0fy1", p.cnt.get("l0fy1", 0))]
        r1row_d = p.dram("r1row_d", [32, PD], F32)
        s1_tok, _ = emit_ln_smalls(p, g, stats1, r1row_d, None, st1, kind="rms", eps=RMS_EPS)

        stats2 = p.sb("rms2_stats", [PD, 64], F32)
        bc_scope = p.scope()
        bc_scope.__enter__()
        attnO = p.sb("l0f_attnO", [PD, DC, T], BF16)
        nx_scope = p.scope()
        nx_scope.__enter__()
        nx1 = p.sb("l0f_nx1", [PD, DC, T], BF16)
        nx1_tok = None
        with p.scope():
            ytl = [p.sb("l0fb_y", [PD, DC, 512], F32) for _ in range(2)]
            sB = [p.sb("l0fb_s", [PD, 512], F32) for _ in range(2)]
            war = [None, None]
            for tt in range(NT5):
                b = tt % 2
                dl = [s1_tok, war[b]] + ydone
                tl = p.dma("sp", ytl[b][:], Yr[:, :, tt * 512:(tt + 1) * 512], deps=dl, ch=f"l0fby{b}")
                tsb = p.dma("sp", sB[b][:], row_bcast_ap(r1row_d, tt * 512, 512), deps=dl, ch=f"l0fbs{b}")
                for dc in range(DC):
                    nx1_tok = p.emit("dve", lambda E, a=ytl[b], s_=sB[b], dc=dc, tt=tt:
                                     E.scalar_tensor_tensor(nx1[:, dc, tt * 512:(tt + 1) * 512],
                                                            a[:, dc], g1sb[:, dc:dc + 1], s_[:],
                                                            ALU.mult, ALU.mult),
                                     deps=[tl, tsb, tg1, war[b]], sig=True)
                war[b] = nx1_tok

        attn_done = None
        with p.scope():
            kc_fm = p.sb("l0f_kc", [PD, 2, T], BF16)
            vc_tm = p.sb("l0f_vc", [PD, NTK, 264], BF16)
            shift_sb = p.sb("l0f_shift", [PD, 1], F32)
            wkc = p.sb("l0f_wkc", [PD, DC, 256], BF16)
            wvc = p.sb("l0f_wvc", [PD, DC, 256], BF16)
            wq = p.sb("l0f_wq", [PD, DC, 256], BF16)
            q_fm = p.sb("l0f_q", [PD, 2, T], BF16)
            qq = p.sb("l0f_qq", [PD, 2, 512], F32)
            qn = p.sb("l0f_qn", [1, 16], F32)
            nrm = p.sb("l0f_nrm", [1, 16], F32)
            rsum = p.sb("l0f_rs", [1, 512], F32)
            rB = p.sb("l0f_rB", [PD, 512], F32)
            Ptile = [p.sb("l0f_P", [PD, 512], BF16) for _ in range(2)]
            sh_d = p.dram("l0f_sh_d", [1, 1], F32)

            twk = p.dma("pool", wkc[:], W["Wkc"].rearrange("(c k) m -> k c m", k=PD), ch="l0fwk")
            twv = p.dma("pool", wvc[:], W["Wvc"].rearrange("(c k) m -> k c m", k=PD), ch="l0fwv")
            kc_toks = []
            for jc in range(2):
                for t5 in range(NT5):
                    bkp = p.banks[(jc * NT5 + t5) % 2]
                    tok = None
                    for kc in range(DC):
                        tok = p.emit("pe", lambda E, kc=kc, jc=jc, t5=t5, bkp=bkp, st=(kc == 0):
                                     E.matmul(bkp[:, 0:512],
                                              wkc[:, kc, jc * PD:(jc + 1) * PD],
                                              nx1[:, kc, t5 * 512:(t5 + 1) * 512],
                                              start=st, stop=(kc == DC - 1)),
                                     deps=[twk, nx1_tok,
                                           kc_toks[-2] if len(kc_toks) >= 2 else None],
                                     sig=(kc == DC - 1))
                    kc_toks.append(p.emit("act", lambda E, jc=jc, t5=t5, bkp=bkp: E.copy(
                        kc_fm[:, jc, t5 * 512:(t5 + 1) * 512], bkp[:, 0:512]),
                        deps=[tok], sig=True))
            kc_done = kc_toks[-1]
            ones_tok = p.emit("dve", lambda E: E.memset(vc_tm[:, :, 256:264], 0.0), sig=True)
            ones_tok = p.emit("dve", lambda E: E.memset(vc_tm[:, :, 256:257], 1.0),
                              deps=[ones_tok], sig=True)
            vc_toks = []
            for tc in range(NTK):
                tok = None
                for kc in range(DC):
                    tok = p.emit("pe", lambda E, kc=kc, tc=tc, st=(kc == 0):
                                 E.matmul(p.banks[2 + tc % 2][:, 0:256],
                                          nx1[:, kc, tc * PD:(tc + 1) * PD],
                                          wvc[:, kc, :], start=st, stop=(kc == DC - 1)),
                                 deps=[twv, nx1_tok,
                                       vc_toks[-2] if len(vc_toks) >= 2 else None],
                                 sig=(kc == DC - 1))
                vc_toks.append(p.emit("act", lambda E, tc=tc: E.copy(
                    vc_tm[:, tc, 0:256], p.banks[2 + tc % 2][:, 0:256]),
                    deps=[tok], sig=True))
            vc_done = vc_toks[-1]
            nrm_toks = []
            for t5 in range(NT5):
                sq_t = []
                for jc in range(2):
                    sq_t.append(p.emit("pool", lambda E, jc=jc, t5=t5: E.tensor_mul(
                        qq[:, jc, :], kc_fm[:, jc, t5 * 512:(t5 + 1) * 512],
                        kc_fm[:, jc, t5 * 512:(t5 + 1) * 512]),
                        deps=[kc_done, nrm_toks[-1] if nrm_toks else None], sig=True))
                mt = None
                for jc in range(2):
                    mt = p.emit("pe", lambda E, jc=jc, st=(jc == 0):
                                E.matmul(p.banks[4][0:1, 0:512], g["ones_f"][:], qq[:, jc, :],
                                         start=st, stop=(jc == 1)),
                                deps=[sq_t[jc], nrm_toks[-1] if nrm_toks else None],
                                sig=(jc == 1))
                nrm_toks.append(p.emit("dve", lambda E, t5=t5: E.tensor_reduce(
                    nrm[0:1, t5:t5 + 1], p.banks[4][0:1, 0:512], AX.X, ALU.max),
                    deps=[mt], sig=True))
            kn_tok = p.emit("dve", lambda E: E.tensor_reduce(nrm[0:1, 8:9], nrm[0:1, 0:8],
                                                             AX.X, ALU.max),
                            deps=[nrm_toks[-1]], sig=True)
            for h in range(4):
                twq = p.dma("pool", wq[:], W["Wq"][:, h * 256:(h + 1) * 256]
                            .rearrange("(c k) m -> k c m", k=PD), deps=[attn_done], ch="l0fwq")
                q_toks = []
                for jc in range(2):
                    for t5 in range(NT5):
                        bkp = p.banks[(jc * NT5 + t5) % 2]
                        tok = None
                        for kc in range(DC):
                            tok = p.emit("pe", lambda E, kc=kc, jc=jc, t5=t5, bkp=bkp, st=(kc == 0):
                                         E.matmul(bkp[:, 0:512],
                                                  wq[:, kc, jc * PD:(jc + 1) * PD],
                                                  nx1[:, kc, t5 * 512:(t5 + 1) * 512],
                                                  start=st, stop=(kc == DC - 1)),
                                         deps=[twq, nx1_tok, attn_done,
                                               q_toks[-2] if len(q_toks) >= 2 else None],
                                         sig=(kc == DC - 1))
                        q_toks.append(p.emit("act", lambda E, jc=jc, t5=t5, bkp=bkp: E.copy(
                            q_fm[:, jc, t5 * 512:(t5 + 1) * 512], bkp[:, 0:512]),
                            deps=[tok], sig=True))
                q_done = q_toks[-1]
                qn_toks = []
                for t5 in range(NT5):
                    sq_t = []
                    for jc in range(2):
                        sq_t.append(p.emit("pool", lambda E, jc=jc, t5=t5: E.tensor_mul(
                            qq[:, jc, :], q_fm[:, jc, t5 * 512:(t5 + 1) * 512],
                            q_fm[:, jc, t5 * 512:(t5 + 1) * 512]),
                            deps=[q_done, kn_tok,
                                  qn_toks[-1] if qn_toks else None], sig=True))
                    mt = None
                    for jc in range(2):
                        mt = p.emit("pe", lambda E, jc=jc, st=(jc == 0):
                                    E.matmul(p.banks[4][0:1, 0:512], g["ones_f"][:], qq[:, jc, :],
                                             start=st, stop=(jc == 1)),
                                    deps=[sq_t[jc], kn_tok,
                                          qn_toks[-1] if qn_toks else None],
                                    sig=(jc == 1))
                    qn_toks.append(p.emit("dve", lambda E, t5=t5: E.tensor_reduce(
                        qn[0:1, t5:t5 + 1], p.banks[4][0:1, 0:512], AX.X, ALU.max),
                        deps=[mt], sig=True))
                t1 = p.emit("dve", lambda E: E.tensor_reduce(qn[0:1, 8:9], qn[0:1, 0:8],
                                                             AX.X, ALU.max),
                            deps=[qn_toks[-1], kn_tok], sig=True)
                t2 = p.emit("dve", lambda E: E.tensor_tensor(qn[0:1, 9:10], qn[0:1, 8:9],
                                                             nrm[0:1, 8:9], ALU.mult),
                            deps=[t2 if False else t1], sig=True)
                t3 = p.emit("act", lambda E: E.activation(qn[0:1, 10:11], qn[0:1, 9:10], AF.Ln),
                            deps=[t2], sig=True)
                t4 = p.emit("act", lambda E: E.activation(qn[0:1, 11:12], qn[0:1, 10:11],
                                                          AF.Exp, scale=0.5), deps=[t3], sig=True)
                t5_ = p.emit("dve", lambda E: E.tensor_scalar(qn[0:1, 12:13], qn[0:1, 11:12],
                                                              -SCALE, None, ALU.mult),
                             deps=[t4], sig=True)
                t6 = p.dma("sp", sh_d, qn[0:1, 12:13], deps=[t5_], ch="l0fsh")
                t7 = p.dma("sp", shift_sb[:], sh_d.broadcast_to([PD, 1]), deps=[t6], ch="l0fsh2")
                prev_tr = None
                for tq in range(NT5):
                    ev_dep = attn_done
                    p_war = [None, None]
                    s_war = [None, None]
                    sum_tok = None
                    for tkc in range(NTK):
                        pr = tkc % 2
                        bkS = p.banks[4 + pr]
                        tok = None
                        for jc in range(2):
                            tok = p.emit("pe", lambda E, jc=jc, tkc=tkc, tq=tq, bkS=bkS,
                                         st=(jc == 0):
                                         E.matmul(bkS[:, 0:512],
                                                  kc_fm[:, jc, tkc * PD:(tkc + 1) * PD],
                                                  q_fm[:, jc, tq * 512:(tq + 1) * 512],
                                                  start=st, stop=(jc == 1)),
                                         deps=[kc_done, q_done, s_war[pr], t5_],
                                         sig=(jc == 1))
                        texp = p.emit("act", lambda E, pr=pr, bkS=bkS: E.activation(
                            Ptile[pr][:], bkS[:], AF.Exp, bias=shift_sb[:], scale=SCALE),
                            deps=[tok, t7, p_war[pr]], sig=True)
                        s_war[pr] = texp
                        for jx in range(2):
                            p.emit("pe", lambda E, jx=jx, tkc=tkc, pr=pr, st=(tkc == 0):
                                   E.matmul(p.banks[jx][:, 0:512],
                                            vc_tm[:, tkc, jx * 128:(jx + 1) * 128],
                                            Ptile[pr][:], start=st, stop=(tkc == NTK - 1)),
                                   deps=[texp, vc_done, ones_tok, ev_dep])
                        sum_tok = p.emit("pe", lambda E, tkc=tkc, pr=pr, st=(tkc == 0):
                                         E.matmul(p.banks[2][0:1, 0:512],
                                                  vc_tm[:, tkc, 256:257], Ptile[pr][:],
                                                  start=st, stop=(tkc == NTK - 1)),
                                         deps=[texp, prev_tr], sig=True)
                        p_war[pr] = sum_tok
                    tr = p.emit("dve", lambda E: E.reciprocal(rsum[:], p.banks[2][0:1, 0:512]),
                                deps=[sum_tok], sig=True)
                    prev_tr = tr
                    tb = p.emit("pe", lambda E: E.matmul(p.banks[3][:, 0:512], g["ones_row"][:],
                                                         rsum[:], start=True, stop=True),
                                deps=[tr], sig=True)
                    tc_ = p.emit("dve", lambda E: E.tensor_copy(rB[:], p.banks[3][:, 0:512]),
                                 deps=[tb], sig=True)
                    for jx in range(2):
                        attn_done = p.emit("dve", lambda E, jx=jx, tq=tq, h=h: E.tensor_tensor(
                            attnO[:, h * 2 + jx, tq * 512:(tq + 1) * 512],
                            p.banks[jx][:, 0:512], rB[:], ALU.mult),
                            deps=[tc_], sig=True)

        nx_scope.__exit__(None, None, None)
        st2 = None
        with p.scope():
            wo = p.sb("l0f_wo", [PD, DC, D], BF16)
            two = p.dma("pool", wo[:], W["Wo"].rearrange("(c k) m -> k c m", k=PD), ch="l0fwo")
            ytl = [p.sb("l0fd_y", [PD, DC, 512], F32) for _ in range(2)]
            sq = [p.sb("l0fd_sq", [PD, DC, 512], F32) for _ in range(2)]
            war = [None, None]
            ystore = [None, None]
            bank_war = [None, None]
            evac_tok = None
            for tt in range(NT5):
                b = tt % 2
                tl = p.dma("sp", ytl[b][:], Yr[:, :, tt * 512:(tt + 1) * 512],
                           deps=[war[b], ystore[b]] + ydone, ch=f"l0fdy{b}")
                sq_toks = []
                add_last = None
                for jc in range(DC):
                    bk = p.banks[jc % 2]
                    tok = None
                    for kc in range(DC):
                        tok = p.emit("pe", lambda E, kc=kc, jc=jc, tt=tt, bk=bk, st=(kc == 0):
                                     E.matmul(bk[:, 0:512], wo[:, kc, jc * PD:(jc + 1) * PD],
                                              attnO[:, kc, tt * 512:(tt + 1) * 512],
                                              start=st, stop=(kc == DC - 1)),
                                     deps=[two, attn_done, war[b], bank_war[jc % 2]],
                                     sig=(kc == DC - 1))
                    ta = p.emit("dve", lambda E, jc=jc, bk=bk, y_=ytl[b]: E.tensor_tensor(
                        y_[:, jc], y_[:, jc], bk[:, 0:512], ALU.add),
                        deps=[tok, tl], sig=True)
                    bank_war[jc % 2] = ta
                    tq = p.emit("pool", lambda E, jc=jc, y_=ytl[b], q_=sq[b]: E.tensor_mul(
                        q_[:, jc], y_[:, jc], y_[:, jc]), deps=[ta], sig=True)
                    sq_toks.append((ta, tq))
                    add_last = ta
                ystore[b] = p.dma("sp", Yr[:, :, tt * 512:(tt + 1) * 512], ytl[b][:],
                                  deps=[add_last], ch=f"l0fdw{b}")
                pe_last = None
                for sub in range(4):
                    for dc in range(DC):
                        pe_last = p.emit("pe", lambda E, a=sq[b], dc=dc, sub=sub, tt=tt,
                                         st=(dc == 0):
                                         E.matmul(bank7[:, tt * 8 + sub * 2 + 1: tt * 8 + sub * 2 + 2],
                                                  a[:, dc, sub * 128:(sub + 1) * 128],
                                                  g["ones_f"][:], start=st, stop=(dc == DC - 1)),
                                         deps=[sq_toks[dc][1], evac_tok] + g["ready"],
                                         sig=(sub == 3 and dc == DC - 1))
                war[b] = pe_last
                st2 = p.emit("act", lambda E, tt=tt: E.copy(stats2[:, tt * 8:(tt + 1) * 8],
                                                            bank7[:, tt * 8:(tt + 1) * 8]),
                             deps=[pe_last], sig=True)
                evac_tok = st2
        bc_scope.__exit__(None, None, None)
        y2done = [("l0fdw0", p.cnt.get("l0fdw0", 0)), ("l0fdw1", p.cnt.get("l0fdw1", 0))]
        r2row_d = p.dram("r2row_d", [32, PD], F32)
        s2_tok, _ = emit_ln_smalls(p, g, stats2, r2row_d, None, st2, kind="rms", eps=RMS_EPS)

        wr = p.sb("l0f_wr", [PD, DC, 8], BF16)
        twr = p.dma("pool", wr[:], W["Wr"].rearrange("(c k) m -> k c m", k=PD), ch="l0fwr")
        ebB = p.sb("l0f_ebB", [PD, 8], F32)
        teb = p.dma("sp", ebB[:], W["expert_bias"].rearrange("(o n) -> o n", o=1)
                    .broadcast_to([PD, 8]), ch="l0feb")
        iota8 = p.sb("l0f_iota8", [PD, 8], F32)
        iotaP = p.sb("l0f_iotaP", [PD, 8], F32)
        ti0 = p.dma("sp", iota8[:], W["c_iota8"], ch="l0fi0")
        ti1 = p.dma("sp", iotaP[:], W["c_iota8p9"], ch="l0fi1")

        w1b = [p.sb("l0f_w1b", [PD, DC, 512], BF16) for _ in range(2)]
        w3b = [p.sb("l0f_w3b", [PD, DC, 512], BF16) for _ in range(2)]
        w2e = p.sb("l0f_w2e", [PD, 8, 1024], BF16)
        xq = p.sb("l0f_xq", [PD, DC, QT], BF16)
        hstage = p.sb("l0f_hst", [PD, 8, QT], BF16)
        acc = p.sb("l0f_acc", [PD, DC, QT], F32)
        cB = p.sb("l0f_cB", [PD, QT], F32)
        ctp = p.sb("l0f_ctp", [8, PD], F32)
        crow_d = p.dram("l0f_crow", [E8, QT], F32)
        NPAR = 4
        ABK = (0, 1, 6, 7)
        sA = [p.sb("l0f_sA", [PD, 256], F32) for _ in range(NPAR)]
        sA2 = [p.sb("l0f_sA2", [PD, 256], F32) for _ in range(NPAR)]
        lgs = p.sb("l0f_lgs", [PD, 8], F32)
        prb = p.sb("l0f_prb", [PD, 8], F32)
        m1 = p.sb("l0f_m1", [PD, 1], F32)
        m2 = p.sb("l0f_m2", [PD, 1], F32)
        i1 = p.sb("l0f_i1", [PD, 1], F32)
        ssum = p.sb("l0f_ssum", [PD, 1], F32)
        eq1 = p.sb("l0f_eq1", [PD, 8], F32)
        eq2 = p.sb("l0f_eq2", [PD, 8], F32)
        msk = p.sb("l0f_msk", [PD, 8], F32)
        c1 = p.sb("l0f_c1", [PD, 1], F32)
        c2 = p.sb("l0f_c2", [PD, 1], F32)
        cmat = p.sb("l0f_cmat", [PD, 8], F32)

        war_w = [None, None]
        war_w2 = None
        upd_dep = []
        for q in range(NQ):
            xq_tok = None
            with p.scope():
                ytl = [p.sb("l0fe_y", [PD, DC, 512], F32) for _ in range(2)]
                sB = [p.sb("l0fe_s", [PD, 512], F32) for _ in range(2)]
                war = [None, None]
                for i5, t0 in enumerate(range(q * QT, (q + 1) * QT, 512)):
                    b = i5 % 2
                    dl = [s2_tok, war[b]] + y2done + upd_dep
                    tl = p.dma("sp", ytl[b][:], Yr[:, :, t0:t0 + 512], deps=dl, ch=f"l0fey{b}")
                    tsb = p.dma("sp", sB[b][:], row_bcast_ap(r2row_d, t0, 512), deps=dl,
                                ch=f"l0fes{b}")
                    for dc in range(DC):
                        xq_tok = p.emit("dve", lambda E, a=ytl[b], s_=sB[b], dc=dc, t0=t0, q=q:
                                        E.scalar_tensor_tensor(
                                            xq[:, dc, t0 - q * QT:t0 - q * QT + 512],
                                            a[:, dc], g2sb[:, dc:dc + 1], s_[:],
                                            ALU.mult, ALU.mult),
                                        deps=[tl, tsb, tg2, war[b]], sig=True)
                    war[b] = xq_tok
            rt_tok = None
            for cidx in range(QT // 128):
                gt0 = cidx * 128
                mml = None
                for kc in range(DC):
                    mml = p.emit("pe", lambda E, kc=kc, gt0=gt0, st=(kc == 0):
                                 E.matmul(p.banks[6][:, 0:8], xq[:, kc, gt0:gt0 + 128],
                                          wr[:, kc, :], start=st, stop=(kc == DC - 1)),
                                 deps=[xq_tok, twr, rt_tok], sig=(kc == DC - 1))
                tev = p.emit("act", lambda E: E.copy(lgs[:], p.banks[6][:, 0:8]),
                             deps=[mml], sig=True)
                p.emit("dve", lambda E: E.tensor_tensor(lgs[:], lgs[:], ebB[:], ALU.add),
                       deps=[tev, teb, ti0, ti1])
                p.emit("dve", lambda E: E.tensor_reduce(m1[:], lgs[:], AX.X, ALU.max))
                p.emit("dve", lambda E: E.tensor_scalar(lgs[:], lgs[:], m1[:], None, ALU.subtract))
                t_e = p.emit("act", lambda E: E.activation(prb[:], lgs[:], AF.Exp),
                             deps=[("dve", p.cnt["dve"])], sig=True)
                p.emit("dve", lambda E: E.tensor_reduce(ssum[:], prb[:], AX.X, ALU.add),
                       deps=[t_e])
                p.emit("dve", lambda E: E.reciprocal(ssum[:], ssum[:]))
                p.emit("dve", lambda E: E.tensor_scalar(prb[:], prb[:], ssum[:], None, ALU.mult))
                p.emit("dve", lambda E: E.tensor_reduce(m1[:], prb[:], AX.X, ALU.max))
                p.emit("dve", lambda E: E.tensor_scalar(eq1[:], prb[:], m1[:], None, ALU.is_ge))
                p.emit("dve", lambda E: E.scalar_tensor_tensor(msk[:], eq1[:], -100.0, iotaP[:],
                                                               ALU.mult, ALU.add))
                p.emit("dve", lambda E: E.tensor_reduce(i1[:], msk[:], AX.X, ALU.min))
                p.emit("dve", lambda E: E.tensor_scalar(eq1[:], iota8[:], i1[:], None, ALU.is_equal))
                p.emit("dve", lambda E: E.scalar_tensor_tensor(msk[:], eq1[:], -1e38, prb[:],
                                                               ALU.mult, ALU.add))
                p.emit("dve", lambda E: E.tensor_reduce(m2[:], msk[:], AX.X, ALU.max))
                p.emit("dve", lambda E: E.tensor_scalar(eq2[:], msk[:], m2[:], None, ALU.is_ge))
                t6 = p.emit("dve", lambda E: E.tensor_tensor(c1[:], m1[:], m2[:], ALU.subtract),
                            sig=True)
                t7 = p.emit("act", lambda E: E.activation(c2[:], c1[:], AF.Tanh, scale=0.5),
                            deps=[t6], sig=True)
                p.emit("dve", lambda E: E.tensor_scalar(c1[:], c2[:], 0.5, 0.5, ALU.mult, ALU.add),
                       deps=[t7])
                p.emit("dve", lambda E: E.tensor_scalar(c2[:], c1[:], -1.0, 1.0, ALU.mult, ALU.add))
                p.emit("dve", lambda E: E.tensor_scalar(cmat[:], eq1[:], c1[:], None, ALU.mult))
                t8 = p.emit("dve", lambda E: E.scalar_tensor_tensor(cmat[:], eq2[:], c2[:], cmat[:],
                                                                    ALU.mult, ALU.add), sig=True)
                ttr = p.emit("pe", lambda E: E.transpose(p.banks[5][0:8, 0:128],
                                                         cmat[:, 0:8], g["ident"][:]),
                             deps=[t8, rt_tok], sig=True)
                tcp = p.emit("act", lambda E: E.copy(ctp[:], p.banks[5][0:8, 0:128]),
                             deps=[ttr], sig=True)
                rt_tok = p.dma("sp", crow_d[:, gt0:gt0 + 128], ctp[:], deps=[tcp, rt_tok],
                               ch="l0fcr")
            acc_tok = None
            first_e = True
            cb_war = None
            par = 0
            prev_h = [rt_tok] * NPAR
            for e in range(E8 + 1):
                shared = (e == E8)
                if not shared:
                    cb_tok = p.dma("sp", cB[:],
                                   crow_d[e].rearrange("(o n) -> o n", o=1).broadcast_to([PD, QT]),
                                   deps=[rt_tok, cb_war], ch="l0fcb")
                    cb_war = None
                h_all = None
                for hb in range(2):
                    wpar = (e * 2 + hb) % 2
                    if shared:
                        src1 = W["sW1"][:, hb * 512:(hb + 1) * 512]
                        src3 = W["sW3"][:, hb * 512:(hb + 1) * 512]
                        src2 = W["sW2"][hb * 512:(hb + 1) * 512, :]
                    else:
                        src1 = W["eW1"][e, :, hb * 512:(hb + 1) * 512]
                        src3 = W["eW3"][e, :, hb * 512:(hb + 1) * 512]
                        src2 = W["eW2"][e, hb * 512:(hb + 1) * 512, :]
                    tw1 = p.dma("pool", w1b[wpar][:], src1.rearrange("(c k) m -> k c m", k=PD),
                                deps=[war_w[wpar]], ch=f"l0fw1{wpar}")
                    tw3 = p.dma("pool", w3b[wpar][:], src3.rearrange("(c k) m -> k c m", k=PD),
                                deps=[war_w[wpar]], ch=f"l0fw3{wpar}")
                    tw2 = p.dma("pool", w2e[:, hb * 4:(hb + 1) * 4, :],
                                src2.rearrange("(c k) m -> k c m", k=PD),
                                deps=[war_w2], ch="l0fw2l")
                    for hc in range(4):
                        for tc in range(NTC):
                            bk = p.banks[ABK[par]]
                            for kc in range(DC):
                                p.emit("pe", lambda E, kc=kc, hc=hc, tc=tc, bk=bk, wp=wpar,
                                       st=(kc == 0):
                                       E.matmul(bk[:, 0:256], w1b[wp][:, kc, hc * 128:(hc + 1) * 128],
                                                xq[:, kc, tc * 256:(tc + 1) * 256],
                                                start=st, stop=(kc == DC - 1)),
                                       deps=[tw1, xq_tok, prev_h[par]])
                            tpe = None
                            for kc in range(DC):
                                tpe = p.emit("pe", lambda E, kc=kc, hc=hc, tc=tc, bk=bk, wp=wpar,
                                             st=(kc == 0):
                                             E.matmul(bk[:, 256:512],
                                                      w3b[wp][:, kc, hc * 128:(hc + 1) * 128],
                                                      xq[:, kc, tc * 256:(tc + 1) * 256],
                                                      start=st, stop=(kc == DC - 1)),
                                             deps=[tw3], sig=(kc == DC - 1))
                            tsl = p.emit("act", lambda E, bk=bk, par=par: E.activation(
                                sA[par][:], bk[:, 0:256], AF.Silu), deps=[tpe], sig=True)
                            if shared:
                                tcm = p.emit("pool", lambda E, par=par: E.tensor_copy(
                                    sA2[par][:], sA[par][:]), deps=[tsl], sig=True)
                            else:
                                tcm = p.emit("pool", lambda E, par=par, tc=tc: E.tensor_mul(
                                    sA2[par][:], sA[par][:], cB[:, tc * 256:(tc + 1) * 256]),
                                    deps=[tsl, cb_tok], sig=True)
                                cb_war = tcm
                            th = p.emit("dve", lambda E, bk=bk, par=par, hb=hb, hc=hc, tc=tc:
                                        E.tensor_tensor(hstage[:, hb * 4 + hc, tc * 256:(tc + 1) * 256],
                                                        sA2[par][:], bk[:, 256:512], ALU.mult),
                                        deps=[tcm], sig=True)
                            prev_h[par] = th
                            h_all = th
                            par = (par + 1) % NPAR
                    war_w[wpar] = h_all
                ev_prev = None
                for tc in range(NTC):
                    mm_last = None
                    for dc2 in range(DC):
                        bk = p.banks[2 + dc2 // 2]
                        half = (dc2 % 2) * 256
                        for hh in range(8):
                            mm_last = p.emit("pe", lambda E, hh=hh, dc2=dc2, tc=tc, bk=bk,
                                             half=half, st=(hh == 0):
                                             E.matmul(bk[:, half:half + 256],
                                                      w2e[:, hh, dc2 * 128:(dc2 + 1) * 128],
                                                      hstage[:, hh, tc * 256:(tc + 1) * 256],
                                                      start=st, stop=(hh == 7)),
                                             deps=[h_all, tw2, ev_prev],
                                             sig=(hh == 7 and dc2 == DC - 1))
                    ev_toks = []
                    for dc2 in range(DC):
                        bk = p.banks[2 + dc2 // 2]
                        half = (dc2 % 2) * 256
                        if first_e:
                            tev = p.emit("dve", lambda E, bk=bk, half=half, dc2=dc2, tc=tc:
                                         E.tensor_copy(acc[:, dc2, tc * 256:(tc + 1) * 256],
                                                       bk[:, half:half + 256]),
                                         deps=[mm_last], sig=True)
                        else:
                            tev = p.emit("dve", lambda E, bk=bk, half=half, dc2=dc2, tc=tc:
                                         E.tensor_tensor(acc[:, dc2, tc * 256:(tc + 1) * 256],
                                                         acc[:, dc2, tc * 256:(tc + 1) * 256],
                                                         bk[:, half:half + 256], ALU.add),
                                         deps=[mm_last], sig=True)
                        ev_toks.append(tev)
                    ev_prev = ev_toks[-1]
                acc_tok = ev_prev
                war_w2 = acc_tok
                first_e = False
            with p.scope():
                xt = [p.sb("l0fu_x", [PD, DC, 512], F32) for _ in range(2)]
                yt2 = [p.sb("l0fu_y", [PD, DC, 512], F32) for _ in range(2)]
                war = [None, None]
                st_toks = []
                for i5, t0 in enumerate(range(q * QT, (q + 1) * QT, 512)):
                    b = i5 % 2
                    tl = p.dma("sp", xt[b][:], Xr[:, :, t0:t0 + 512],
                               deps=[war[b], acc_tok], ch=f"l0fux{b}")
                    tly = p.dma("sp", yt2[b][:], Yr[:, :, t0:t0 + 512],
                                deps=[war[b], acc_tok] + y2done, ch=f"l0fuy{b}")
                    ta = None
                    for dc in range(DC):
                        p.emit("dve", lambda E, x_=xt[b], y_=yt2[b], dc=dc:
                               E.tensor_tensor(x_[:, dc], x_[:, dc], y_[:, dc], ALU.add),
                               deps=[tl, tly])
                        ta = p.emit("dve", lambda E, x_=xt[b], dc=dc, t0=t0, q=q:
                                    E.tensor_tensor(x_[:, dc], x_[:, dc],
                                                    acc[:, dc, t0 - q * QT:t0 - q * QT + 512],
                                                    ALU.add), sig=True)
                    ts_ = p.dma("sp", Xr[:, :, t0:t0 + 512], xt[b][:], deps=[ta], ch=f"l0fuw{b}")
                    war[b] = ts_
                    st_toks.append(ts_)
            upd_dep = st_toks[-2:]
    return st_toks


def build_l0ffn_prog(NQ_lim=None, E_lim=None):
    p = Prog()
    c = Consts(p)
    p.alloc_banks()
    X = p.nc.dram_tensor("Xio", [D, T], F32, kind="ExternalInput").ap()
    Xout = p.nc.dram_tensor("Xout", [D, T], F32, kind="ExternalOutput").ap()
    W = {}
    for nm, shp in [("g1", [D]), ("g2", [D]), ("Wq", [D, D]), ("Wkc", [D, 256]),
                    ("Wvc", [D, 256]), ("Wo", [D, D]), ("Wr", [D, 8]),
                    ("expert_bias", [8]), ("sW1", [D, D]), ("sW2", [D, D]),
                    ("sW3", [D, D]), ("eW1", [8, D, D]), ("eW2", [8, D, D]),
                    ("eW3", [8, D, D])]:
        W[nm] = p.nc.dram_tensor(nm, shp, F32, kind="ExternalInput").ap()
    iota = np.arange(8, dtype=np.float32)
    W["c_iota8"] = c.add("c_iota8", np.broadcast_to(iota, (PD, 8)).copy())
    W["c_iota8p9"] = c.add("c_iota8p9", np.broadcast_to(iota + 100.0, (PD, 8)).copy())
    g = load_globals(p, c)
    Xs = p.dram("Xs", [D, T], F32)
    cp = []
    for dc in range(DC):
        cp.append(p.dma("sp", Xs[dc * PD:(dc + 1) * PD, :], X[dc * PD:(dc + 1) * PD, :], ch="st"))
    cp = cp[-1:]
    stores = emit_l0ffn(p, g, Xs, W, deps=cp)
    out_toks = []
    for dc in range(DC):
        out_toks.append(p.dma("sp", Xout[dc * PD:(dc + 1) * PD, :], Xs[dc * PD:(dc + 1) * PD, :],
                              deps=stores[-2:], ch="out"))
    p.emit("sp", lambda E: E.nop(), deps=out_toks[-1:], sig=True)
    nc = p.finalize()
    return p, c, nc


def prep_l0ffn_weights(ins):
    return {
        "g1": ins["L0_ffn_g1"], "g2": ins["L0_ffn_g2"],
        "Wq": np.ascontiguousarray(ins["L0_ffn_Wq"].T),
        "Wkc": np.ascontiguousarray(ins["L0_ffn_Wkc"].T),
        "Wvc": np.ascontiguousarray(ins["L0_ffn_Wvc"].T),
        "Wo": np.ascontiguousarray(ins["L0_ffn_Wo"].T),
        "Wr": np.ascontiguousarray(ins["L0_ffn_Wr"].T),
        "expert_bias": ins["L0_ffn_expert_bias"],
        "sW1": np.ascontiguousarray(ins["L0_ffn_sW1"].T),
        "sW2": np.ascontiguousarray(ins["L0_ffn_sW2"].T),
        "sW3": np.ascontiguousarray(ins["L0_ffn_sW3"].T),
        "eW1": np.ascontiguousarray(ins["L0_ffn_eW1"].transpose(0, 2, 1)),
        "eW2": np.ascontiguousarray(ins["L0_ffn_eW2"].transpose(0, 2, 1)),
        "eW3": np.ascontiguousarray(ins["L0_ffn_eW3"].transpose(0, 2, 1)),
    }



def emit_y_bf16(p, g, ln, X, Ybf_d, deps=()):
    Xr = X.rearrange("(c k) t -> k c t", k=PD)
    Yw = Ybf_d.rearrange("(c k) t -> k c t", k=PD)
    with p.scope():
        xt = [p.sb("yb_x", [PD, DC, 512], F32) for _ in range(2)]
        sB = [p.sb("yb_s", [PD, 512], F32) for _ in range(2)]
        bB = [p.sb("yb_b", [PD, 512], F32) for _ in range(2)]
        yb = [p.sb("yb_y", [PD, DC, 512], BF16) for _ in range(2)]
        tmp = [p.sb("yb_t", [PD, 512], F32) for _ in range(2)]
        war = [None, None]
        ystore = [None, None]
        for tt in range(NT5):
            b = tt % 2
            dl = list(deps) + [war[b]] + ln.ready
            tl = p.dma("sp", xt[b][:], Xr[:, :, tt * 512:(tt + 1) * 512], deps=dl, ch=f"ybx{b}")
            tsb = p.dma("sp", sB[b][:], row_bcast_ap(ln.srow_d, tt * 512, 512), deps=dl, ch=f"ybs{b}")
            tbb = p.dma("sp", bB[b][:], row_bcast_ap(ln.brow_d, tt * 512, 512), deps=dl, ch=f"ybb{b}")
            last = None
            for dc in range(DC):
                tp = p.emit("pool", lambda E, a=xt[b], o=tmp[b], s_=sB[b], dc=dc:
                            E.tensor_mul(o[:], a[:, dc], s_[:]),
                            deps=[tl, tsb, war[b], ystore[b], last], sig=True)
                last = p.emit("dve", lambda E, o=tmp[b], y_=yb[b], b_=bB[b], dc=dc:
                              E.tensor_tensor(y_[:, dc], o[:], b_[:], ALU.add),
                              deps=[tp, tbb], sig=True)
            war[b] = last
            ystore[b] = p.dma("sp", Yw[:, :, tt * 512:(tt + 1) * 512], yb[b][:],
                              deps=[last], ch=f"ybw{b}")
    return [("ybw0", p.cnt.get("ybw0", 0)), ("ybw1", p.cnt.get("ybw1", 0))]


def emit_proj_from_hbm(p, g, name, W_ap, src_d, Mcols, evac, deps=(), tt=512,
                       banks=(0, 1), cast=False):
    KC = DC
    Sr = src_d.rearrange("(c k) t -> k c t", k=PD)
    with p.scope():
        wsb = p.sb(f"{name}_w", [PD, KC, Mcols], BF16)
        tw = p.dma("pool", wsb[:], W_ap.rearrange("(c k) m -> k c m", k=PD),
                   deps=deps, ch=f"{name}w")
        st = [p.sb(f"{name}_s", [PD, DC, tt], BF16) for _ in range(2)]
        war = [None, None]
        bank_war = {b: None for b in banks}
        last = None
        pi = 0
        for t0 in range(0, T, tt):
            b = (t0 // tt) % 2
            tl = p.dma("sp", st[b][:], Sr[:, :, t0:t0 + tt], deps=list(deps) + [war[b]],
                       ch=f"{name}s{b}")
            for jc in range(Mcols // PD):
                bk = p.banks[banks[pi % len(banks)]]
                bw = bank_war[banks[pi % len(banks)]]
                pi += 1
                tok = None
                for kc in range(KC):
                    tok = p.emit("pe", lambda E, kc=kc, jc=jc, b=b, bk=bk, st_=(kc == 0):
                                 E.matmul(bk[:, 0:tt], wsb[:, kc, jc * PD:(jc + 1) * PD],
                                          st[b][:, kc, :], start=st_, stop=(kc == KC - 1)),
                                 deps=[tw, tl, bw], sig=(kc == KC - 1))
                ev = evac(jc * PD, t0, tt, bk, tok)
                bank_war[banks[(pi - 1) % len(banks)]] = ev
                war[b] = ev
                last = ev
    return last


def emit_norm_max(p, g, src_d, rows, out_sb, out_col, deps=(), bank_i=6):
    RC = rows // PD
    Sr = src_d.rearrange("(c k) t -> k c t", k=PD)
    bank = p.banks[bank_i]
    with p.scope():
        st = [p.sb("nm_s", [PD, RC, 512], BF16) for _ in range(2)]
        sq = [p.sb("nm_q", [PD, RC, 512], F32) for _ in range(2)]
        mx = p.sb("nm_m", [1, NT5], F32)
        war = [None, None]
        red = None
        for t5 in range(NT5):
            b = t5 % 2
            tl = p.dma("sp", st[b][:], Sr[:, :, t5 * 512:(t5 + 1) * 512],
                       deps=list(deps) + [war[b]], ch=f"nms{b}")
            sq_t = []
            for rc in range(RC):
                sq_t.append(p.emit("pool", lambda E, b=b, rc=rc: E.tensor_mul(
                    sq[b][:, rc], st[b][:, rc], st[b][:, rc]),
                    deps=[tl, war[b]], sig=True))
            mt = None
            for rc in range(RC):
                mt = p.emit("pe", lambda E, b=b, rc=rc, st_=(rc == 0):
                            E.matmul(bank[0:1, 0:512], g["ones_f"][:], sq[b][:, rc],
                                     start=st_, stop=(rc == RC - 1)),
                            deps=[sq_t[rc], red] + g["ready"], sig=(rc == RC - 1))
            red = p.emit("dve", lambda E, t5=t5: E.tensor_reduce(
                mx[0:1, t5:t5 + 1], bank[0:1, 0:512], AX.X, ALU.max), deps=[mt], sig=True)
            war[b] = mt
        tok = p.emit("dve", lambda E: E.tensor_reduce(out_sb[0:1, out_col:out_col + 1],
                                                      mx[:], AX.X, ALU.max),
                     deps=[red], sig=True)
    return tok



def emit_l0ab(p, g, X, W, deps=()):
    SCALE = 0.03125
    WIN = 1024
    ln = OuterLN(p, g, X, deps=deps)
    Ybf = p.dram("l0ab_Y", [D, T], BF16)
    Qd = p.dram("l0ab_Q", [D, T], BF16)
    Vd = p.dram("l0ab_V", [T // PD, PD, 1032], BF16)
    Xr = X.rearrange("(c k) t -> k c t", k=PD)
    ytok = emit_y_bf16(p, g, ln, X, Ybf, deps=deps)
    with p.scope():
        def ev_q(j0, t0, w, bk, tok, _s=[None, None], _st=[None, None]):
            b = (t0 // 512) % 2
            if _s[b] is None:
                _s[b] = p.sb("l0ab_qe", [PD, 512], BF16)
            t1 = p.emit("act", lambda E, s_=_s[b], w=w, bk=bk: E.copy(s_[:, 0:w], bk[:, 0:w]),
                        deps=[tok, _st[b]], sig=True)
            t2 = p.dma("sp", Qd[j0:j0 + PD, t0:t0 + w], _s[b][:, 0:w], deps=[t1], ch=f"l0abq{b}")
            _st[b] = t2
            return t2
        q_tok = emit_proj_from_hbm(p, g, "l0abQ", W["W_qk"], Ybf, D, ev_q, deps=ytok)
        qdone = [("l0abq0", p.cnt.get("l0abq0", 0)), ("l0abq1", p.cnt.get("l0abq1", 0)), q_tok]
        vv = p.sb("l0ab_vt", [PD, 1032], BF16)
        wvt = p.sb("l0ab_wv", [PD, DC, D], BF16)
        twv = p.dma("pool", wvt[:], W["WvT"].rearrange("(c k) m -> k c m", k=PD), ch="l0abwv")
        yst = [p.sb("l0ab_ys", [PD, DC, 512], BF16) for _ in range(2)]
        war = [None, None]
        vstore = None
        mtok = p.emit("dve", lambda E: E.memset(vv[:, 1024:1032], 0.0), sig=True)
        mtok = p.emit("dve", lambda E: E.memset(vv[:, 1024:1025], 1.0), deps=[mtok], sig=True)
        Yr = Ybf.rearrange("(c k) t -> k c t", k=PD)
        for t5 in range(NT5):
            b = t5 % 2
            tl = p.dma("sp", yst[b][:], Yr[:, :, t5 * 512:(t5 + 1) * 512],
                       deps=ytok + [war[b]], ch=f"l0abv{b}")
            for tc4 in range(4):
                tc = t5 * 4 + tc4
                ev = None
                for half in range(2):
                    tok = None
                    for kc in range(DC):
                        tok = p.emit("pe", lambda E, kc=kc, b=b, tc4=tc4, half=half,
                                     st=(kc == 0):
                                     E.matmul(p.banks[2 + half][:, 0:512],
                                              yst[b][:, kc, tc4 * 128:(tc4 + 1) * 128],
                                              wvt[:, kc, half * 512:(half + 1) * 512],
                                              start=st, stop=(kc == DC - 1)),
                                     deps=[twv, tl, vstore], sig=(kc == DC - 1))
                    ev = p.emit("dve", lambda E, half=half: E.tensor_copy(
                        vv[:, half * 512:(half + 1) * 512], p.banks[2 + half][:, 0:512]),
                        deps=[tok, mtok], sig=True)
                vstore = p.dma("sp", Vd[tc], vv[:], deps=[ev], ch="l0abvw")
                war[b] = vstore
        vdone = [("l0abvw", p.cnt.get("l0abvw", 0))]
        sm = p.sb("l0ab_sm", [1, 4], F32)
        nt = emit_norm_max(p, g, Qd, D, sm, 0, deps=qdone)
        t3 = p.emit("act", lambda E: E.activation(sm[0:1, 1:2], sm[0:1, 0:1], AF.Ln),
                    deps=[nt], sig=True)
        t4 = p.emit("act", lambda E: E.activation(sm[0:1, 2:3], sm[0:1, 1:2], AF.Exp, scale=0.5),
                    deps=[t3], sig=True)
        t5_ = p.emit("dve", lambda E: E.tensor_scalar(sm[0:1, 3:4], sm[0:1, 2:3],
                                                      32.01 * SCALE, None, ALU.mult),
                     deps=[t4], sig=True)
        sh_d = p.dram("l0ab_sh", [1, 1], F32)
        t6 = p.dma("sp", sh_d, sm[0:1, 3:4], deps=[t5_], ch="l0absh")
        shB = p.sb("l0ab_shB", [PD, 1], F32)
        t7 = p.dma("sp", shB[:], sh_d.broadcast_to([PD, 1]), deps=[t6], ch="l0absh2")
        slB = p.sb("l0ab_slB", [PD, 1], F32)
        t8 = p.dma("sp", slB[:], W["alibi_slope"].rearrange("(o n) -> o n", o=1)
                   .broadcast_to([PD, 1]), ch="l0absl")
        iqc = p.sb("l0ab_iq", [PD, 1], F32)
        t9 = p.dma("sp", iqc[:], W["c_iota128"], ch="l0abio")
        trim = p.sb("l0ab_tri", [PD, PD], BF16)
        t10 = p.dma("sp", trim[:], W["c_trimask"], ch="l0abtm")
        qt = [p.sb("l0ab_qt", [PD, DC, 128], BF16) for _ in range(2)]
        kt = [p.sb("l0ab_kt", [PD, DC, 128], BF16) for _ in range(2)]
        NW = 9
        Vw = p.sb("l0ab_Vw", [PD, NW, 1032], BF16)
        Pw = p.sb("l0ab_Pw", [PD, NW, 128], BF16)
        rbias = p.sb("l0ab_rb", [PD, 4], F32)
        rsum = p.sb("l0ab_rsum", [1, 128], F32)
        rB = p.sb("l0ab_rB", [PD, 128], F32)
        delta = p.sb("l0ab_dl", [PD, DC, 512], F32)
        Qr = Qd.rearrange("(c k) t -> k c t", k=PD)
        qwar = [None, None]
        kwar = [None, None]
        vwar = None
        pwar = None
        prev_tr = None
        prev_texp = None
        del_tok = None
        upd = []
        for tq in range(T // PD):
            bq = tq % 2
            tql = p.dma("sp", qt[bq][:], Qr[:, :, tq * 128:(tq + 1) * 128],
                        deps=qdone + [qwar[bq]], ch=f"l0abqt{bq}")
            tk_lo = max(0, tq - 8)
            nw = tq + 1 - tk_lo
            texp = None
            vls = []
            for i in range(nw):
                tkc = tk_lo + i
                bk_ = i % 2
                pr = i % 2
                bkS = p.banks[4 + pr]
                tkl = p.dma("sp", kt[bk_][:], Yr[:, :, tkc * 128:(tkc + 1) * 128],
                            deps=ytok + [kwar[bk_]], ch=f"l0abkt{bk_}")
                tvl = p.dma("sp", Vw[:, i, :], Vd[tkc], deps=vdone + [vwar],
                            ch=f"l0abvt{i}")
                vls.append(tvl)
                tok = None
                for kc in range(DC):
                    tok = p.emit("pe", lambda E, kc=kc, bk_=bk_, bq=bq, bkS=bkS, st=(kc == 0):
                                 E.matmul(bkS[:, 0:128], kt[bk_][:, kc, :], qt[bq][:, kc, :],
                                          start=st, stop=(kc == DC - 1)),
                                 deps=[tkl, tql, prev_texp], sig=(kc == DC - 1))
                kwar[bk_] = tok
                tb1 = p.emit("dve", lambda E, d=float(tkc * 128 - tq * 128): E.tensor_scalar(
                    rbias[:, 0:1], iqc[:], d, None, ALU.add),
                    deps=[t9, prev_tr, prev_texp], sig=True)
                tb2 = p.emit("dve", lambda E: E.tensor_tensor(rbias[:, 1:2], rbias[:, 0:1],
                                                              slB[:], ALU.mult),
                             deps=[tb1, t8], sig=True)
                tb3 = p.emit("dve", lambda E: E.tensor_tensor(rbias[:, 2:3], rbias[:, 1:2],
                                                              shB[:], ALU.add),
                             deps=[tb2, t7], sig=True)
                texp = p.emit("act", lambda E, i=i, bkS=bkS: E.activation(
                    Pw[:, i, :], bkS[:, 0:128], AF.Exp, bias=rbias[:, 2:3], scale=SCALE),
                    deps=[tok, tb3, pwar], sig=True)
                prev_texp = texp
                if tkc == tq:
                    texp = p.emit("dve", lambda E, i=i: E.tensor_tensor(
                        Pw[:, i, :], Pw[:, i, :], trim[:], ALU.mult),
                        deps=[texp, t10], sig=True)
            sum_tok = None
            for i in range(nw):
                sum_tok = p.emit("pe", lambda E, i=i, st=(i == 0), lk=(i == nw - 1):
                                 E.matmul(p.banks[2][0:1, 0:128], Vw[:, i, 1024:1025],
                                          Pw[:, i, :], start=st, stop=lk),
                                 deps=[texp, vls[i], prev_tr, del_tok], sig=(i == nw - 1))
            tr = p.emit("dve", lambda E: E.reciprocal(rsum[:], p.banks[2][0:1, 0:128]),
                        deps=[sum_tok], sig=True)
            prev_tr = tr
            tb = p.emit("pe", lambda E: E.matmul(p.banks[3][:, 0:128], g["ones_row"][:],
                                                 rsum[:], start=True, stop=True),
                        deps=[tr], sig=True)
            tc_ = p.emit("dve", lambda E: E.tensor_copy(rB[:], p.banks[3][:, 0:128]),
                         deps=[tb], sig=True)
            jx_ev = {0: None, 1: None}
            for jx in range(8):
                bko = p.banks[jx % 2]
                mm_last = None
                for i in range(nw):
                    mm_last = p.emit("pe", lambda E, i=i, jx=jx, bko=bko, st=(i == 0),
                                     lk=(i == nw - 1):
                                     E.matmul(bko[:, 0:128], Vw[:, i, jx * 128:(jx + 1) * 128],
                                              Pw[:, i, :], start=st, stop=lk),
                                     deps=[texp, vls[i], jx_ev[jx % 2], del_tok],
                                     sig=(i == nw - 1))
                ev = p.emit("dve", lambda E, jx=jx, tq=tq, bko=bko: E.tensor_tensor(
                    delta[:, jx, (tq % 4) * 128:(tq % 4 + 1) * 128],
                    bko[:, 0:128], rB[:], ALU.mult),
                    deps=[mm_last, tc_], sig=True)
                jx_ev[jx % 2] = ev
                del_tok = ev
            pwar = mm_last
            vwar = mm_last
            qwar[bq] = mm_last
            if tq % 4 == 3:
                sts, _ = emit_x_update(p, g, X, delta, (tq - 3) * 128, (tq + 1) * 128, del_tok)
                upd = sts
    return upd



def emit_l1ab(p, g, X, W, deps=()):
    SCALE = 0.125
    NTK = T // PD
    ln = OuterLN(p, g, X, deps=deps)
    Ybf = p.dram("l1ab_Y", [D, T], BF16)
    Qd = p.dram("l1ab_Q", [D, T], BF16)
    Kd = p.dram("l1ab_K", [D, T], BF16)
    Vd = p.dram("l1ab_V", [NTK, PD, 16 * 66], BF16)
    ytok = emit_y_bf16(p, g, ln, X, Ybf, deps=deps)
    with p.scope():
        bq, tbq = load_bias_chunks(p, W["bq"], "l1ab_bq")
        bk, tbk = load_bias_chunks(p, W["bk"], "l1ab_bk")
        bv, tbv = load_bias_chunks(p, W["bv"], "l1ab_bv")
        bp, tbp = load_bias_chunks(p, W["bp"], "l1ab_bp")

        def mk_ev(dst, bias_sb, bias_tok, chname):
            bufs = [None, None]
            stoks = [None, None]
            def ev(j0, t0, w, bk_, tok):
                b = (t0 // 512) % 2
                if bufs[b] is None:
                    bufs[b] = p.sb(f"{chname}_e", [PD, 512], BF16)
                t1 = p.emit("dve", lambda E, s_=bufs[b], w=w, bk_=bk_, j0=j0: E.tensor_scalar(
                    s_[:, 0:w], bk_[:, 0:w], bias_sb[:, j0 // PD:j0 // PD + 1], None, ALU.add),
                    deps=[tok, bias_tok, stoks[b]], sig=True)
                t2 = p.dma("sp", dst[j0:j0 + PD, t0:t0 + w], bufs[b][:, 0:w],
                           deps=[t1], ch=f"{chname}{b}")
                stoks[b] = t2
                return t2
            return ev
        q_tok = emit_proj_from_hbm(p, g, "l1abQ", W["WqT"], Ybf, D,
                                   mk_ev(Qd, bq, tbq, "l1abq"), deps=ytok)
        k_tok = emit_proj_from_hbm(p, g, "l1abK", W["WkT"], Ybf, D,
                                   mk_ev(Kd, bk, tbk, "l1abk"), deps=ytok)
        qdone = [("l1abq0", p.cnt.get("l1abq0", 0)), ("l1abq1", p.cnt.get("l1abq1", 0)), q_tok]
        kdone = [("l1abk0", p.cnt.get("l1abk0", 0)), ("l1abk1", p.cnt.get("l1abk1", 0)), k_tok]
        wvt = p.sb("l1ab_wv", [PD, DC, D], BF16)
        twv = p.dma("pool", wvt[:], W["WvT"].rearrange("(c k) m -> k c m", k=PD), ch="l1abwv")
        bvB = p.sb("l1ab_bvB", [PD, D], F32)
        tbvB = p.dma("sp", bvB[:], W["bv"].rearrange("(o n) -> o n", o=1).broadcast_to([PD, D]),
                     ch="l1abbv")
        vv = p.sb("l1ab_vt", [PD, 16 * 66], BF16)
        yst = [p.sb("l1ab_ys", [PD, DC, 512], BF16) for _ in range(2)]
        war = [None, None]
        vstore = None
        mtok = p.emit("dve", lambda E: E.memset(vv[:], 0.0), sig=True)
        mtok = p.emit("dve", lambda E: E.memset(
            vv[:].rearrange("p (h c) -> p h c", c=66)[:, :, 64:65], 1.0), deps=[mtok], sig=True)
        Yr = Ybf.rearrange("(c k) t -> k c t", k=PD)
        for t5 in range(NT5):
            b = t5 % 2
            tl = p.dma("sp", yst[b][:], Yr[:, :, t5 * 512:(t5 + 1) * 512],
                       deps=ytok + [war[b]], ch=f"l1abv{b}")
            for tc4 in range(4):
                tc = t5 * 4 + tc4
                ev = None
                for half in range(2):
                    tok = None
                    for kc in range(DC):
                        tok = p.emit("pe", lambda E, kc=kc, b=b, tc4=tc4, half=half,
                                     st=(kc == 0):
                                     E.matmul(p.banks[2 + half][:, 0:512],
                                              yst[b][:, kc, tc4 * 128:(tc4 + 1) * 128],
                                              wvt[:, kc, half * 512:(half + 1) * 512],
                                              start=st, stop=(kc == DC - 1)),
                                     deps=[twv, tl, vstore], sig=(kc == DC - 1))
                    t1 = p.emit("dve", lambda E, half=half: E.scalar_tensor_tensor(
                        vv[:].rearrange("p (h c) -> p h c", c=66)[:, half * 8:(half + 1) * 8, 0:64],
                        p.banks[2 + half][:, 0:512].rearrange("p (h c) -> p h c", c=64),
                        1.0,
                        bvB[:, half * 512:(half + 1) * 512].rearrange("p (h c) -> p h c", c=64),
                        ALU.mult, ALU.add),
                        deps=[tok, tbvB, mtok], sig=True)
                    ev = t1
                vstore = p.dma("sp", Vd[tc], vv[:], deps=[ev], ch="l1abvw")
                war[b] = vstore
        vdone = [("l1abvw", p.cnt.get("l1abvw", 0))]
        lam = p.sb("l1ab_lam", [1, 512], F32)
        lsc = p.sb("l1ab_lsc", [1, 16], F32)
        tl1 = p.dma("sp", lam[0:1, 0:512], W["lambda_q1"].rearrange("h o c -> (h o c)")
                    .rearrange("(o n) -> o n", o=1), ch="l1abl1")
        lq2 = p.sb("l1ab_l2", [1, 512], F32)
        tmp5 = p.sb("l1ab_tmp5", [1, 512], F32)
        tl3 = p.dma("sp", lq2[0:1, 0:512], W["lambda_k1"].rearrange("h o c -> (h o c)")
                    .rearrange("(o n) -> o n", o=1), ch="l1abl2")
        t1 = p.emit("dve", lambda E: E.tensor_tensor(tmp5[:], lam[:], lq2[:], ALU.mult),
                    deps=[tl1, tl3], sig=True)
        t2 = p.emit("dve", lambda E: E.tensor_reduce(
            lsc[0:1, 0:8], tmp5[0:1, :].rearrange("o (h c) -> o h c", c=64), AX.X, ALU.add),
            deps=[t1], sig=True)
        tl4 = p.dma("sp", lam[0:1, 0:512], W["lambda_q2"].rearrange("h o c -> (h o c)")
                    .rearrange("(o n) -> o n", o=1), deps=[t1], ch="l1abl1")
        tl5 = p.dma("sp", lq2[0:1, 0:512], W["lambda_k2"].rearrange("h o c -> (h o c)")
                    .rearrange("(o n) -> o n", o=1), deps=[t1], ch="l1abl2")
        t3 = p.emit("dve", lambda E: E.tensor_tensor(tmp5[:], lam[:], lq2[:], ALU.mult),
                    deps=[tl4, tl5], sig=True)
        t4 = p.emit("dve", lambda E: E.tensor_reduce(
            lsc[0:1, 8:16], tmp5[0:1, :].rearrange("o (h c) -> o h c", c=64), AX.X, ALU.add),
            deps=[t3], sig=True)
        t5_ = p.emit("dve", lambda E: E.tensor_scalar_min(lsc[:], lsc[:], 5.0),
                     deps=[t2, t4], sig=True)
        t6 = p.emit("act", lambda E: E.activation(lsc[:], lsc[:], AF.Exp), deps=[t5_], sig=True)
        t7 = p.emit("dve", lambda E: E.tensor_tensor(lsc[0:1, 0:8], lsc[0:1, 0:8],
                                                     lsc[0:1, 8:16], ALU.subtract),
                    deps=[t6], sig=True)
        t8 = p.emit("dve", lambda E: E.tensor_reduce(lsc[0:1, 8:9], lsc[0:1, 0:8], AX.X, ALU.add),
                    deps=[t7], sig=True)
        t9 = p.emit("dve", lambda E: E.tensor_scalar(lsc[0:1, 9:10], lsc[0:1, 8:9],
                                                     0.125, 0.8, ALU.mult, ALU.add),
                    deps=[t8], sig=True)
        t10 = p.emit("dve", lambda E: E.tensor_scalar(lsc[0:1, 10:11], lsc[0:1, 9:10],
                                                      -1.0, None, ALU.mult), deps=[t9], sig=True)
        lv_d = p.dram("l1ab_lv", [1, 1], F32)
        t11 = p.dma("sp", lv_d, lsc[0:1, 10:11], deps=[t10], ch="l1ablv")
        lvB = p.sb("l1ab_lvB", [PD, 1], F32)
        t12 = p.dma("sp", lvB[:], lv_d.broadcast_to([PD, 1]), deps=[t11], ch="l1ablv2")
        sm = p.sb("l1ab_sm", [1, 6], F32)
        nq = emit_norm_max(p, g, Qd, D, sm, 0, deps=qdone)
        nk = emit_norm_max(p, g, Kd, D, sm, 1, deps=kdone)
        u1 = p.emit("dve", lambda E: E.tensor_tensor(sm[0:1, 2:3], sm[0:1, 0:1], sm[0:1, 1:2],
                                                     ALU.mult), deps=[nq, nk], sig=True)
        u2 = p.emit("act", lambda E: E.activation(sm[0:1, 3:4], sm[0:1, 2:3], AF.Ln),
                    deps=[u1], sig=True)
        u3 = p.emit("act", lambda E: E.activation(sm[0:1, 4:5], sm[0:1, 3:4], AF.Exp, scale=0.5),
                    deps=[u2], sig=True)
        u4 = p.emit("dve", lambda E: E.tensor_scalar(sm[0:1, 5:6], sm[0:1, 4:5], -SCALE, None,
                                                     ALU.mult), deps=[u3], sig=True)
        sh_d = p.dram("l1ab_sh", [1, 1], F32)
        u5 = p.dma("sp", sh_d, sm[0:1, 5:6], deps=[u4], ch="l1absh")
        shB = p.sb("l1ab_shB", [PD, 1], F32)
        u6 = p.dma("sp", shB[:], sh_d.broadcast_to([PD, 1]), deps=[u5], ch="l1absh2")
        o_fm = p.sb("l1ab_o", [PD, DC, T], BF16)
        kh = p.sb("l1ab_kh", [64, T], BF16)
        qh = p.sb("l1ab_qh", [64, T], BF16)
        vh = p.sb("l1ab_vh", [PD, NTK, 66], BF16)
        Pt = [p.sb("l1ab_P", [PD, 512], BF16) for _ in range(2)]
        rsum = p.sb("l1ab_rs", [1, 512], F32)
        rB = p.sb("l1ab_rB", [64, 512], F32)
        o_done = None
        for h in range(16):
            tkh = p.dma("sp", kh[:], Kd[h * 64:(h + 1) * 64, :], deps=kdone + [o_done],
                        ch="l1abkh")
            tqh = p.dma("sp", qh[:], Qd[h * 64:(h + 1) * 64, :], deps=qdone + [o_done],
                        ch="l1abqh")
            tvh = p.dma("sp", vh[:], Vd[:, :, h * 66:(h + 1) * 66].rearrange("a k c -> k a c"),
                        deps=vdone + [o_done], ch="l1abvh")
            prev_tr = None
            for tq in range(NT5):
                p_war = [None, None]
                s_war = [None, None]
                sum_tok = None
                for tkc in range(NTK):
                    pr = tkc % 2
                    bkS = p.banks[4 + pr]
                    tok = p.emit("pe", lambda E, tkc=tkc, tq=tq, bkS=bkS:
                                 E.matmul(bkS[:, 0:512], kh[:, tkc * PD:(tkc + 1) * PD],
                                          qh[:, tq * 512:(tq + 1) * 512],
                                          start=True, stop=True),
                                 deps=[tkh, tqh, s_war[pr]], sig=True)
                    texp = p.emit("act", lambda E, pr=pr, bkS=bkS: E.activation(
                        Pt[pr][:], bkS[:], AF.Exp, bias=shB[:], scale=SCALE),
                        deps=[tok, u6, p_war[pr]], sig=True)
                    s_war[pr] = texp
                    sum_tok = p.emit("pe", lambda E, tkc=tkc, pr=pr, st=(tkc == 0):
                                     E.matmul(p.banks[0][0:65, 0:512], vh[:, tkc, 0:65],
                                              Pt[pr][:], start=st, stop=(tkc == NTK - 1)),
                                     deps=[texp, tvh, prev_tr, o_done], sig=True)
                    p_war[pr] = sum_tok
                tr = p.emit("dve", lambda E: E.reciprocal(rsum[:], p.banks[0][64:65, 0:512]),
                            deps=[sum_tok], sig=True)
                prev_tr = tr
                tb = p.emit("pe", lambda E: E.matmul(p.banks[3][0:64, 0:512],
                                                     g["ones_row"][0:1, 0:64], rsum[:],
                                                     start=True, stop=True),
                            deps=[tr], sig=True)
                tc_ = p.emit("dve", lambda E: E.tensor_copy(rB[:], p.banks[3][0:64, 0:512]),
                             deps=[tb], sig=True)
                o_done = p.emit("dve", lambda E, h=h, tq=tq: E.tensor_tensor(
                    o_fm[(h % 2) * 64:(h % 2) * 64 + 64, h // 2, tq * 512:(tq + 1) * 512],
                    p.banks[0][0:64, 0:512], rB[:], ALU.mult),
                    deps=[tc_], sig=True)
        cmb = None
        for j in range(4):
            for t5 in range(NT5):
                cmb = p.emit("dve", lambda E, j=j, t5=t5: E.scalar_tensor_tensor(
                    o_fm[:, j, t5 * 512:(t5 + 1) * 512], o_fm[:, j + 4, t5 * 512:(t5 + 1) * 512],
                    lvB[:], o_fm[:, j, t5 * 512:(t5 + 1) * 512], ALU.mult, ALU.add),
                    deps=[o_done, t12], sig=True)
        wp = p.sb("l1ab_wp", [PD, DC, D], BF16)
        twp = p.dma("pool", wp[:], W["WpT"].rearrange("(c k) m -> k c m", k=PD), ch="l1abwp")
        delta = p.sb("l1ab_dl", [PD, DC, 512], F32)
        del_tok = None
        upd = []
        bank_war = [None, None]
        for t5 in range(NT5):
            for jc in range(DC):
                bk_ = p.banks[jc % 2]
                tok = None
                for kc in range(DC):
                    tok = p.emit("pe", lambda E, kc=kc, jc=jc, t5=t5, bk_=bk_, st=(kc == 0):
                                 E.matmul(bk_[:, 0:512], wp[:, kc, jc * PD:(jc + 1) * PD],
                                          o_fm[:, kc, t5 * 512:(t5 + 1) * 512],
                                          start=st, stop=(kc == DC - 1)),
                                 deps=[twp, cmb, bank_war[jc % 2]], sig=(kc == DC - 1))
                del_tok = p.emit("dve", lambda E, jc=jc, bk_=bk_: E.tensor_scalar(
                    delta[:, jc, :], bk_[:, 0:512], bp[:, jc:jc + 1], None, ALU.add),
                    deps=[tok, tbp], sig=True)
                bank_war[jc % 2] = del_tok
            sts, _ = emit_x_update(p, g, X, delta, t5 * 512, (t5 + 1) * 512, del_tok)
            upd = sts
    return upd



def emit_l1aa(p, g, X, W, deps=()):
    NH, NKV, HD, CH = 16, 4, 64, 128
    NC = T // CH
    ln = OuterLN(p, g, X, deps=deps)
    Ybf = p.dram("l1aa_Y", [D, T], BF16)
    Od = p.dram("l1aa_O", [D, T], BF16)
    ytok = emit_y_bf16(p, g, ln, X, Ybf, deps=deps)
    Yr = Ybf.rearrange("(c k) t -> k c t", k=PD)
    with p.scope():
        permT = p.sb("l1aa_perm", [PD, PD], BF16)
        tpm = p.dma("sp", permT[:], W["c_permT"], ch="l1aapm")
        cosf = p.sb("l1aa_cosf", [PD, T], BF16)
        sinf = p.sb("l1aa_sinf", [PD, T], BF16)
        tcf = p.dma("sp", cosf[:], W["c_cos_fm"], ch="l1aacf")
        tsf = p.dma("sp", sinf[:], W["c_sin_fm"], ch="l1aasf")
        cost = p.sb("l1aa_cost", [PD, NC, HD], BF16)
        sint = p.sb("l1aa_sint", [PD, NC, HD], BF16)
        tct = p.dma("sp", cost[:], W["c_cos_tm"], ch="l1aact")
        tst = p.dma("sp", sint[:], W["c_sin_tm"], ch="l1aast")
        maskT = p.sb("l1aa_mask", [PD, NH, CH], BF16)
        tmk = p.dma("sp", maskT[:], W["c_maskT"], ch="l1aamk")
        qdec = p.sb("l1aa_qdec", [PD, NH, CH], BF16)
        tqd = p.dma("sp", qdec[:], W["c_qdec"], ch="l1aaqd")
        kdec = p.sb("l1aa_kdec", [PD, NH], F32)
        tkd = p.dma("sp", kdec[:], W["c_kdec"], ch="l1aakd")

        q_fm = p.sb("l1aa_q", [PD, DC, T], BF16)

        def qmap(h):
            return ((h % 4) + 4 * (h // 8), 64 * ((h // 4) % 2))

        def ev_q(j0, t0, w, bk_, tok):
            tk2 = tok
            for hh in (j0 // 64, j0 // 64 + 1):
                pch, pb = qmap(hh)
                rb = (hh % 2) * 64
                tk2 = p.emit("act", lambda E, pch=pch, pb=pb, rb=rb, t0=t0, w=w, bk_=bk_:
                             E.copy(q_fm[pb:pb + 64, pch, t0:t0 + w], bk_[rb:rb + 64, 0:w]),
                             deps=[tok], sig=True)
            return tk2
        q_tok = emit_proj_from_hbm(p, g, "l1aaQ", W["WqT"], Ybf, D, ev_q, deps=ytok)
        rtmp = p.sb("l1aa_rt", [PD, 512], F32)
        rtmp2 = p.sb("l1aa_rt2", [PD, 512], F32)
        rope_tok = None
        for dc in range(DC):
            for t5 in range(NT5):
                mm = p.emit("pe", lambda E, dc=dc, t5=t5: E.matmul(
                    p.banks[6][:, 0:512], permT[:], q_fm[:, dc, t5 * 512:(t5 + 1) * 512],
                    start=True, stop=True), deps=[q_tok, tpm, rope_tok], sig=True)
                w1 = p.emit("dve", lambda E, t5=t5: E.tensor_tensor(
                    rtmp[:], p.banks[6][:, 0:512], sinf[:, t5 * 512:(t5 + 1) * 512], ALU.mult),
                    deps=[mm, tsf], sig=True)
                w2 = p.emit("dve", lambda E, dc=dc, t5=t5: E.tensor_tensor(
                    rtmp2[:], q_fm[:, dc, t5 * 512:(t5 + 1) * 512],
                    cosf[:, t5 * 512:(t5 + 1) * 512], ALU.mult), deps=[tcf], sig=True)
                rope_tok = p.emit("dve", lambda E, dc=dc, t5=t5: E.tensor_tensor(
                    q_fm[:, dc, t5 * 512:(t5 + 1) * 512], rtmp[:], rtmp2[:], ALU.add),
                    deps=[w1, w2], sig=True)
        k_fm = p.sb("l1aa_kf", [PD, 2, T], BF16)
        def ev_k(j0, t0, w, bk_, tok):
            return p.emit("act", lambda E, j0=j0, t0=t0, w=w, bk_=bk_: E.copy(
                k_fm[:, j0 // PD, t0:t0 + w], bk_[:, 0:w]), deps=[tok], sig=True)
        k_tok = emit_proj_from_hbm(p, g, "l1aaK", W["WkT"], Ybf, 256, ev_k, deps=ytok,
                                   banks=(2, 3))
        kr_tok = None
        for dc in range(2):
            for t5 in range(NT5):
                mm = p.emit("pe", lambda E, dc=dc, t5=t5: E.matmul(
                    p.banks[6][:, 0:512], permT[:], k_fm[:, dc, t5 * 512:(t5 + 1) * 512],
                    start=True, stop=True), deps=[k_tok, tpm, kr_tok, rope_tok], sig=True)
                w1 = p.emit("dve", lambda E, t5=t5: E.tensor_tensor(
                    rtmp[:], p.banks[6][:, 0:512], sinf[:, t5 * 512:(t5 + 1) * 512], ALU.mult),
                    deps=[mm, tsf], sig=True)
                w2 = p.emit("dve", lambda E, dc=dc, t5=t5: E.tensor_tensor(
                    rtmp2[:], k_fm[:, dc, t5 * 512:(t5 + 1) * 512],
                    cosf[:, t5 * 512:(t5 + 1) * 512], ALU.mult), deps=[tcf], sig=True)
                kr_tok = p.emit("dve", lambda E, dc=dc, t5=t5: E.tensor_tensor(
                    k_fm[:, dc, t5 * 512:(t5 + 1) * 512], rtmp[:], rtmp2[:], ALU.add),
                    deps=[w1, w2], sig=True)
        k_tm = p.sb("l1aa_ktm", [PD, NC, 256], BF16)
        v_tm = p.sb("l1aa_vtm", [PD, NC, 256], BF16)
        wkt = p.sb("l1aa_wkt", [PD, DC, 256], BF16)
        wvt = p.sb("l1aa_wvt", [PD, DC, 256], BF16)
        twk = p.dma("pool", wkt[:], W["WkT"].rearrange("(c k) m -> k c m", k=PD), ch="l1aawk")
        twv = p.dma("pool", wvt[:], W["WvT"].rearrange("(c k) m -> k c m", k=PD), ch="l1aawv")
        yst = [p.sb("l1aa_ys", [PD, DC, 512], BF16) for _ in range(2)]
        war = [None, None]
        tm_tok = None
        for t5 in range(NT5):
            b = t5 % 2
            tl = p.dma("sp", yst[b][:], Yr[:, :, t5 * 512:(t5 + 1) * 512],
                       deps=ytok + [war[b]], ch=f"l1aay{b}")
            for tc4 in range(4):
                tc = t5 * 4 + tc4
                for which in range(2):
                    wsb2 = wkt if which == 0 else wvt
                    dst = k_tm if which == 0 else v_tm
                    tok = None
                    for kc in range(DC):
                        tok = p.emit("pe", lambda E, kc=kc, b=b, tc4=tc4, which=which,
                                     st=(kc == 0):
                                     E.matmul(p.banks[2 + which][:, 0:256],
                                              yst[b][:, kc, tc4 * 128:(tc4 + 1) * 128],
                                              (wkt if which == 0 else wvt)[:, kc, :],
                                              start=st, stop=(kc == DC - 1)),
                                     deps=[twk, twv, tl, tm_tok], sig=(kc == DC - 1))
                    tm_tok = p.emit("dve", lambda E, tc=tc, which=which, dst=dst: E.tensor_copy(
                        dst[:, tc, :], p.banks[2 + which][:, 0:256]), deps=[tok], sig=True)
                war[b] = tm_tok
        ktr_tok = None
        for tc in range(NC):
            kv = k_tm[:, tc, :].rearrange("p (h c) -> p h c", c=HD)
            t1 = p.emit("dve", lambda E, kv=kv: E.tensor_scalar(
                rtmp[:, 0:256].rearrange("p (h c) -> p h c", c=HD)[:, :, 0:32],
                kv[:, :, 32:64], -1.0, None, ALU.mult), deps=[tm_tok, ktr_tok], sig=True)
            t2 = p.emit("dve", lambda E, kv=kv: E.tensor_copy(
                rtmp[:, 0:256].rearrange("p (h c) -> p h c", c=HD)[:, :, 32:64],
                kv[:, :, 0:32]), deps=[t1], sig=True)
            t3 = p.emit("dve", lambda E, tc=tc: E.tensor_tensor(
                rtmp[:, 0:256].rearrange("p (h c) -> p h c", c=HD)[:, :, :],
                rtmp[:, 0:256].rearrange("p (h c) -> p h c", c=HD)[:, :, :],
                sint[:, tc, None, :].broadcast_to([PD, 4, HD]) if False else
                sint[:, tc:tc + 1, :].broadcast_to([PD, 4, HD]), ALU.mult),
                deps=[t2, tst], sig=True)
            t4 = p.emit("dve", lambda E, kv=kv, tc=tc: E.tensor_tensor(
                rtmp2[:, 0:256].rearrange("p (h c) -> p h c", c=HD)[:, :, :], kv[:, :, :],
                cost[:, tc:tc + 1, :].broadcast_to([PD, 4, HD]), ALU.mult),
                deps=[tct], sig=True)
            ktr_tok = p.emit("dve", lambda E, kv=kv: E.tensor_tensor(
                kv[:, :, :], rtmp[:, 0:256].rearrange("p (h c) -> p h c", c=HD)[:, :, :],
                rtmp2[:, 0:256].rearrange("p (h c) -> p h c", c=HD)[:, :, :], ALU.add),
                deps=[t3, t4], sig=True)
        kvf = p.sb("l1aa_kvf", [PD, NH * 64], F32)
        kvb = p.sb("l1aa_kvb", [PD, NH * 64], BF16)
        Ptl = p.sb("l1aa_P", [PD, CH], BF16)
        qd_t = p.sb("l1aa_qdt", [PD, CH], BF16)
        kd_t = p.sb("l1aa_kdt", [PD, 64], BF16)
        ob = p.sb("l1aa_ob", [64, CH], BF16)
        kv0 = p.emit("dve", lambda E: E.memset(kvf[:], 0.0), sig=True)
        kv0 = p.emit("dve", lambda E: E.memset(kvb[:], 0.0), deps=[kv0], sig=True)
        kv_tok = [kv0] * NH
        slopes = [2.0 ** (-0.5 * (i + 1)) for i in range(NH)]
        ostore = None
        for ci in range(NC):
            for h in range(NH):
                kvh = h // 4
                krow = (kvh % 2) * 64
                kch = kvh // 2
                qch, qrow = ((h % 4) + 4 * (h // 8), 64 * ((h // 4) % 2))
                bkS = p.banks[4 + h % 2]
                smm = p.emit("pe", lambda E, ci=ci, krow=krow, kch=kch, qrow=qrow, qch=qch,
                             bkS=bkS:
                             E.matmul(bkS[:, 0:CH],
                                      k_fm[krow:krow + 64, kch, ci * CH:(ci + 1) * CH],
                                      q_fm[qrow:qrow + 64, qch, ci * CH:(ci + 1) * CH],
                                      start=True, stop=True),
                             deps=[kr_tok, rope_tok, ("dve", p.cnt["dve"])], sig=True)
                tp = p.emit("dve", lambda E, h=h, bkS=bkS: E.tensor_tensor(
                    Ptl[:], bkS[:, 0:CH], maskT[:, h, :], ALU.mult),
                    deps=[smm, tmk], sig=True)
                tqd2 = p.emit("dve", lambda E, h=h, ci=ci, qrow=qrow, qch=qch: E.tensor_tensor(
                    qd_t[qrow:qrow + 64, :], q_fm[qrow:qrow + 64, qch, ci * CH:(ci + 1) * CH],
                    qdec[qrow:qrow + 64, h, :], ALU.mult), deps=[tqd], sig=True)
                om1 = p.emit("pe", lambda E, ci=ci, kvh=kvh, h=h: E.matmul(
                    p.banks[h % 2][0:64, 0:CH], v_tm[:, ci, kvh * 64:(kvh + 1) * 64], Ptl[:],
                    start=True, stop=False), deps=[tp, tm_tok, ktr_tok], sig=False)
                om2 = p.emit("pe", lambda E, h=h, qrow=qrow: E.matmul(
                    p.banks[h % 2][0:64, 0:CH], kvb[qrow:qrow + 64, h * 64:(h + 1) * 64],
                    qd_t[qrow:qrow + 64, :], start=False, stop=True),
                    deps=[tqd2, kv_tok[h]], sig=True)
                oev = p.emit("dve", lambda E, h=h: E.tensor_copy(ob[:], p.banks[h % 2][0:64, 0:CH]),
                             deps=[om2, ostore], sig=True)
                ostore = p.dma("sp", Od[h * 64:(h + 1) * 64, ci * CH:(ci + 1) * CH], ob[:],
                               deps=[oev], ch="l1aaow")
                tkd2 = p.emit("dve", lambda E, h=h, ci=ci, kvh=kvh: E.tensor_scalar(
                    kd_t[:], k_tm[:, ci, kvh * 64:(kvh + 1) * 64], kdec[:, h:h + 1], None,
                    ALU.mult), deps=[tkd, ktr_tok], sig=True)
                dmm = p.emit("pe", lambda E, ci=ci, kvh=kvh, h=h: E.matmul(
                    p.banks[2 + h % 2][0:64, 0:64], kd_t[:], v_tm[:, ci, kvh * 64:(kvh + 1) * 64],
                    start=True, stop=True), deps=[tkd2, ("dve", p.cnt["dve"])], sig=True)
                up1 = p.emit("dve", lambda E, h=h, qrow=qrow, bd=float(slopes[h]):
                             E.scalar_tensor_tensor(kvf[qrow:qrow + 64, h * 64:(h + 1) * 64],
                                                    kvf[qrow:qrow + 64, h * 64:(h + 1) * 64],
                                                    float(np.exp(-bd * CH)),
                                                    p.banks[2 + h % 2][0:64, 0:64],
                                                    ALU.mult, ALU.add),
                             deps=[dmm], sig=True)
                kv_tok[h] = p.emit("dve", lambda E, h=h, qrow=qrow: E.tensor_copy(
                    kvb[qrow:qrow + 64, h * 64:(h + 1) * 64],
                    kvf[qrow:qrow + 64, h * 64:(h + 1) * 64]),
                    deps=[up1], sig=True)
        odone = [("l1aaow", p.cnt.get("l1aaow", 0)), ostore]
    with p.scope():
        delta = p.sb("l1aa_dl", [PD, DC, 512], F32)
        upd = []
        dtoks = {}
        def ev_o(j0, t0, w, bk_, tok):
            t1 = p.emit("act", lambda E, j0=j0, t0=t0, w=w, bk_=bk_: E.copy(
                delta[:, j0 // PD, 0:w], bk_[:, 0:w]), deps=[tok, dtoks.get("u")], sig=True)
            dtoks[j0 // PD] = t1
            if j0 // PD == DC - 1:
                sts, _ = emit_x_update(p, g, X, delta, t0, t0 + w, t1)
                dtoks.clear()
                dtoks["u"] = sts[-1] if False else ("dve", p.cnt["dve"])
                return sts[-1]
            return t1
        last = emit_proj_from_hbm(p, g, "l1aaO", W["WoT"], Od, D, ev_o, deps=odone,
                                  banks=(0, 1), tt=512)
    return [last]



def vec_chunks_ap(ap):
    flat = ap
    if len(ap.shape) == 3:
        flat = ap.rearrange("o a c -> (o a c)")
    return flat.rearrange("(c k) -> k c", k=PD)


def emit_l0aa(p, g, X, W, deps=()):
    ln = OuterLN(p, g, X, deps=deps)
    Yh = p.dram("l0aa_Y", [D, T], F32)
    Kd = p.dram("l0aa_K", [D, T], F32)
    Vd = p.dram("l0aa_V", [D, T], F32)
    Rd = p.dram("l0aa_R", [D, T], F32)
    Yr = Yh.rearrange("(c k) t -> k c t", k=PD)
    Xr = X.rearrange("(c k) t -> k c t", k=PD)
    bank7 = p.banks[7]
    with p.scope():
        vecs = {}
        vtoks = {}
        for nm in ["time_decay", "time_first", "ln1_g", "ln1_b", "ln2_g", "ln2_b",
                   "mix_k", "mix_v", "mix_r", "cmix_k", "cmix_r"]:
            vecs[nm] = p.sb("l0aa_" + nm, [PD, DC], F32)
            vtoks[nm] = p.dma("sp", vecs[nm][:], vec_chunks_ap(W[nm]), ch="v" + nm, slow=True)
        vready = list(vtoks.values())

        stats1 = p.sb("l0aa_st1", [PD, 64], F32)
        with p.scope():
            xt = [p.sb("l0aa_x", [PD, DC, 512], F32) for _ in range(2)]
            sB = [p.sb("l0aa_s", [PD, 512], F32) for _ in range(2)]
            bB = [p.sb("l0aa_b", [PD, 512], F32) for _ in range(2)]
            yt = [p.sb("l0aa_yt", [PD, DC, 512], F32) for _ in range(2)]
            sq = [p.sb("l0aa_sq", [PD, DC, 512], F32) for _ in range(2)]
            war = [None, None]
            ystore = [None, None]
            evac_tok = None
            st1 = None
            for tt in range(NT5):
                b = tt % 2
                dl = list(deps) + [war[b]] + ln.ready
                tl = p.dma("sp", xt[b][:], Xr[:, :, tt * 512:(tt + 1) * 512], deps=dl, ch=f"lax{b}")
                tsb = p.dma("sp", sB[b][:], row_bcast_ap(ln.srow_d, tt * 512, 512), deps=dl,
                            ch=f"las{b}")
                tbb = p.dma("sp", bB[b][:], row_bcast_ap(ln.brow_d, tt * 512, 512), deps=dl,
                            ch=f"lab{b}")
                sq_toks = []
                ylast = None
                for dc in range(DC):
                    tp = p.emit("pool", lambda E, a=xt[b], o=yt[b], s_=sB[b], dc=dc:
                                E.tensor_mul(o[:, dc], a[:, dc], s_[:]),
                                deps=[tl, tsb, war[b], ystore[b]], sig=True)
                    ylast = p.emit("dve", lambda E, o=yt[b], b_=bB[b], dc=dc:
                                   E.tensor_tensor(o[:, dc], o[:, dc], b_[:], ALU.add),
                                   deps=[tp, tbb], sig=True)
                    sq_toks.append(p.emit("pool", lambda E, a=yt[b], o=sq[b], dc=dc:
                                          E.tensor_mul(o[:, dc], a[:, dc], a[:, dc]),
                                          deps=[ylast], sig=True))
                ystore[b] = p.dma("sp", Yr[:, :, tt * 512:(tt + 1) * 512], yt[b][:],
                                  deps=[ylast], ch=f"law{b}")
                pe_last = None
                for sub in range(4):
                    for cc2 in range(2):
                        src = yt[b] if cc2 == 0 else sq[b]
                        dtk = [ylast] if cc2 == 0 else sq_toks
                        for dc in range(DC):
                            pe_last = p.emit("pe", lambda E, a=src, dc=dc, sub=sub, tt=tt,
                                             cc2=cc2, st=(dc == 0):
                                             E.matmul(bank7[:, tt * 8 + sub * 2 + cc2:
                                                            tt * 8 + sub * 2 + cc2 + 1],
                                                      a[:, dc, sub * 128:(sub + 1) * 128],
                                                      g["ones_f"][:], start=st,
                                                      stop=(dc == DC - 1)),
                                             deps=[dtk[dc if cc2 else 0] if cc2 else ylast,
                                                   evac_tok] + g["ready"],
                                             sig=(sub == 3 and cc2 == 1 and dc == DC - 1))
                war[b] = pe_last
                st1 = p.emit("act", lambda E, tt=tt: E.copy(stats1[:, tt * 8:(tt + 1) * 8],
                                                            bank7[:, tt * 8:(tt + 1) * 8]),
                             deps=[pe_last], sig=True)
                evac_tok = st1
        ydone = [("law0", p.cnt.get("law0", 0)), ("law1", p.cnt.get("law1", 0))]
        s1row_d = p.dram("l0aa_s1r", [32, PD], F32)
        b1row_d = p.dram("l0aa_b1r", [32, PD], F32)
        s1_tok, b1_tok = emit_ln_smalls(p, g, stats1, s1row_d, b1row_d, st1, kind="ln")

        with p.scope():
            wtk = p.sb("l0aa_wtk", [PD, DC, D], BF16)
            wtv = p.sb("l0aa_wtv", [PD, DC, D], BF16)
            wtr = p.sb("l0aa_wtr", [PD, DC, D], BF16)
            t_k = p.dma("pool", wtk[:], W["w_tk"].rearrange("(c k) m -> k c m", k=PD), ch="lawk")
            t_v = p.dma("pool", wtv[:], W["w_tv"].rearrange("(c k) m -> k c m", k=PD), ch="lawv")
            t_r = p.dma("pool", wtr[:], W["w_tr"].rearrange("(c k) m -> k c m", k=PD), ch="lawr")
            ytl = [p.sb("la2_y", [PD, DC, 512], F32) for _ in range(2)]
            sB = [p.sb("la2_s", [PD, 512], F32) for _ in range(2)]
            bB = [p.sb("la2_b", [PD, 512], F32) for _ in range(2)]
            ht = [p.sb("la2_h", [PD, DC, 513], F32) for _ in range(2)]
            xk = p.sb("la2_xk", [PD, DC, 512], BF16)
            xv = p.sb("la2_xv", [PD, DC, 512], BF16)
            xr = p.sb("la2_xr", [PD, DC, 512], BF16)
            dfs = p.sb("la2_df", [PD, DC, 512], F32)
            evb = [p.sb("la2_ev", [PD, 512], F32) for _ in range(2)]
            war = [None, None]
            hprev = [None, None]
            mix_tok = None
            carry_tok = None
            store_toks = []
            bank_war = {}
            est = {}
            for tc in range(NT5):
                b = tc % 2
                dl = [s1_tok, b1_tok, war[b]] + ydone + vready
                tl = p.dma("sp", ytl[b][:], Yr[:, :, tc * 512:(tc + 1) * 512], deps=dl,
                           ch=f"la2y{b}")
                tsb = p.dma("sp", sB[b][:], row_bcast_ap(s1row_d, tc * 512, 512), deps=dl,
                            ch=f"la2s{b}")
                tbb = p.dma("sp", bB[b][:], row_bcast_ap(b1row_d, tc * 512, 512), deps=dl,
                            ch=f"la2b{b}")
                h_toks = []
                for dc in range(DC):
                    t1 = p.emit("dve", lambda E, a=ytl[b], o=ht[b], s_=sB[b], dc=dc, vecs=vecs:
                                E.scalar_tensor_tensor(o[:, dc, 1:513], a[:, dc],
                                                       vecs["ln1_g"][:, dc:dc + 1], s_[:],
                                                       ALU.mult, ALU.mult),
                                deps=[tl, tsb, war[b], hprev[b], mix_tok], sig=True)
                    t2 = p.emit("dve", lambda E, o=ht[b], b_=bB[b], dc=dc, vecs=vecs:
                                E.scalar_tensor_tensor(o[:, dc, 1:513], b_[:],
                                                       vecs["ln1_g"][:, dc:dc + 1],
                                                       o[:, dc, 1:513], ALU.mult, ALU.add),
                                deps=[tbb], sig=True)
                    t3 = p.emit("dve", lambda E, o=ht[b], dc=dc, vecs=vecs:
                                E.tensor_scalar(o[:, dc, 1:513], o[:, dc, 1:513],
                                                vecs["ln1_b"][:, dc:dc + 1], None, ALU.add),
                                sig=True)
                    h_toks.append(t3)
                if tc == 0:
                    carry_tok = p.emit("dve", lambda E, o=ht[b]: E.memset(o[:, :, 0:1], 0.0),
                                       deps=[hprev[b]], sig=True)
                else:
                    carry_tok = p.emit("dve", lambda E, o=ht[b], pv=ht[1 - b]:
                                       E.tensor_copy(o[:, :, 0:1], pv[:, :, 512:513]),
                                       deps=[h_toks[-1], hprev[1 - b]], sig=True)
                mix_last = None
                for dc in range(DC):
                    td = p.emit("dve", lambda E, o=ht[b], dc=dc, dfs=dfs:
                                E.tensor_tensor(dfs[:, dc], o[:, dc, 1:513], o[:, dc, 0:512],
                                                ALU.subtract),
                                deps=[h_toks[dc], carry_tok], sig=True)
                    for nm, dst in [("mix_k", xk), ("mix_v", xv), ("mix_r", xr)]:
                        mix_last = p.emit("dve", lambda E, o=ht[b], dc=dc, nm=nm, dst=dst, dfs=dfs, vecs=vecs:
                                          E.scalar_tensor_tensor(dst[:, dc], dfs[:, dc],
                                                                 vecs[nm][:, dc:dc + 1],
                                                                 o[:, dc, 0:512],
                                                                 ALU.mult, ALU.add),
                                          deps=[td], sig=True)
                mix_tok = mix_last
                pi = 0
                for which, (src, wsb2, dst, tw) in enumerate(
                        [(xk, wtk, Kd, t_k), (xv, wtv, Vd, t_v), (xr, wtr, Rd, t_r)]):
                    src, wsb2, dst, tw = [(xk, wtk, Kd, t_k), (xv, wtv, Vd, t_v),
                                          (xr, wtr, Rd, t_r)][which]
                    for jc in range(DC):
                        bkp = p.banks[pi % 2]
                        bw = bank_war.get(pi % 2)
                        pi += 1
                        tok = None
                        for kc in range(DC):
                            tok = p.emit("pe", lambda E, kc=kc, jc=jc, src=src, wsb2=wsb2,
                                         bkp=bkp, st=(kc == 0):
                                         E.matmul(bkp[:, 0:512],
                                                  wsb2[:, kc, jc * PD:(jc + 1) * PD],
                                                  src[:, kc, :], start=st, stop=(kc == DC - 1)),
                                         deps=[tw, mix_tok, bw], sig=(kc == DC - 1))
                        b2 = pi % 2
                        if which == 2:
                            tev = p.emit("act", lambda E, bkp=bkp, b2=b2: E.activation(
                                evb[b2][:], bkp[:, 0:512], AF.Sigmoid),
                                deps=[tok, est.get(b2)], sig=True)
                        else:
                            tev = p.emit("act", lambda E, bkp=bkp, b2=b2: E.copy(
                                evb[b2][:], bkp[:, 0:512]), deps=[tok, est.get(b2)], sig=True)
                        bank_war[(pi - 1) % 2] = tev
                        tst = p.dma("sp", dst[jc * PD:(jc + 1) * PD, tc * 512:(tc + 1) * 512],
                                    evb[b2][:], deps=[tev], ch=f"lakv{b2}")
                        est[b2] = tst
                        store_toks.append(tst)
                war[b] = mix_tok
                hprev[b] = mix_tok
            kvr_done = [("lakv0", p.cnt.get("lakv0", 0)), ("lakv1", p.cnt.get("lakv1", 0))]

        with p.scope():
            kt = [p.sb("lb_k", [PD, 512], F32) for _ in range(2)]
            vt = [p.sb("lb_v", [PD, 512], F32) for _ in range(2)]
            rt = [p.sb("lb_r", [PD, 512], F32) for _ in range(2)]
            ybt = [p.sb("lb_y", [PD, 512], F32) for _ in range(2)]
            ekm = p.sb("lb_ekm", [PD, 512], F32)
            euk = p.sb("lb_euk", [PD, 512], F32)
            ewk = p.sb("lb_ewk", [PD, 512], F32)
            gt = p.sb("lb_g", [PD, 512], F32)
            gkv = p.sb("lb_gkv", [PD, 512], F32)
            num = p.sb("lb_num", [PD, 512], F32)
            den = p.sb("lb_den", [PD, 512], F32)
            Dbuf = p.sb("lb_D", [PD, 513], F32)
            Mbuf = p.sb("lb_M", [PD, 513], F32)
            sm = p.sb("lb_sm", [PD, 8], F32)
            war = [None, None]
            ystw = [None, None]
            for cc in range(DC):
                i0 = p.emit("dve", lambda E: E.memset(sm[:], 0.0), sig=True)
                i1 = p.emit("dve", lambda E, cc=cc, vecs=vecs: E.tensor_copy(sm[:, 2:3],
                                                         vecs["time_first"][:, cc:cc + 1]),
                            deps=[i0] + vready, sig=True)
                i2 = p.emit("act", lambda E, cc=cc, vecs=vecs: E.activation(sm[:, 3:4],
                                                        vecs["time_decay"][:, cc:cc + 1],
                                                        AF.Exp), deps=[i1], sig=True)
                i3 = p.emit("dve", lambda E: E.memset(Dbuf[:, 0:1], 0.0), sig=True)
                i4 = p.emit("dve", lambda E: E.memset(Mbuf[:, 0:1], 0.0), sig=True)
                scan_tok = i4
                for tc in range(NT5):
                    b = tc % 2
                    dl = kvr_done + [war[b]]
                    tkl = p.dma("sp", kt[b][:], Kd[cc * PD:(cc + 1) * PD, tc * 512:(tc + 1) * 512],
                                deps=dl, ch=f"lbk{b}")
                    tvl = p.dma("sp", vt[b][:], Vd[cc * PD:(cc + 1) * PD, tc * 512:(tc + 1) * 512],
                                deps=dl, ch=f"lbv{b}")
                    trl = p.dma("sp", rt[b][:], Rd[cc * PD:(cc + 1) * PD, tc * 512:(tc + 1) * 512],
                                deps=dl, ch=f"lbr{b}")
                    tyl = p.dma("sp", ybt[b][:], Yr[:, cc, tc * 512:(tc + 1) * 512],
                                deps=[war[b], ystw[b]] + ydone, ch=f"lby{b}")
                    e1 = p.emit("act", lambda E, b=b: E.activation(ekm[:], kt[b][:], AF.Exp,
                                                                   bias=sm[:, 1:2]),
                                deps=[tkl, i2, scan_tok], sig=True)
                    e2 = p.emit("act", lambda E, b=b: E.activation(euk[:], kt[b][:], AF.Exp,
                                                                   bias=sm[:, 2:3]),
                                deps=[tkl], sig=True)
                    e3 = p.emit("dve", lambda E: E.tensor_scalar(ewk[:], ekm[:], sm[:, 3:4],
                                                                 None, ALU.mult),
                                deps=[e1, i2], sig=True)
                    s1 = p.emit("dve", lambda E: E.tensor_tensor_scan(
                        Dbuf[:, 1:513], sm[:, 3:4].broadcast_to([PD, 512]), ewk[:],
                        Dbuf[:, 0:1], ALU.mult, ALU.add), deps=[e3], sig=True)
                    g1 = p.emit("dve", lambda E: E.tensor_tensor(gt[:], Dbuf[:, 0:512], ekm[:],
                                                                 ALU.max), deps=[s1], sig=True)
                    g2 = p.emit("dve", lambda E: E.reciprocal(gt[:], gt[:]), sig=True)
                    g3 = p.emit("dve", lambda E: E.tensor_tensor(gt[:], gt[:], Dbuf[:, 1:513],
                                                                 ALU.mult), sig=True)
                    g4 = p.emit("dve", lambda E: E.tensor_tensor(gkv[:], gt[:], ekm[:], ALU.mult),
                                sig=True)
                    g5 = p.emit("dve", lambda E, b=b: E.tensor_tensor(gkv[:], gkv[:], vt[b][:],
                                                                      ALU.mult),
                                deps=[tvl], sig=True)
                    s2 = p.emit("dve", lambda E: E.tensor_tensor_scan(
                        Mbuf[:, 1:513], gt[:], gkv[:], Mbuf[:, 0:1], ALU.mult, ALU.add),
                        deps=[g5], sig=True)
                    y1 = p.emit("dve", lambda E, b=b: E.tensor_tensor(num[:], euk[:], vt[b][:],
                                                                      ALU.mult),
                                deps=[e2, tvl], sig=True)
                    y2 = p.emit("dve", lambda E: E.tensor_tensor(num[:], num[:], Mbuf[:, 0:512],
                                                                 ALU.add), deps=[s2], sig=True)
                    y3 = p.emit("dve", lambda E: E.tensor_tensor(den[:], Dbuf[:, 0:512], euk[:],
                                                                 ALU.add), sig=True)
                    y4 = p.emit("dve", lambda E: E.reciprocal(den[:], den[:]), sig=True)
                    y5 = p.emit("dve", lambda E: E.tensor_tensor(num[:], num[:], den[:],
                                                                 ALU.mult), sig=True)
                    u1 = p.emit("dve", lambda E, b=b: E.tensor_tensor(num[:], num[:], rt[b][:],
                                                                      ALU.mult),
                                deps=[trl], sig=True)
                    u2 = p.emit("dve", lambda E, b=b: E.tensor_tensor(ybt[b][:], ybt[b][:],
                                                                      num[:], ALU.add),
                                deps=[tyl], sig=True)
                    ystw[b] = p.dma("sp", Yr[:, cc, tc * 512:(tc + 1) * 512], ybt[b][:],
                                    deps=[u2], ch=f"lbyw{b}")
                    war[b] = u2
                    r1 = p.emit("act", lambda E: E.activation(sm[:, 4:5], Dbuf[:, 512:513],
                                                              AF.Ln), deps=[s2], sig=True)
                    r2 = p.emit("dve", lambda E: E.tensor_tensor(sm[:, 0:1], sm[:, 0:1],
                                                                 sm[:, 4:5], ALU.add),
                                deps=[r1], sig=True)
                    r3 = p.emit("dve", lambda E: E.tensor_scalar(sm[:, 1:2], sm[:, 0:1], -1.0,
                                                                 None, ALU.mult), sig=True)
                    r4 = p.emit("dve", lambda E, cc=cc, vecs=vecs: E.tensor_tensor(
                        sm[:, 2:3], vecs["time_first"][:, cc:cc + 1], sm[:, 0:1], ALU.subtract),
                        sig=True)
                    r5 = p.emit("dve", lambda E: E.reciprocal(sm[:, 5:6], Dbuf[:, 512:513]),
                                sig=True)
                    r6 = p.emit("dve", lambda E: E.tensor_tensor(Mbuf[:, 0:1], Mbuf[:, 512:513],
                                                                 sm[:, 5:6], ALU.mult), sig=True)
                    scan_tok = p.emit("dve", lambda E: E.memset(Dbuf[:, 0:1], 1.0), sig=True)
            yw_done = [("lbyw0", p.cnt.get("lbyw0", 0)), ("lbyw1", p.cnt.get("lbyw1", 0))]

        stats2 = p.sb("l0aa_st2", [PD, 64], F32)
        st2 = emit_ln_stats(p, g, Yh, stats2, deps=yw_done)
        s2row_d = p.dram("l0aa_s2r", [32, PD], F32)
        b2row_d = p.dram("l0aa_b2r", [32, PD], F32)
        s2_tok, b2_tok = emit_ln_smalls(p, g, stats2, s2row_d, b2row_d, st2, kind="ln")

        with p.scope():
            wcv_all = p.sb("lc_wcv", [PD, 32, D], BF16)
            tcv = p.dma("pool", wcv_all[:], W["w_cv"].rearrange("(c k) m -> k c m", k=PD),
                        ch="lcwcv")
            wcr = p.sb("lc_wcr", [PD, DC, D], BF16)
            tcr = p.dma("pool", wcr[:], W["w_cr"].rearrange("(c k) m -> k c m", k=PD), ch="lcwcr")
            wckb = [p.sb("lc_wck", [PD, DC, 512], BF16)]
            war_ck = [None]
            yct = p.sb("lc_y", [PD, DC, 512], F32)
            sB = p.sb("lc_s", [PD, 512], F32)
            bB = p.sb("lc_b", [PD, 512], F32)
            ht = p.sb("lc_h", [PD, DC, 513], F32)
            carryT = p.sb("lc_carry", [PD, DC, 1], F32)
            ck = p.sb("lc_ck", [PD, DC, 512], BF16)
            cr = p.sb("lc_cr", [PD, DC, 512], BF16)
            dfc = p.sb("lc_df", [PD, 512], F32)
            kst = p.sb("lc_kst", [PD, 32, 512], BF16)
            scr = p.sb("lc_scr", [PD, DC, 512], BF16)
            rl = p.sb("lc_rl", [PD, 512], F32)
            xt2 = p.sb("lc_x", [PD, DC, 512], F32)
            yload = None
            xstore = None
            mix_tok = None
            kst_war = None
            scr_war = None
            upd = []
            for tc in range(NT5):
                dl = [s2_tok, b2_tok, yload] + yw_done
                tl = p.dma("sp", yct[:], Yr[:, :, tc * 512:(tc + 1) * 512], deps=dl, ch="lcy")
                tsb = p.dma("sp", sB[:], row_bcast_ap(s2row_d, tc * 512, 512), deps=dl, ch="lcs")
                tbb = p.dma("sp", bB[:], row_bcast_ap(b2row_d, tc * 512, 512), deps=dl, ch="lcb")
                if tc == 0:
                    carry_tok = p.emit("dve", lambda E: E.memset(carryT[:], 0.0), sig=True)
                else:
                    carry_tok = p.emit("dve", lambda E, ht=ht: E.tensor_copy(carryT[:], ht[:, :, 512:513]),
                                       deps=[mix_tok], sig=True)
                h_toks = []
                for dc in range(DC):
                    t1 = p.emit("dve", lambda E, o=ht, dc=dc, vecs=vecs, yct=yct, sB=sB:
                                E.scalar_tensor_tensor(o[:, dc, 1:513], yct[:, dc],
                                                       vecs["ln2_g"][:, dc:dc + 1], sB[:],
                                                       ALU.mult, ALU.mult),
                                deps=[tl, tsb, carry_tok, mix_tok], sig=True)
                    t2 = p.emit("dve", lambda E, o=ht, dc=dc, vecs=vecs, bB=bB:
                                E.scalar_tensor_tensor(o[:, dc, 1:513], bB[:],
                                                       vecs["ln2_g"][:, dc:dc + 1],
                                                       o[:, dc, 1:513], ALU.mult, ALU.add),
                                deps=[tbb], sig=True)
                    t3 = p.emit("dve", lambda E, o=ht, dc=dc, vecs=vecs:
                                E.tensor_scalar(o[:, dc, 1:513], o[:, dc, 1:513],
                                                vecs["ln2_b"][:, dc:dc + 1], None, ALU.add),
                                sig=True)
                    h_toks.append(t3)
                ccar = p.emit("dve", lambda E, ht=ht: E.tensor_copy(ht[:, :, 0:1], carryT[:]),
                              deps=[h_toks[-1]], sig=True)
                mix_last = None
                for dc in range(DC):
                    td = p.emit("dve", lambda E, o=ht, dc=dc, dfc=dfc:
                                E.tensor_tensor(dfc[:], o[:, dc, 1:513], o[:, dc, 0:512],
                                                ALU.subtract),
                                deps=[h_toks[dc], ccar], sig=True)
                    for nm, dst in [("cmix_k", ck), ("cmix_r", cr)]:
                        mix_last = p.emit("dve", lambda E, o=ht, dc=dc, nm=nm, dst=dst,
                                          dfc=dfc, vecs=vecs:
                                          E.scalar_tensor_tensor(dst[:, dc], dfc[:],
                                                                 vecs[nm][:, dc:dc + 1],
                                                                 o[:, dc, 0:512],
                                                                 ALU.mult, ALU.add),
                                          deps=[td], sig=True)
                mix_tok = mix_last
                yload = mix_tok
                pi = 0
                bank_war2 = {}
                for jc in range(DC):
                    bkp = p.banks[pi % 2]
                    bw = bank_war2.get(pi % 2)
                    pi += 1
                    tok = None
                    for kc in range(DC):
                        tok = p.emit("pe", lambda E, kc=kc, jc=jc, bkp=bkp, st=(kc == 0):
                                     E.matmul(bkp[:, 0:512], wcr[:, kc, jc * PD:(jc + 1) * PD],
                                              cr[:, kc, :], start=st, stop=(kc == DC - 1)),
                                     deps=[tcr, mix_tok, bw, scr_war], sig=(kc == DC - 1))
                    scr_tok = p.emit("act", lambda E, jc=jc, bkp=bkp: E.activation(
                        scr[:, jc, :], bkp[:, 0:512], AF.Sigmoid), deps=[tok], sig=True)
                    bank_war2[(pi - 1) % 2] = scr_tok
                scr_war = scr_tok
                kst_tok = None
                for ffb in range(8):
                    wb = 0
                    twc = p.dma("pool", wckb[wb][:],
                                W["w_ck"][:, ffb * 512:(ffb + 1) * 512]
                                .rearrange("(c k) m -> k c m", k=PD),
                                deps=[war_ck[wb]], ch=f"lcck{wb}")
                    for ffc in range(4):
                        bkp = p.banks[pi % 2]
                        bw = bank_war2.get(pi % 2)
                        pi += 1
                        tok = None
                        for kc in range(DC):
                            tok = p.emit("pe", lambda E, kc=kc, ffc=ffc, wb=wb, bkp=bkp,
                                         st=(kc == 0):
                                         E.matmul(bkp[:, 0:512],
                                                  wckb[wb][:, kc, ffc * 128:(ffc + 1) * 128],
                                                  ck[:, kc, :], start=st, stop=(kc == DC - 1)),
                                         deps=[twc, mix_tok, bw, kst_war], sig=(kc == DC - 1))
                        tr1 = p.emit("dve", lambda E, bkp=bkp: E.tensor_scalar_max(
                            rl[:], bkp[:, 0:512], 0.0), deps=[tok, kst_tok], sig=True)
                        kst_tok = p.emit("act", lambda E, ffb=ffb, ffc=ffc: E.activation(
                            kst[:, ffb * 4 + ffc, :], rl[:], AF.Square), deps=[tr1], sig=True)
                        bank_war2[(pi - 1) % 2] = tr1
                    war_ck[wb] = kst_tok
                kst_war = kst_tok
                tlx = p.dma("sp", xt2[:], Xr[:, :, tc * 512:(tc + 1) * 512],
                            deps=[xstore, kst_tok], ch="lcx")
                add_last = None
                for dc2 in range(DC):
                    bko = p.banks[2 + dc2 // 2]
                    half = (dc2 % 2) * 256
                    for hf in range(2):
                        mm_last = None
                        for ffi in range(32):
                            mm_last = p.emit("pe", lambda E, ffi=ffi, dc2=dc2, bko=bko,
                                             half=half, hf=hf, st=(ffi == 0):
                                             E.matmul(bko[:, half:half + 256],
                                                      wcv_all[:, ffi, dc2 * PD:(dc2 + 1) * PD],
                                                      kst[:, ffi, hf * 256:(hf + 1) * 256],
                                                      start=st, stop=(ffi == 31)),
                                             deps=[kst_tok, tcv, add_last],
                                             sig=(ffi == 31))
                        e1 = p.emit("dve", lambda E, dc2=dc2, bko=bko, half=half, hf=hf:
                                    E.tensor_tensor(rl[:, 0:256], bko[:, half:half + 256],
                                                    scr[:, dc2, hf * 256:(hf + 1) * 256],
                                                    ALU.mult),
                                    deps=[mm_last, tlx], sig=True)
                        e2 = p.emit("dve", lambda E, dc2=dc2, hf=hf:
                                    E.tensor_tensor(xt2[:, dc2, hf * 256:(hf + 1) * 256],
                                                    xt2[:, dc2, hf * 256:(hf + 1) * 256],
                                                    yct[:, dc2, hf * 256:(hf + 1) * 256],
                                                    ALU.add), sig=True)
                        add_last = p.emit("dve", lambda E, dc2=dc2, hf=hf:
                                          E.tensor_tensor(xt2[:, dc2, hf * 256:(hf + 1) * 256],
                                                          xt2[:, dc2, hf * 256:(hf + 1) * 256],
                                                          rl[:, 0:256], ALU.add), sig=True)
                xstore = p.dma("sp", Xr[:, :, tc * 512:(tc + 1) * 512], xt2[:],
                               deps=[add_last], ch="lcxw")
                upd = [xstore]
    return [("lcxw", p.cnt.get("lcxw", 0))]



def _std_prog(wspec, consts_fn, emitfn):
    p = Prog()
    c = Consts(p)
    p.alloc_banks()
    X = p.nc.dram_tensor("Xio", [D, T], F32, kind="ExternalInput").ap()
    Xout = p.nc.dram_tensor("Xout", [D, T], F32, kind="ExternalOutput").ap()
    W = {}
    for nm, shp in wspec:
        W[nm] = p.nc.dram_tensor(nm, list(shp), F32, kind="ExternalInput").ap()
    if consts_fn is not None:
        consts_fn(c, W)
    g = load_globals(p, c)
    Xs = p.dram("Xs", [D, T], F32)
    cp = []
    for dc in range(DC):
        cp.append(p.dma("sp", Xs[dc * PD:(dc + 1) * PD, :], X[dc * PD:(dc + 1) * PD, :], ch="st"))
    cp = cp[-1:]
    fin = emitfn(p, g, Xs, W, deps=cp)
    out_toks = []
    for dc in range(DC):
        out_toks.append(p.dma("sp", Xout[dc * PD:(dc + 1) * PD, :], Xs[dc * PD:(dc + 1) * PD, :],
                              deps=list(fin), ch="out"))
    p.emit("sp", lambda E: E.nop(), deps=out_toks[-1:], sig=True)
    nc = p.finalize()
    return p, c, nc


def _l1aa_consts(c, W):
    NH, HD, CH = 16, 64, 128
    inv = 1.0 / 10000.0 ** (np.arange(0, HD, 2, dtype=np.float64) / HD)
    ang = np.outer(np.arange(T, dtype=np.float64), inv)
    emb = np.concatenate([ang, ang], axis=1)
    cos = np.cos(emb).astype(np.float32)
    sin = np.sin(emb).astype(np.float32)
    import ml_dtypes
    cos_fm = np.tile(cos.T, (2, 1)).astype(ml_dtypes.bfloat16)
    sin_fm = np.tile(sin.T, (2, 1)).astype(ml_dtypes.bfloat16)
    cos_tm = cos.reshape(T // PD, PD, HD).transpose(1, 0, 2).astype(ml_dtypes.bfloat16)
    sin_tm = sin.reshape(T // PD, PD, HD).transpose(1, 0, 2).astype(ml_dtypes.bfloat16)
    perm = np.zeros((HD, HD), np.float32)
    for i in range(32):
        perm[i, 32 + i] = -1.0
        perm[32 + i, i] = 1.0
    permT = np.kron(np.eye(2, dtype=np.float32), perm.T).astype(ml_dtypes.bfloat16)
    sl = np.array([2.0 ** (-0.5 * (i + 1)) for i in range(NH)], np.float64)
    t_ = np.arange(CH, dtype=np.float64)
    m = np.where(t_[:, None] >= t_[None, :], np.exp(-sl[:, None, None] * (t_[:, None] - t_[None, :])), 0.0)
    maskT = m.transpose(2, 0, 1).astype(np.float32)
    maskT = np.ascontiguousarray(maskT.transpose(0, 1, 2)).astype(ml_dtypes.bfloat16)
    qdec = np.exp(-sl[:, None] * (t_ + 1.0))
    qdec = np.broadcast_to(qdec[None, :, :], (128, NH, CH)).astype(ml_dtypes.bfloat16)
    kdec = np.exp(-sl[None, :] * (CH - 1.0 - t_[:, None])).astype(np.float32)
    W["c_permT"] = c.add("c_permT", np.ascontiguousarray(permT))
    W["c_cos_fm"] = c.add("c_cos_fm", np.ascontiguousarray(cos_fm))
    W["c_sin_fm"] = c.add("c_sin_fm", np.ascontiguousarray(sin_fm))
    W["c_cos_tm"] = c.add("c_cos_tm", np.ascontiguousarray(cos_tm))
    W["c_sin_tm"] = c.add("c_sin_tm", np.ascontiguousarray(sin_tm))
    W["c_maskT"] = c.add("c_maskT", np.ascontiguousarray(maskT))
    W["c_qdec"] = c.add("c_qdec", np.ascontiguousarray(qdec))
    W["c_kdec"] = c.add("c_kdec", np.ascontiguousarray(kdec))


def _l0ab_consts(c, W):
    import ml_dtypes
    W["c_iota128"] = c.add("c_iota128", np.arange(PD, dtype=np.float32).reshape(PD, 1))
    tri = (np.arange(PD)[None, :] >= np.arange(PD)[:, None]).astype(ml_dtypes.bfloat16)
    W["c_trimask"] = c.add("c_trimask", np.ascontiguousarray(tri))


SUBS = {
    "l0aa": dict(
        wspec=[("time_decay", [D]), ("time_first", [D]), ("w_tk", [D, D]),
               ("w_tv", [D, D]), ("w_tr", [D, D]), ("w_ck", [D, 4 * D]),
               ("w_cv", [4 * D, D]), ("w_cr", [D, D]), ("ln1_g", [D]), ("ln1_b", [D]),
               ("ln2_g", [D]), ("ln2_b", [D]), ("mix_k", [1, 1, D]), ("mix_v", [1, 1, D]),
               ("mix_r", [1, 1, D]), ("cmix_k", [1, 1, D]), ("cmix_r", [1, 1, D])],
        consts=None, emit=None,
        prep=lambda ins: {
            "time_decay": ins["L0_attn_a_time_decay"], "time_first": ins["L0_attn_a_time_first"],
            "w_tk": ins["L0_attn_a_w_tk"], "w_tv": ins["L0_attn_a_w_tv"],
            "w_tr": ins["L0_attn_a_w_tr"], "w_ck": ins["L0_attn_a_w_ck"],
            "w_cv": ins["L0_attn_a_w_cv"], "w_cr": ins["L0_attn_a_w_cr"],
            "ln1_g": ins["L0_attn_a_ln1_g"], "ln1_b": ins["L0_attn_a_ln1_b"],
            "ln2_g": ins["L0_attn_a_ln2_g"], "ln2_b": ins["L0_attn_a_ln2_b"],
            "mix_k": ins["L0_attn_a_mix_k"], "mix_v": ins["L0_attn_a_mix_v"],
            "mix_r": ins["L0_attn_a_mix_r"], "cmix_k": ins["L0_attn_a_cmix_k"],
            "cmix_r": ins["L0_attn_a_cmix_r"]}),
    "l0ab": dict(
        wspec=[("W_qk", [D, D]), ("WvT", [D, D]), ("alibi_slope", [1])],
        consts=_l0ab_consts, emit=None,
        prep=lambda ins: {
            "W_qk": ins["L0_attn_b_W_qk"],
            "WvT": np.ascontiguousarray(ins["L0_attn_b_W_v"].T),
            "alibi_slope": ins["L0_attn_b_alibi_slope"]}),
    "l1aa": dict(
        wspec=[("WqT", [D, D]), ("WkT", [D, 256]), ("WvT", [D, 256]), ("WoT", [D, D])],
        consts=_l1aa_consts, emit=None,
        prep=lambda ins: {
            "WqT": np.ascontiguousarray(ins["L1_attn_a_Wq"].T),
            "WkT": np.ascontiguousarray(ins["L1_attn_a_Wk"].T),
            "WvT": np.ascontiguousarray(ins["L1_attn_a_Wv"].T),
            "WoT": np.ascontiguousarray(ins["L1_attn_a_Wo"].T)}),
    "l1ab": dict(
        wspec=[("WqT", [D, D]), ("WkT", [D, D]), ("WvT", [D, D]), ("WpT", [D, D]),
               ("bq", [D]), ("bk", [D]), ("bv", [D]), ("bp", [D]),
               ("lambda_q1", [8, 1, 64]), ("lambda_k1", [8, 1, 64]),
               ("lambda_q2", [8, 1, 64]), ("lambda_k2", [8, 1, 64])],
        consts=None, emit=None,
        prep=lambda ins: {
            "WqT": np.ascontiguousarray(ins["L1_attn_b_Wq"].T),
            "WkT": np.ascontiguousarray(ins["L1_attn_b_Wk"].T),
            "WvT": np.ascontiguousarray(ins["L1_attn_b_Wv"].T),
            "WpT": np.ascontiguousarray(ins["L1_attn_b_Wp"].T),
            "bq": ins["L1_attn_b_bq"], "bk": ins["L1_attn_b_bk"],
            "bv": ins["L1_attn_b_bv"], "bp": ins["L1_attn_b_bp"],
            "lambda_q1": ins["L1_attn_b_lambda_q1"], "lambda_k1": ins["L1_attn_b_lambda_k1"],
            "lambda_q2": ins["L1_attn_b_lambda_q2"], "lambda_k2": ins["L1_attn_b_lambda_k2"]}),
}


def build_sub_prog(name):
    spec = SUBS[name]
    emit = {"l0aa": emit_l0aa, "l0ab": emit_l0ab, "l1aa": emit_l1aa, "l1ab": emit_l1ab}[name]
    return _std_prog(spec["wspec"], spec["consts"], emit)


def prep_l1ffn_weights(ins):
    return {"w_gate": ins["L1_ffn_w_gate"], "w1": ins["L1_ffn_w1"],
            "w3": ins["L1_ffn_w3"], "w2": ins["L1_ffn_w2"]}


_PROG_CACHE = {}


def _get_prog(name):
    if name not in _PROG_CACHE:
        if name == "l1ffn":
            _PROG_CACHE[name] = build_l1ffn_prog()
        elif name == "l0ffn":
            _PROG_CACHE[name] = build_l0ffn_prog()
        else:
            _PROG_CACHE[name] = build_sub_prog(name)
    return _PROG_CACHE[name]


CHAIN = ["l0aa", "l0ffn", "l0ab", "l0ffn", "l1aa", "l1ffn", "l1ab", "l1ffn"]


def kernel(**inputs):
    x = np.asarray(inputs["L0_attn_a_x"], dtype=np.float32)
    B = x.shape[0]
    Xs = [np.ascontiguousarray(x[b].T) for b in range(B)]
    for name in CHAIN:
        p, c, nc = _get_prog(name)
        if name == "l1ffn":
            Wnp = prep_l1ffn_weights(inputs)
        elif name == "l0ffn":
            Wnp = prep_l0ffn_weights(inputs)
        else:
            Wnp = SUBS[name]["prep"](inputs)
        in_maps = []
        for b in range(B):
            m = dict(c.np_map)
            m.update(Wnp)
            m["Xio"] = Xs[b]
            in_maps.append(m)
        res = run_bass_kernel_spmd(nc, in_maps, core_ids=list(range(B)))
        Xs = [np.ascontiguousarray(r["Xout"]) for r in res.results]
    out = np.stack([Xs[b].T for b in range(B)], axis=0)
    return out.astype(np.float32)


EMITS = {"l0aa": emit_l0aa, "l0ffn": emit_l0ffn, "l0ab": emit_l0ab,
         "l1aa": emit_l1aa, "l1ffn": emit_l1ffn, "l1ab": emit_l1ab}

WSPECS = {
    "l0ffn": [("g1", [D]), ("g2", [D]), ("Wq", [D, D]), ("Wkc", [D, 256]),
              ("Wvc", [D, 256]), ("Wo", [D, D]), ("Wr", [D, 8]), ("expert_bias", [8]),
              ("sW1", [D, D]), ("sW2", [D, D]), ("sW3", [D, D]),
              ("eW1", [8, D, D]), ("eW2", [8, D, D]), ("eW3", [8, D, D])],
    "l1ffn": [("w_gate", [D, 8]), ("w1", [8, D, 2048]), ("w3", [8, D, 2048]),
              ("w2", [8, 2048, D])],
}


def build_fused_prog():
    p = Prog()
    c = Consts(p)
    p.alloc_banks()
    X = p.nc.dram_tensor("Xio", [D, T], F32, kind="ExternalInput").ap()
    Xout = p.nc.dram_tensor("Xout", [D, T], F32, kind="ExternalOutput").ap()
    iota = np.arange(8, dtype=np.float32)
    Wall = {}
    for name in set(CHAIN):
        wspec = SUBS[name]["wspec"] if name in SUBS else WSPECS[name]
        W = {}
        for nm, shp in wspec:
            W[nm] = p.nc.dram_tensor(f"W{name}_{nm}", list(shp), F32,
                                     kind="ExternalInput").ap()
        W["c_iota8"] = c.add("c_iota8", np.broadcast_to(iota, (PD, 8)).copy())
        W["c_iota8p9"] = c.add("c_iota8p9", np.broadcast_to(iota + 100.0, (PD, 8)).copy())
        if name in SUBS and SUBS[name]["consts"] is not None:
            SUBS[name]["consts"](c, W)
        Wall[name] = W
    g = load_globals(p, c)
    Xs = p.dram("Xs", [D, T], F32)
    cp = []
    for dc in range(DC):
        cp.append(p.dma("sp", Xs[dc * PD:(dc + 1) * PD, :], X[dc * PD:(dc + 1) * PD, :], ch="st"))
    deps = cp[-1:]
    for name in CHAIN:
        p.reset_channels()
        fin = EMITS[name](p, g, Xs, Wall[name], deps=deps)
        deps = list(fin)
    out_toks = []
    for dc in range(DC):
        out_toks.append(p.dma("sp", Xout[dc * PD:(dc + 1) * PD, :],
                              Xs[dc * PD:(dc + 1) * PD, :], deps=deps, ch="out"))
    p.emit("sp", lambda E: E.nop(), deps=out_toks[-1:], sig=True)
    nc = p.finalize()
    return p, c, nc


def fused_in_map(c, inputs):
    m = dict(c.np_map)
    for name in set(CHAIN):
        if name == "l1ffn":
            Wnp = prep_l1ffn_weights(inputs)
        elif name == "l0ffn":
            Wnp = prep_l0ffn_weights(inputs)
        else:
            Wnp = SUBS[name]["prep"](inputs)
        for k, v in Wnp.items():
            m[f"W{name}_{k}"] = np.ascontiguousarray(np.asarray(v, dtype=np.float32))
    return m


FUSED = True


def kernel(**inputs):
    x = np.asarray(inputs["L0_attn_a_x"], dtype=np.float32)
    B = x.shape[0]
    if FUSED:
        if "fused" not in _PROG_CACHE:
            _PROG_CACHE["fused"] = build_fused_prog()
        p, c, nc = _PROG_CACHE["fused"]
        base = fused_in_map(c, inputs)
        in_maps = []
        for b in range(B):
            m = dict(base)
            m["Xio"] = np.ascontiguousarray(x[b].T)
            in_maps.append(m)
        res = run_bass_kernel_spmd(nc, in_maps, core_ids=list(range(B)))
        out = np.stack([res.results[b]["Xout"].T for b in range(B)], axis=0)
        return out.astype(np.float32)
    Xs = [np.ascontiguousarray(x[b].T) for b in range(B)]
    for name in CHAIN:
        p, c, nc = _get_prog(name)
        if name == "l1ffn":
            Wnp = prep_l1ffn_weights(inputs)
        elif name == "l0ffn":
            Wnp = prep_l0ffn_weights(inputs)
        else:
            Wnp = SUBS[name]["prep"](inputs)
        in_maps = []
        for b in range(B):
            m = dict(c.np_map)
            m.update(Wnp)
            m["Xio"] = Xs[b]
            in_maps.append(m)
        res = run_bass_kernel_spmd(nc, in_maps, core_ids=list(range(B)))
        Xs = [np.ascontiguousarray(r["Xout"]) for r in res.results]
    out = np.stack([Xs[b].T for b in range(B)], axis=0)
    return out.astype(np.float32)
